# version 07
# baseline (speedup 1.0000x reference)
import numpy as np
import ml_dtypes
from contextlib import ExitStack
import concourse.bass as bass
import concourse.mybir as mybir
from concourse.bass_utils import run_bass_kernel_spmd

F32 = mybir.dt.float32
BF16 = mybir.dt.bfloat16
I32 = mybir.dt.int32
AF = mybir.ActivationFunctionType
ALU = mybir.AluOpType
AX = mybir.AxisListType
BF16NP = ml_dtypes.bfloat16

NCORES = 8
TC = 1024
D = 2048
H0 = 8192
H1 = 5632
NH = 16
HD = 128
MAGIC = 12582912.0
BIGNEG = -30000.0
SOFF = 32.0
LNEPS = 1e-5
QSCL = 0.08838834764831845

ENGS = ("t", "v", "s", "g", "y")
QUEUE_ENG = {"qS": "y", "qW0": "y", "qW1": "y", "qA0": "y", "qA1": "y",
             "qD0": "y", "qD1": "s",
             "qG": "g", "qX0": "g", "qX1": "g", "qO": "g"}


class Sem:
    __slots__ = ("h", "cnt", "name", "joined")

    def __init__(self, h, name):
        self.h, self.cnt, self.name, self.joined = h, 0, name, 0


class Emit:
    def __init__(self, nc, ctx):
        self.nc, self.ctx = nc, ctx
        self.bufs = {e: [] for e in ENGS}
        self.sems = {}

    def sem(self, name):
        if name not in self.sems:
            self.sems[name] = Sem(self.ctx.enter_context(self.nc.semaphore(name)),
                                  name)
        return self.sems[name]

    def op(self, eng, fn):
        self.bufs[eng].append(fn)

    def wait(self, eng, sem, cnt=None):
        n = sem.cnt if cnt is None else cnt
        if n > 0:
            self.bufs[eng].append(lambda e, h=sem.h, n=n: e.wait_ge(h, n))

    def dr(self, eng):
        self.bufs[eng].append(lambda e: e.drain())

    def opi(self, eng, fn, sem, k=1):
        if k == 0:
            self.bufs[eng].append(fn)
            return
        sem.cnt += k
        self.bufs[eng].append(lambda e, f=fn, h=sem.h, k=k: f(e).then_inc(h, k))

    def join(self):
        for qn, eng in QUEUE_ENG.items():
            if qn in self.sems:
                s = self.sems[qn]
                if s.cnt > s.joined:
                    self.wait(eng, s)
                    s.joined = s.cnt
        ph = self.sem("ph")
        ph.cnt += 5
        tgt = ph.cnt
        for e in ENGS:
            if e in ("t", "v", "s"):
                self.bufs[e].append(lambda eng: eng.drain())
            self.bufs[e].append(lambda eng, h=ph.h: eng.sem_inc(h, 1))
            self.bufs[e].append(lambda eng, h=ph.h, n=tgt: eng.wait_ge(h, n))

    _blk_n = 0

    def flush_block(self, name=None):
        Emit._blk_n += 1
        name = f"{name or 'blk'}_{Emit._blk_n}"
        bufs = self.bufs
        self.bufs = {e: [] for e in ENGS}
        with self.nc.Block(name) as blk:
            for key, reg in (("t", blk.tensor), ("v", blk.vector),
                             ("s", blk.scalar), ("g", blk.gpsimd),
                             ("y", blk.sync)):
                def body(e, ops=bufs[key]):
                    for f in ops:
                        f(e)
                reg(body)



def _consts_per_core(c):
    i_ = np.arange(64)
    Dt = (i_[None, :] - i_[:, None]).astype(np.float32)
    Dt[i_[None, :] < i_[:, None]] = 1e9
    Pc = np.full((16, 8), 1e9, np.float32)
    for r in range(c):
        Pc[:, r] = 1024.0 * (c - 1 - r)
    pos = (np.arange(TC, dtype=np.float64) + TC * c)
    half = HD // 2
    inv_freq = 1.0 / (10000.0 ** (np.arange(half, dtype=np.float64) / half))
    freqs = np.outer(pos, inv_freq)
    masks = np.zeros((4, 128, 512), np.float32)
    kk = np.arange(128)[:, None]
    qq = np.arange(512)[None, :]
    for j in range(4):
        masks[j] = np.where(128 * j + kk <= qq, 1.0, 0.0)
    masks = masks.astype(BF16NP)
    return dict(
        ident=np.eye(128, dtype=BF16NP),
        ident_f=np.eye(128, dtype=np.float32),
        ones_f=np.ones((1, 128), np.float32),
        ones_rbf=np.ones((1, 128), BF16NP),
        ones_bf=np.ones((128, 1), BF16NP),
        iota_loc=np.arange(TC, dtype=np.float32).reshape(1, TC),
        iota_m64=(np.arange(TC) % 64).astype(np.float32).reshape(1, TC),
        iota64=np.arange(64, dtype=np.float32).reshape(1, 64),
        k64m=(64.0 - (np.arange(128) % 64)).astype(np.float32).reshape(128, 1),
        Dt=Dt, Pc=Pc,
        cos=np.cos(freqs).astype(np.float32),
        sin=np.sin(freqs).astype(np.float32),
        fmask=masks,
    )


def _shard_inputs(inp):
    x = np.ascontiguousarray(np.asarray(inp["L0_attn_x"], np.float32).reshape(8192, D))
    w1 = np.asarray(inp["L0_ffn_w1"], np.float32)
    w2 = np.asarray(inp["L0_ffn_w2"], np.float32)
    w1T = np.ascontiguousarray(w1.T)
    w2T = np.ascontiguousarray(w2.T)
    qkvoT = np.stack([np.ascontiguousarray(np.asarray(inp[k], np.float32).T)
                      for k in ("L1_attn_w_q", "L1_attn_w_k",
                                "L1_attn_w_v", "L1_attn_w_o")])
    gT = np.ascontiguousarray(np.asarray(inp["L1_ffn_wq_gate"], np.int32).T)
    uT = np.ascontiguousarray(np.asarray(inp["L1_ffn_wq_up"], np.int32).T)
    dT_raw = np.ascontiguousarray(np.asarray(inp["L1_ffn_wq_down"], np.int32).T)
    dT = np.zeros((48 * 128, D), np.int32)
    dT[:H1] = dT_raw
    base = dict(
        wqkv=np.ascontiguousarray(np.asarray(inp["L0_attn_Wqkv"], np.float32)),
        bqkv=np.asarray(inp["L0_attn_bqkv"], np.float32).reshape(1, 3 * D),
        wout=np.ascontiguousarray(np.asarray(inp["L0_attn_Wout"], np.float32)),
        bout=np.asarray(inp["L0_attn_bout"], np.float32).reshape(1, D),
        slopes=np.asarray(inp["L0_attn_slopes"], np.float32).reshape(1, 16),
        sg=np.asarray(inp["L1_ffn_s_gate"], np.float32).reshape(1, H1),
        su=np.asarray(inp["L1_ffn_s_up"], np.float32).reshape(1, H1),
        sd=np.asarray(inp["L1_ffn_s_down"], np.float32).reshape(1, D),
    )
    maps = []
    for c in range(NCORES):
        m = dict(base)
        m.update(
            x0=np.ascontiguousarray(x[TC * c:TC * (c + 1)]),
            w1_rows=np.ascontiguousarray(w1[1024 * c:1024 * (c + 1)]),
            w1T_sh=np.ascontiguousarray(w1T[256 * c:256 * (c + 1)]),
            w2_rows=np.ascontiguousarray(w2[256 * c:256 * (c + 1)]),
            w2T_sh=np.ascontiguousarray(w2T[1024 * c:1024 * (c + 1)]),
            qkvoT_sh=np.ascontiguousarray(qkvoT[:, 256 * c:256 * (c + 1), :]),
            gT_sh=np.ascontiguousarray(gT[256 * c:256 * (c + 1)]),
            uT_sh=np.ascontiguousarray(uT[256 * c:256 * (c + 1)]),
            dT_sh=np.ascontiguousarray(dT[768 * c:768 * (c + 1)]),
        )
        m.update(_consts_per_core(c))
        maps.append(m)
    return maps



class B:
    pass


def _psum(nc, *a, **k):
    return nc.psum_tensor(*a, allow_name_mangling=True, **k)


def build(upto=6):
    nc = bass.Bass("TRN2", target_bir_lowering=False, debug=False,
                   detect_race_conditions=False)
    b = B()
    b.nc = nc
    dt = nc.dram_tensor

    def st_(*a, **k):
        return nc.sbuf_tensor(*a, allow_name_mangling=True, **k)
    b.st = st_
    io = {}
    io["x0"] = dt("x0", [TC, D], F32, kind="ExternalInput").ap()
    io["y"] = dt("y", [TC, D], F32, kind="ExternalOutput").ap()
    io["dbg0"] = dt("dbg0", [128, H0], F32, kind="ExternalOutput").ap()
    io["dbg1"] = dt("dbg1", [128, 64], F32, kind="ExternalOutput").ap()
    io["dbg2"] = dt("dbg2", [128, D], F32, kind="ExternalOutput").ap()
    for name, shape, dty in [
        ("w1_rows", [1024, D], F32), ("w1T_sh", [256, H0], F32),
        ("w2_rows", [256, H0], F32), ("w2T_sh", [1024, D], F32),
        ("wqkv", [D, 3 * D], F32), ("bqkv", [1, 3 * D], F32),
        ("wout", [D, D], F32), ("bout", [1, D], F32), ("slopes", [1, 16], F32),
        ("qkvoT_sh", [4, 256, D], F32),
        ("gT_sh", [256, H1], I32), ("uT_sh", [256, H1], I32),
        ("dT_sh", [768, D], I32),
        ("sg", [1, H1], F32), ("su", [1, H1], F32), ("sd", [1, D], F32),
        ("ident", [128, 128], BF16), ("ident_f", [128, 128], F32),
        ("ones_f", [1, 128], F32),
        ("ones_rbf", [1, 128], BF16), ("ones_bf", [128, 1], BF16),
        ("iota_loc", [1, TC], F32), ("iota_m64", [1, TC], F32),
        ("iota64", [1, 64], F32), ("k64m", [128, 1], F32),
        ("Dt", [64, 64], F32), ("Pc", [16, 8], F32),
        ("cos", [TC, 64], F32), ("sin", [TC, 64], F32),
        ("fmask", [4, 128, 512], BF16),
    ]:
        io[name] = dt(name, shape, dty, kind="ExternalInput").ap()
    b.io = io

    dr = {}
    dr["xa"] = dt("xa", [TC, D], F32)
    dr["xb"] = dt("xb", [TC, D], F32)
    dr["st_sh"] = dt("st_sh", [1, 1280], F32)
    dr["st_all"] = dt("st_all", [8, 1280], F32)
    dr["w1b_sh"] = dt("w1b_sh", [2, 128, H0], BF16)
    dr["w1b"] = dt("w1b", [16, 128, H0], BF16)
    dr["w2b_sh"] = dt("w2b_sh", [8, 128, D], BF16)
    dr["w2b"] = dt("w2b", [64, 128, D], BF16)
    dr["qkvob_sh"] = dt("qkvob_sh", [4, 2, 128, D], BF16)
    dr["qkvob"] = dt("qkvob", [8, 4, 2, 128, D], BF16)
    dr["gb_sh"] = dt("gb_sh", [2, 128, H1], BF16)
    dr["ub_sh"] = dt("ub_sh", [2, 128, H1], BF16)
    dr["db_sh"] = dt("db_sh", [6, 128, D], BF16)
    dr["gb"] = dt("gb", [16, 128, H1], BF16)
    dr["ub"] = dt("ub", [16, 128, H1], BF16)
    dr["db"] = dt("db", [48, 128, D], BF16)
    dr["qkv0"] = dt("qkv0", [TC, 3 * D], BF16)
    dr["Tc_sh"] = dt("Tc_sh", [16, 128, 128], F32)
    dr["Tc"] = dt("Tc", [8, 16, 128, 128], F32)
    dr["qkv1_sh"] = dt("qkv1_sh", [3, NH, 128, TC], BF16)
    dr["qkv1"] = dt("qkv1", [8, 3, NH, 128, TC], BF16)
    dr["o1_sh"] = dt("o1_sh", [2, 128, 8192], BF16)
    dr["o1"] = dt("o1", [8, 2, 128, 8192], BF16)
    b.dr = dr

    ctx = ExitStack()
    b.ctx = ctx
    E = Emit(nc, ctx)
    b.E = E
    g = {}
    g["ident"] = ctx.enter_context(st_("c_ident", [128, 128], BF16))
    g["ident_f"] = ctx.enter_context(st_("c_identf", [128, 128], F32))
    g["ones_f"] = ctx.enter_context(st_("c_ones_f", [1, 128], F32))
    g["ones_rbf"] = ctx.enter_context(st_("c_ones_rbf", [1, 128], BF16))
    g["ones_bf"] = ctx.enter_context(st_("c_ones_bf", [128, 1], BF16))
    g["stat"] = ctx.enter_context(st_("g_stat", [128, 64], F32))
    b.g = g

    for s in ("qS", "qW0", "qW1", "qA0", "qA1", "qD0", "qD1",
              "qG", "qX0", "qX1", "qO", "tE", "vE", "sE", "cc"):
        E.sem(s)

    qS = E.sem("qS")
    for name, dst in (("ident", g["ident"]), ("ident_f", g["ident_f"]),
                      ("ones_f", g["ones_f"]),
                      ("ones_rbf", g["ones_rbf"]), ("ones_bf", g["ones_bf"])):
        E.opi("y", lambda e, s=dst, i=io[name]: e.dma_start(out=s[:], in_=i), qS, 16)
    E.opi("y", lambda e: e.dma_start(out=dr["xa"].ap(), in_=io["x0"]), qS, 16)
    E.join()
    E.flush_block("p0")

    _prep(b)

    xc = "xa"
    seq = [("f0", "_ffn0"), ("la", "_lattn"), ("f0", "_ffn0"),
           ("f1", "_ffn1"), ("sa", "_sattn"), ("f1", "_ffn1")]
    import os as _os
    only = _os.environ.get("MK_ONLY")
    run_seq = [seq[int(only) - 1]] if only else seq[:upto]
    for i, (tag, fn) in enumerate(run_seq):
        phase_i = i + 1
        last = (phase_i == 6) and not only
        dst = None if last else ("xb" if xc == "xa" else "xa")
        globals()[fn](b, xc, dst, final=last)
        if dst is not None:
            xc = dst
    upto = 0 if only else upto
    if upto < 6:
        E.opi("y", lambda e, x=xc: e.dma_start(out=io["y"], in_=dr[x].ap()), qS, 16)
        E.join()
        E.flush_block("fincopy")
    ctx.close()
    return nc



def _bcast(b, psb, src_ap_fn, dst, w, f32=True):
    E, g = b.E, b.g
    tE, vE = E.sem("tE"), E.sem("vE")
    E.wait("t", vE)
    for j in range((w + 511) // 512):
        ww = min(512, w - 512 * j)
        E.opi("t", lambda e, j=j, ww=ww: e.matmul(
            psb[:, :ww], g["ones_f"][0:1, :], src_ap_fn(j, ww),
            start=True, stop=True), tE, 1)
        E.wait("v", tE)
        E.opi("v", lambda e, j=j, ww=ww: e.tensor_copy(
            dst[:, 512 * j:512 * j + ww], psb[:, :ww]), vE, 1)
        E.wait("t", vE)


def _ag(b, name_in, name_out):
    E, dr = b.E, b.dr
    cc, qG = E.sem("cc"), E.sem("qG")
    E.wait("g", qG)
    E.wait("g", E.sem("qO"))
    E.opi("g", lambda e: e.collective_compute(
        "AllGather", ALU.bypass, replica_groups=[list(range(NCORES))],
        ins=[dr[name_in].ap()], outs=[dr[name_out].ap()]), cc, 1)
    if not hasattr(b, "cc_marks"):
        b.cc_marks = {}
    b.cc_marks[name_out] = cc.cnt


def _ternary_quant_ops(b, wt_ap, rc, sc, tq, out_ap, w):
    E = b.E
    vE = E.sem("vE")
    E.wait("v", E.sem("qG"))
    E.op("v", lambda e: e.tensor_tensor(tq[:, :w], wt_ap, rc[:, :w], ALU.mult))
    E.op("v", lambda e: e.tensor_scalar(tq[:, :w], tq[:, :w], MAGIC, MAGIC,
                                        ALU.add, ALU.subtract))
    E.op("v", lambda e: e.tensor_scalar(tq[:, :w], tq[:, :w], 1.0, -1.0,
                                        ALU.min, ALU.max))
    E.opi("v", lambda e: e.tensor_tensor(out_ap, tq[:, :w], sc[:, :w], ALU.mult),
          vE, 1)



def _prep(b):
    import os
    pstop = int(os.environ.get("MK_PSTOP", "9"))
    nc, E, io, dr, g, st_ = b.nc, b.E, b.io, b.dr, b.g, b.st
    if pstop <= 0:
        return
    qS, qG = E.sem("qS"), E.sem("qG")
    qW0, qW1 = E.sem("qW0"), E.sem("qW1")
    qX0, qX1 = E.sem("qX0"), E.sem("qX1")
    tE, vE, sE, cc = E.sem("tE"), E.sem("vE"), E.sem("sE"), E.sem("cc")

    with ExitStack() as px:
        wrow = px.enter_context(st_("p_wrow", [128, H0], F32))
        wrow2 = px.enter_context(st_("p_wrow2", [128, H0], F32))
        stt = px.enter_context(st_("p_stats", [128, 10], F32))
        srow = px.enter_context(st_("p_srow", [1, 1280], F32))
        pT = px.enter_context(_psum(nc, "pp_T", [1, 512], F32))
        vh = []
        for j in range(10):
            buf = wrow if j % 2 == 0 else wrow2
            q = qW0 if j % 2 == 0 else qW1
            if j >= 2:
                E.wait("y", vE, vh[j - 2])
            if j < 8:
                E.opi("y", lambda e, bu=buf, j=j: e.dma_start(
                    out=bu[:, :D], in_=io["w1_rows"][128 * j:128 * (j + 1), :]),
                    q, 16)
                E.wait("v", q)
                E.opi("v", lambda e, bu=buf, j=j: e.tensor_reduce(
                    stt[:, j:j + 1], bu[:, :D], axis=AX.X, op=ALU.add,
                    apply_absolute_value=True), vE, 1)
            else:
                jj = j - 8
                E.opi("y", lambda e, bu=buf, jj=jj: e.dma_start(
                    out=bu[:, :], in_=io["w2_rows"][128 * jj:128 * (jj + 1), :]),
                    q, 16)
                E.wait("v", q)
                E.opi("v", lambda e, bu=buf, j=j: e.tensor_reduce(
                    stt[:, j:j + 1], bu[:, :], axis=AX.X, op=ALU.add,
                    apply_absolute_value=True), vE, 1)
            vh.append(vE.cnt)
        pamode = int(os.environ.get("MK_PAMODE", "4"))
        if pamode >= 2:
            E.wait("t", vE)
            for j0 in range(0, 10, 4):
                nb = min(4, 10 - j0)
                for k in range(nb):
                    E.opi("t", lambda e, j=j0 + k, s=k: e.transpose(
                        pT[0:1, 128 * s:128 * s + 128], stt[:, j:j + 1],
                        g["ident_f"][:]), tE, 1)
                E.wait("v", tE)
                E.opi("v", lambda e, j0=j0, nb=nb: e.tensor_copy(
                    srow[0:1, 128 * j0:128 * (j0 + nb)],
                    pT[0:1, 0:128 * nb]), vE, 1)
                E.wait("t", vE)
        if pamode >= 3:
            E.wait("g", vE)
            E.opi("g", lambda e: e.dma_start(out=dr["st_sh"].ap(), in_=srow[:]),
                  qG, 16)
        if pamode >= 4:
            _ag(b, "st_sh", "st_all")
        E.join()
        E.flush_block("prepA")
    if pstop <= 1:
        return

    with ExitStack() as px:
        wt = px.enter_context(st_("pb_wt", [128, H0], F32))
        wt2 = px.enter_context(st_("pb_wt2", [128, H0], F32))
        tq = px.enter_context(st_("pb_tq", [128, 2048], F32))
        ob = px.enter_context(st_("pb_ob", [128, H0], BF16))
        rc = px.enter_context(st_("pb_rc", [128, 2048], F32))
        sc = px.enter_context(st_("pb_sc", [128, 2048], F32))
        psb = px.enter_context(_psum(nc, "pp_b", [128, 512], F32))
        g["row"] = px.enter_context(st_("pb_row", [1, H0], F32))
        g["row2"] = px.enter_context(st_("pb_row2", [1, H0], F32))
        E.wait("y", cc)
        E.opi("y", lambda e: e.dma_start(
            out=g["row"][0:1, :H0].rearrange("a (r n) -> a r n", r=8),
            in_=dr["st_all"].ap()[:, 0:1024].rearrange("r n -> () r n")), qS, 16)
        E.wait("v", qS)
        E.op("v", lambda e: e.tensor_scalar(
            g["row"][0:1, :H0], g["row"][0:1, :H0], 1.0 / D, 1e-5,
            ALU.mult, ALU.max))
        E.dr("v")
        E.opi("v", lambda e: e.reciprocal(g["row2"][0:1, :H0], g["row"][0:1, :H0]),
              vE, 1)
        E.dr("v")
        for i in range(2):
            buf = wt if i % 2 == 0 else wt2
            q = qW0 if i % 2 == 0 else qW1
            E.opi("y", lambda e, bu=buf, i=i: e.dma_start(
                out=bu[:], in_=io["w1T_sh"][128 * i:128 * (i + 1), :]), q, 16)
            E.wait("v", q)
            for sl in range(4):
                lo = 2048 * sl
                _bcast(b, psb, lambda j, ww, lo=lo: g["row2"][0:1, lo + 512 * j:lo + 512 * j + ww],
                       rc, 2048)
                _bcast(b, psb, lambda j, ww, lo=lo: g["row"][0:1, lo + 512 * j:lo + 512 * j + ww],
                       sc, 2048)
                _ternary_quant_ops(b, buf[:, lo:lo + 2048], rc, sc, tq,
                                   ob[:, lo:lo + 2048], 2048)
            E.wait("g", vE)
            E.opi("g", lambda e, i=i: e.dma_start(
                out=dr["w1b_sh"].ap()[i], in_=ob[:]), qG, 16)
        _ag(b, "w1b_sh", "w1b")
        E.join()
        E.flush_block("prepB")
    if pstop <= 2:
        return

    with ExitStack() as px:
        wt = px.enter_context(st_("pc_wt", [128, D], F32))
        wt2 = px.enter_context(st_("pc_wt2", [128, D], F32))
        tq = px.enter_context(st_("pc_tq", [128, D], F32))
        ob = px.enter_context(st_("pc_ob", [128, D], BF16))
        rc = px.enter_context(st_("pc_rc", [128, D], F32))
        sc = px.enter_context(st_("pc_sc", [128, D], F32))
        psb = px.enter_context(_psum(nc, "pp_c", [128, 512], F32))
        g["row"] = px.enter_context(st_("pc_row", [1, D], F32))
        g["row2"] = px.enter_context(st_("pc_row2", [1, D], F32))
        E.opi("y", lambda e: e.dma_start(
            out=g["row"][0:1, :D].rearrange("a (r n) -> a r n", r=8),
            in_=dr["st_all"].ap()[:, 1024:1280].rearrange("r n -> () r n")), qS, 16)
        E.wait("v", qS)
        E.op("v", lambda e: e.tensor_scalar(
            g["row"][0:1, :D], g["row"][0:1, :D], 1.0 / H0, 1e-5,
            ALU.mult, ALU.max))
        E.dr("v")
        E.opi("v", lambda e: e.reciprocal(g["row2"][0:1, :D], g["row"][0:1, :D]),
              vE, 1)
        E.dr("v")
        _bcast(b, psb, lambda j, ww: g["row2"][0:1, 512 * j:512 * j + ww], rc, D)
        _bcast(b, psb, lambda j, ww: g["row"][0:1, 512 * j:512 * j + ww], sc, D)
        obh = []
        for i in range(8):
            buf = wt if i % 2 == 0 else wt2
            q = qW0 if i % 2 == 0 else qW1
            if i >= 1:
                E.wait("y", vE)
                if i >= 1:
                    E.wait("y", qG)
            E.opi("y", lambda e, bu=buf, i=i: e.dma_start(
                out=bu[:], in_=io["w2T_sh"][128 * i:128 * (i + 1), :]), q, 16)
            E.wait("v", q)
            _ternary_quant_ops(b, buf[:], rc, sc, tq, ob[:], D)
            E.wait("g", vE)
            E.opi("g", lambda e, i=i: e.dma_start(
                out=dr["w2b_sh"].ap()[i], in_=ob[:]), qG, 16)
        _ag(b, "w2b_sh", "w2b")
        E.join()
        E.flush_block("prepC")
    if pstop <= 3:
        return

    with ExitStack() as px:
        wt = px.enter_context(st_("pd_wt", [128, D], F32))
        awt = px.enter_context(st_("pd_awt", [128, D], F32))
        tq = px.enter_context(st_("pd_tq", [128, D], F32))
        ob = px.enter_context(st_("pd_ob", [128, D], BF16))
        rc = px.enter_context(st_("pd_rc", [128, D], F32))
        sc = px.enter_context(st_("pd_sc", [128, D], F32))
        psb = px.enter_context(_psum(nc, "pp_d", [128, 512], F32))
        pss = px.enter_context(_psum(nc, "pp_ds", [1, 2048], F32))
        onesc = px.enter_context(st_("pd_ones", [128, 1], F32))
        g["row"] = px.enter_context(st_("pd_row", [1, D], F32))
        g["row2"] = px.enter_context(st_("pd_row2", [1, D], F32))
        E.op("v", lambda e: e.memset(onesc[:], 1.0))
        for mat in range(4):
            for i in range(2):
                t_i = mat * 2 + i
                q = qW0 if t_i % 2 == 0 else qW1
                if t_i >= 1:
                    E.wait("y", vE)
                    E.wait("y", qG)
                E.opi("y", lambda e, mat=mat, i=i: e.dma_start(
                    out=wt[:], in_=io["qkvoT_sh"][mat, 128 * i:128 * (i + 1), :]),
                    q, 16)
                E.wait("s", q)
                E.opi("s", lambda e: e.activation(awt[:], wt[:], AF.Abs), sE, 1)
                E.wait("t", sE)
                E.wait("t", vE)
                for j in range(4):
                    E.opi("t", lambda e, j=j: e.matmul(
                        pss[0:1, 512 * j:512 * (j + 1)], onesc[:],
                        awt[:, 512 * j:512 * (j + 1)], start=True, stop=True),
                        tE, 1)
                E.wait("v", tE)
                E.op("v", lambda e: e.tensor_scalar(
                    g["row"][0:1, :D], pss[0:1, :], 1.0 / 128.0, 1e-8,
                    ALU.mult, ALU.max))
                E.dr("v")
                E.opi("v", lambda e: e.reciprocal(
                    g["row2"][0:1, :D], g["row"][0:1, :D]), vE, 1)
                E.dr("v")
                _bcast(b, psb, lambda j, ww: g["row2"][0:1, 512 * j:512 * j + ww],
                       rc, D)
                _bcast(b, psb, lambda j, ww: g["row"][0:1, 512 * j:512 * j + ww],
                       sc, D)
                _ternary_quant_ops(b, wt[:], rc, sc, tq, ob[:], D)
                E.wait("g", vE)
                E.opi("g", lambda e, mat=mat, i=i: e.dma_start(
                    out=dr["qkvob_sh"].ap()[mat, i], in_=ob[:]), qG, 16)
        _ag(b, "qkvob_sh", "qkvob")
        E.join()
        E.flush_block("prepD")
    if pstop <= 4:
        return

    with ExitStack() as px:
        wt = px.enter_context(st_("pe_wt", [128, H1], F32))
        tq = px.enter_context(st_("pe_tq", [128, H1], F32))
        ob = px.enter_context(st_("pe_ob", [128, H1], BF16))
        scl = px.enter_context(st_("pe_scl", [128, H1], F32))
        psb = px.enter_context(_psum(nc, "pp_e", [128, 512], F32))
        g["row"] = px.enter_context(st_("pe_row", [1, H1], F32))
        for srow_name, wsh, osh, width, nt in (
                ("sg", "gT_sh", "gb_sh", H1, 2), ("su", "uT_sh", "ub_sh", H1, 2),
                ("sd", "dT_sh", "db_sh", D, 6)):
            E.wait("y", tE)
            E.opi("y", lambda e, n=srow_name, w=width: e.dma_start(
                out=g["row"][0:1, :w], in_=io[n]), qS, 16)
            E.wait("t", qS)
            _bcast(b, psb, lambda j, ww: g["row"][0:1, 512 * j:512 * j + ww],
                   scl, width)
            for i in range(nt):
                q = qX0 if i % 2 == 0 else qX1
                if i >= 1:
                    E.wait("g", vE)
                    E.wait("g", qG)
                E.opi("g", lambda e, wsh=wsh, i=i, w=width: e.dma_start(
                    out=wt[:, :w], in_=io[wsh][128 * i:128 * (i + 1), :]), q, 16)
                E.wait("v", q)
                E.wait("v", qG)
                E.op("v", lambda e, w=width: e.tensor_scalar(
                    tq[:, :w], wt[:, :w], -128.0, 0.0, ALU.add, ALU.add))
                E.opi("v", lambda e, w=width: e.tensor_tensor(
                    ob[:, :w], tq[:, :w], scl[:, :w], ALU.mult), vE, 1)
                E.wait("g", vE)
                E.opi("g", lambda e, osh=osh, i=i: e.dma_start(
                    out=dr[osh].ap()[i], in_=ob[:, :dr[osh].shape[2]]), qG, 16)
        _ag(b, "gb_sh", "gb")
        _ag(b, "ub_sh", "ub")
        _ag(b, "db_sh", "db")
        E.join()
        E.flush_block("prepE")



def _ln_stats(b, xt_ap, sq, mi, quant):
    E, g = b.E, b.g
    vE, sE = E.sem("vE"), E.sem("sE")
    S = g["stat"]
    c0 = 16 * mi

    def col(j):
        return S[:, c0 + j:c0 + j + 1]
    E.wait("s", vE)
    E.opi("s", lambda e: e.activation(sq[:], xt_ap, AF.Square), sE, 1)
    E.op("v", lambda e: e.tensor_reduce(col(5), xt_ap, axis=AX.X, op=ALU.add))
    E.wait("v", sE)
    E.op("v", lambda e: e.tensor_reduce(col(6), sq[:], axis=AX.X, op=ALU.add))
    E.dr("v")
    E.op("v", lambda e: e.tensor_scalar(col(0), col(5), 1.0 / D, None, ALU.mult))
    E.op("v", lambda e: e.tensor_scalar(col(6), col(6), 1.0 / D, None, ALU.mult))
    E.dr("v")
    E.op("v", lambda e: e.tensor_tensor(col(7), col(0), col(0), ALU.mult))
    E.dr("v")
    E.op("v", lambda e: e.tensor_tensor(col(6), col(6), col(7), ALU.subtract))
    E.dr("v")
    E.opi("v", lambda e: e.tensor_scalar(col(6), col(6), 1.0, LNEPS,
                                         ALU.mult, ALU.add), vE, 1)
    E.dr("v")
    E.wait("s", vE)
    E.opi("s", lambda e: e.activation(col(7), col(6), AF.Sqrt), sE, 1)
    E.dr("s")
    E.wait("v", sE)
    E.opi("v", lambda e: e.reciprocal(col(1), col(7)), vE, 1)
    if quant:
        E.op("v", lambda e: e.tensor_reduce(col(5), xt_ap, axis=AX.X, op=ALU.max))
        E.op("v", lambda e: e.tensor_reduce(col(6), xt_ap, axis=AX.X, op=ALU.min))
        E.dr("v")
        E.op("v", lambda e: e.tensor_tensor(col(5), col(5), col(0), ALU.subtract))
        E.op("v", lambda e: e.tensor_tensor(col(6), col(0), col(6), ALU.subtract))
        E.dr("v")
        E.op("v", lambda e: e.tensor_tensor(col(5), col(5), col(6), ALU.max))
        E.dr("v")
        E.op("v", lambda e: e.tensor_tensor(col(5), col(5), col(1), ALU.mult))
        E.dr("v")
        E.op("v", lambda e: e.tensor_scalar(col(5), col(5), 1.0, 1e-5,
                                            ALU.mult, ALU.max))
        E.dr("v")
        E.op("v", lambda e: e.tensor_scalar(col(2), col(5), 1.0 / 127.0, None,
                                            ALU.mult))
        E.op("v", lambda e: e.reciprocal(col(6), col(5)))
        E.dr("v")
        E.op("v", lambda e: e.tensor_scalar(col(3), col(6), 127.0, None, ALU.mult))
        E.dr("v")
        E.opi("v", lambda e: e.tensor_tensor(col(4), col(1), col(3), ALU.mult),
              vE, 1)
        E.dr("v")


def _transpose_128(b, src_bf_ap_fn, nchunk, psT, dst, dst_off):
    E, g = b.E, b.g
    tE, vE = E.sem("tE"), E.sem("vE")
    for b0 in range(0, nchunk, 8):
        nb = min(8, nchunk - b0)
        for k in range(nb):
            E.opi("t", lambda e, k=k, b0=b0: e.transpose(
                psT[:, 128 * k:128 * (k + 1)], src_bf_ap_fn(b0 + k),
                g["ident"][:]), tE, 1)
        E.wait("v", tE)
        E.opi("v", lambda e, b0=b0, nb=nb: e.tensor_copy(
            dst[:, dst_off + 128 * b0:dst_off + 128 * (b0 + nb)],
            psT[:, :128 * nb]), vE, 1)
        E.wait("t", vE)



def _ffn0(b, src, dst, final=False):
    nc, E, io, dr, g, st_ = b.nc, b.E, b.io, b.dr, b.g, b.st
    qW0, qW1 = E.sem("qW0"), E.sem("qW1")
    qX0, qX1 = E.sem("qX0"), E.sem("qX1")
    qA0, qA1 = E.sem("qA0"), E.sem("qA1")
    qO = E.sem("qO")
    tE, vE, sE = E.sem("tE"), E.sem("vE"), E.sem("sE")
    xsrc = dr[src].ap()
    xdst = io["y"] if final else dr[dst].ap()
    S = g["stat"]

    E.wait("y", E.sem("cc"), b.cc_marks["w2b"])
    with ExitStack() as px:
        xt = [px.enter_context(st_(f"f0_xt{i}", [128, D], F32)) for i in range(2)]
        sq = px.enter_context(st_("f0_sq", [128, H0 // 4], F32))
        xqb = px.enter_context(st_("f0_xqb", [128, D], BF16))
        xqT = [px.enter_context(st_(f"f0_xqT{i}", [128, D], BF16)) for i in range(2)]
        hb = [px.enter_context(st_(f"f0_hb{i}", [128, H0], F32)) for i in range(2)]
        hqT = [px.enter_context(st_(f"f0_hqT{i}", [128, H0], BF16)) for i in range(2)]
        w1s = [px.enter_context(st_(f"f0_w1s{i}", [128, 16 * 512], BF16))
               for i in range(2)]
        w2s = [px.enter_context(st_(f"f0_w2s{i}", [128, 16 * 512], BF16))
               for i in range(2)]
        psT = px.enter_context(_psum(nc, "f0_psT", [128, 1024], BF16))
        qO_hist = []
        for p in range(4):
            qA = qA0 if p % 2 == 0 else qA1
            if p >= 1:
                E.wait("y", qO)
            for mi in range(2):
                m = 2 * p + mi
                E.opi("y", lambda e, mi=mi, m=m: e.dma_start(
                    out=xt[mi][:], in_=xsrc[128 * m:128 * (m + 1), :]), qA, 16)
            for mi in range(2):
                E.wait("v", qA)
                E.wait("s", qA)
                _ln_stats(b, xt[mi][:], sq, mi, quant=True)
                c0 = 16 * mi
                E.dr("v")
                E.op("v", lambda e, mi=mi, c0=c0: e.tensor_scalar(
                    sq[:, :D], xt[mi][:], S[:, c0:c0 + 1], S[:, c0 + 4:c0 + 5],
                    ALU.subtract, ALU.mult))
                E.opi("v", lambda e: e.tensor_scalar(
                    xqb[:], sq[:, :D], MAGIC, MAGIC, ALU.add, ALU.subtract),
                    vE, 1)
                E.wait("t", vE)
                _transpose_128(b, lambda k, mi=mi: xqb[:, 128 * k:128 * (k + 1)],
                               16, psT, xqT[mi], 0)
            with ExitStack() as pf:
                psF = [pf.enter_context(_psum(nc, f"f0_psF{i}", [128, 512], F32))
                       for i in range(4)]
                tEh, sEh = {}, {}
                tEh[-2] = tEh[-1] = b.f0_prev_ffn1_last if hasattr(b, "f0_prev_ffn1_last") else 0
                for mi in range(2):
                    E.op("v", lambda e, mi=mi: e.memset(
                        S[:, 16 * mi + 11:16 * mi + 12], 0.0))
                E.wait("s", vE)
                for n in range(16):
                    q = qW0 if n % 2 == 0 else qW1
                    if tEh[n - 2]:
                        E.wait("y", tE, tEh[n - 2])
                    for half in range(2):
                        E.opi("y", lambda e, n=n, half=half: e.dma_start(
                            out=w1s[n % 2][:, 4096 * half:4096 * (half + 1)]
                            .rearrange("p (c h) -> p c h", h=512),
                            in_=dr["w1b"].ap()[8 * half:8 * (half + 1), :,
                                               512 * n:512 * (n + 1)]
                            .transpose([1, 0, 2])), q, 16)
                    E.wait("t", q)
                    if n >= 2:
                        E.wait("t", sE, sEh[n - 2])
                    for mi in range(2):
                        bank = psF[2 * mi + n % 2]
                        for k in range(16):
                            E.opi("t", lambda e, mi=mi, k=k, n=n, bank=bank:
                                  e.matmul(bank[:],
                                           xqT[mi][:, 128 * k:128 * (k + 1)],
                                           w1s[n % 2][:, 512 * k:512 * (k + 1)],
                                           start=(k == 0), stop=(k == 15)),
                                  tE, 1 if k == 15 else 0)
                    tEh[n] = tE.cnt
                    E.wait("s", tE)
                    for mi in range(2):
                        c0 = 16 * mi
                        bank = psF[2 * mi + n % 2]
                        E.opi("s", lambda e, mi=mi, n=n, c0=c0, bank=bank:
                              e.activation(hb[mi][:, 512 * n:512 * (n + 1)],
                                           bank[:], AF.Gelu_apprx_tanh,
                                           scale=S[:, c0 + 2:c0 + 3]), sE, 1)
                    sEh[n] = sE.cnt
                    E.wait("v", sE)
                    for mi in range(2):
                        c0 = 16 * mi
                        E.op("v", lambda e, mi=mi, n=n, c0=c0: e.tensor_reduce(
                            S[:, c0 + 8:c0 + 9], hb[mi][:, 512 * n:512 * (n + 1)],
                            axis=AX.X, op=ALU.max, apply_absolute_value=True))
                        E.dr("v")
                        E.op("v", lambda e, c0=c0: e.tensor_tensor(
                            S[:, c0 + 11:c0 + 12], S[:, c0 + 11:c0 + 12],
                            S[:, c0 + 8:c0 + 9], ALU.max))
            b.f0_prev_ffn1_last = E.sem("tE").cnt
            import os as _os
            if p == 0 and _os.environ.get("MK_DBG"):
                E.join()
                E.opi("g", lambda e: e.dma_start(out=io["dbg0"], in_=hb[0][:]),
                      E.sem("qO"), 16)
                E.opi("g", lambda e: e.dma_start(out=io["dbg1"], in_=S[:]),
                      E.sem("qO"), 16)
                E.opi("g", lambda e: e.dma_start(out=io["dbg2"], in_=xqT[0][:]),
                      E.sem("qO"), 16)
                E.join()
            for mi in range(2):
                c0 = 16 * mi
                E.dr("v")
                E.op("v", lambda e, c0=c0: e.tensor_scalar(
                    S[:, c0 + 8:c0 + 9], S[:, c0 + 11:c0 + 12], 1.0, 1e-5,
                    ALU.mult, ALU.max))
                E.dr("v")
                E.op("v", lambda e, c0=c0: e.reciprocal(
                    S[:, c0 + 7:c0 + 8], S[:, c0 + 8:c0 + 9]))
                E.dr("v")
                E.op("v", lambda e, c0=c0: e.tensor_scalar(
                    S[:, c0 + 9:c0 + 10], S[:, c0 + 7:c0 + 8], 127.0, None,
                    ALU.mult))
                E.opi("v", lambda e, c0=c0: e.tensor_scalar(
                    S[:, c0 + 10:c0 + 11], S[:, c0 + 8:c0 + 9], 0.5 / 127.0,
                    None, ALU.mult), vE, 1)
                E.dr("v")
            for mi in range(2):
                c0 = 16 * mi
                for kq in range(4):
                    E.op("v", lambda e, mi=mi, kq=kq, c0=c0: e.tensor_scalar(
                        sq[:], hb[mi][:, 2048 * kq:2048 * (kq + 1)],
                        S[:, c0 + 9:c0 + 10], MAGIC, ALU.mult, ALU.add))
                    E.opi("v", lambda e: e.tensor_scalar(
                        xqb[:], sq[:], MAGIC, None, ALU.subtract), vE, 1)
                    E.wait("t", vE)
                    _transpose_128(b, lambda k: xqb[:, 128 * k:128 * (k + 1)],
                                   16, psT, hqT[mi], 2048 * kq)
            with ExitStack() as pf:
                psD = [pf.enter_context(_psum(nc, f"f0_psD{i}", [128, 512], F32))
                       for i in range(4)]
                visit = 0
                tEh2 = {}
                tEh2[-2] = tEh2[-1] = getattr(b, "f0_prev_ffn2_last", 0)
                for half in range(2):
                    for kq in range(4):
                        for nd2 in range(2):
                            nd = 2 * half + nd2
                            q = qW0 if visit % 2 == 0 else qW1
                            if tEh2[visit - 2]:
                                E.wait("y", tE, tEh2[visit - 2])
                            for hh in range(2):
                                E.opi("y", lambda e, kq=kq, nd=nd, hh=hh, v2=visit:
                                      e.dma_start(
                                          out=w2s[v2 % 2][:, 4096 * hh:4096 * (hh + 1)]
                                          .rearrange("p (c d) -> p c d", d=512),
                                          in_=dr["w2b"].ap()[16 * kq + 8 * hh:
                                                             16 * kq + 8 * (hh + 1),
                                                             :, 512 * nd:512 * (nd + 1)]
                                          .transpose([1, 0, 2])), q, 16)
                            E.wait("t", q)
                            if kq == 0:
                                E.wait("t", vE)
                            for mi in range(2):
                                bank = psD[2 * mi + nd2]
                                for k in range(16):
                                    E.opi("t", lambda e, mi=mi, k=k, kq=kq, v2=visit,
                                          bank=bank: e.matmul(
                                              bank[:],
                                              hqT[mi][:, 2048 * kq + 128 * k:
                                                      2048 * kq + 128 * (k + 1)],
                                              w2s[v2 % 2][:, 512 * k:512 * (k + 1)],
                                              start=(kq == 0 and k == 0),
                                              stop=(kq == 3 and k == 15)),
                                          tE, 1 if k == 15 else 0)
                            tEh2[visit] = tE.cnt
                            visit += 1
                    E.wait("v", tE)
                    for mi in range(2):
                        c0 = 16 * mi
                        for nd2 in range(2):
                            nd = 2 * half + nd2
                            bank = psD[2 * mi + nd2]
                            E.opi("v", lambda e, mi=mi, nd=nd, c0=c0, bank=bank:
                                  e.scalar_tensor_tensor(
                                      xt[mi][:, 512 * nd:512 * (nd + 1)],
                                      bank[:], S[:, c0 + 10:c0 + 11],
                                      xt[mi][:, 512 * nd:512 * (nd + 1)],
                                      ALU.mult, ALU.add), vE, 1)
            b.f0_prev_ffn2_last = E.sem("tE").cnt
            E.wait("g", vE)
            for mi in range(2):
                m = 2 * p + mi
                E.opi("g", lambda e, mi=mi, m=m: e.dma_start(
                    out=xdst[128 * m:128 * (m + 1), :], in_=xt[mi][:]), qO, 16)
            qO_hist.append(qO.cnt)
        E.join()
        E.flush_block("ffn0")



def _ffn1(b, src, dst, final=False):
    nc, E, io, dr, g, st_ = b.nc, b.E, b.io, b.dr, b.g, b.st
    qW0, qW1 = E.sem("qW0"), E.sem("qW1")
    qX0, qX1 = E.sem("qX0"), E.sem("qX1")
    qA0, qA1 = E.sem("qA0"), E.sem("qA1")
    qO = E.sem("qO")
    tE, vE, sE = E.sem("tE"), E.sem("vE"), E.sem("sE")
    xsrc = dr[src].ap()
    xdst = io["y"] if final else dr[dst].ap()
    S = g["stat"]

    E.wait("y", E.sem("cc"), b.cc_marks["db"])
    with ExitStack() as px:
        xt = [px.enter_context(st_(f"f1_xt{i}", [128, D], F32)) for i in range(2)]
        sq = px.enter_context(st_("f1_sq", [128, D], F32))
        xqb = px.enter_context(st_("f1_xqb", [128, D], BF16))
        xqT = [px.enter_context(st_(f"f1_xqT{i}", [128, D], BF16)) for i in range(2)]
        gst = [px.enter_context(st_(f"f1_gst{i}", [128, 512], F32)) for i in range(2)]
        mid = px.enter_context(st_("f1_mid", [128, 512], BF16))
        midT = [px.enter_context(st_(f"f1_midT{i}", [128, 44 * 128], BF16))
                for i in range(2)]
        wgs = [px.enter_context(st_(f"f1_wgs{i}", [128, 16 * 512], BF16))
               for i in range(2)]
        wus = [px.enter_context(st_(f"f1_wus{i}", [128, 16 * 512], BF16))
               for i in range(2)]
        wds = [px.enter_context(st_(f"f1_wds{i}", [128, 11 * 512], BF16))
               for i in range(2)]
        psT = px.enter_context(_psum(nc, "f1_psT", [128, 1024], BF16))
        for p in range(4):
            qA = qA0 if p % 2 == 0 else qA1
            if p >= 1:
                E.wait("y", qO)
            for mi in range(2):
                m = 2 * p + mi
                E.opi("y", lambda e, mi=mi, m=m: e.dma_start(
                    out=xt[mi][:], in_=xsrc[128 * m:128 * (m + 1), :]), qA, 16)
            for mi in range(2):
                E.wait("v", qA)
                E.wait("s", qA)
                _ln_stats(b, xt[mi][:], sq, mi, quant=False)
                c0 = 16 * mi
                E.dr("v")
                E.opi("v", lambda e, mi=mi, c0=c0: e.tensor_scalar(
                    xqb[:], xt[mi][:], S[:, c0:c0 + 1], S[:, c0 + 1:c0 + 2],
                    ALU.subtract, ALU.mult), vE, 1)
                E.wait("t", vE)
                _transpose_128(b, lambda k: xqb[:, 128 * k:128 * (k + 1)],
                               16, psT, xqT[mi], 0)
            with ExitStack() as pf:
                psG = [pf.enter_context(_psum(nc, f"f1_psG{i}", [128, 512], F32))
                       for i in range(2)]
                psU = [pf.enter_context(_psum(nc, f"f1_psU{i}", [128, 512], F32))
                       for i in range(2)]
                tEh, vEh, sEh = {}, {}, {}
                tEh[-2] = tEh[-1] = getattr(b, "f1_prev_gu_last", 0)
                for n in range(11):
                    q = qW0 if n % 2 == 0 else qW1
                    qg = qX0 if n % 2 == 0 else qX1
                    if tEh[n - 2]:
                        E.wait("y", tE, tEh[n - 2])
                    for hh in range(2):
                        E.opi("y", lambda e, n=n, hh=hh: e.dma_start(
                            out=wgs[n % 2][:, 4096 * hh:4096 * (hh + 1)]
                            .rearrange("p (c h) -> p c h", h=512),
                            in_=dr["gb"].ap()[8 * hh:8 * (hh + 1), :,
                                              512 * n:512 * (n + 1)]
                            .transpose([1, 0, 2])), q, 16)
                        E.opi("y", lambda e, n=n, hh=hh: e.dma_start(
                            out=wus[n % 2][:, 4096 * hh:4096 * (hh + 1)]
                            .rearrange("p (c h) -> p c h", h=512),
                            in_=dr["ub"].ap()[8 * hh:8 * (hh + 1), :,
                                              512 * n:512 * (n + 1)]
                            .transpose([1, 0, 2])), q, 16)
                    E.wait("t", q)
                    if n >= 1:
                        E.wait("t", sE, sEh[n - 1])
                        E.wait("t", vE, vEh[n - 1])
                    for mi in range(2):
                        for k in range(16):
                            E.opi("t", lambda e, mi=mi, k=k, n=n: e.matmul(
                                psG[mi][:], xqT[mi][:, 128 * k:128 * (k + 1)],
                                wgs[n % 2][:, 512 * k:512 * (k + 1)],
                                start=(k == 0), stop=(k == 15)),
                                tE, 1 if k == 15 else 0)
                        for k in range(16):
                            E.opi("t", lambda e, mi=mi, k=k, n=n: e.matmul(
                                psU[mi][:], xqT[mi][:, 128 * k:128 * (k + 1)],
                                wus[n % 2][:, 512 * k:512 * (k + 1)],
                                start=(k == 0), stop=(k == 15)),
                                tE, 1 if k == 15 else 0)
                    tEh[n] = tE.cnt
                    E.wait("s", tE)
                    for mi in range(2):
                        E.opi("s", lambda e, mi=mi: e.activation(
                            gst[mi][:], psG[mi][:], AF.Silu), sE, 1)
                    sEh[n] = sE.cnt
                    E.wait("v", sE)
                    for mi in range(2):
                        E.opi("v", lambda e, mi=mi: e.tensor_tensor(
                            mid[:], gst[mi][:], psU[mi][:], ALU.mult), vE, 1)
                        E.wait("t", vE)
                        nch = 4 if n < 10 else 4
                        _transpose_128(b, lambda k: mid[:, 128 * k:128 * (k + 1)],
                                       4, psT, midT[mi], 512 * n)
                    vEh[n] = vE.cnt
            b.f1_prev_gu_last = E.sem("tE").cnt
            with ExitStack() as pf:
                psD = [pf.enter_context(_psum(nc, f"f1_psD{i}", [128, 512], F32))
                       for i in range(4)]
                visit = 0
                tEh2 = {}
                tEh2[-2] = tEh2[-1] = getattr(b, "f1_prev_dn_last", 0)
                for half in range(2):
                    for kdq in range(4):
                        nk = 11
                        for nd2 in range(2):
                            nd = 2 * half + nd2
                            q = qW0 if visit % 2 == 0 else qW1
                            if tEh2[visit - 2]:
                                E.wait("y", tE, tEh2[visit - 2])
                            E.opi("y", lambda e, kdq=kdq, nd=nd, v2=visit:
                                  e.dma_start(
                                      out=wds[v2 % 2][:, :11 * 512]
                                      .rearrange("p (c d) -> p c d", d=512),
                                      in_=dr["db"].ap()[11 * kdq:11 * (kdq + 1),
                                                        :, 512 * nd:512 * (nd + 1)]
                                      .transpose([1, 0, 2])), q, 16)
                            E.wait("t", q)
                            if kdq == 0:
                                E.wait("t", vE)
                            for mi in range(2):
                                bank = psD[2 * mi + nd2]
                                for k in range(11):
                                    kk = 11 * kdq + k
                                    E.opi("t", lambda e, mi=mi, k=k, kk=kk, v2=visit,
                                          bank=bank: e.matmul(
                                              bank[:],
                                              midT[mi][:, 128 * kk:128 * (kk + 1)],
                                              wds[v2 % 2][:, 512 * k:512 * (k + 1)],
                                              start=(kk == 0), stop=(kk == 43)),
                                          tE, 1 if k == 10 else 0)
                            tEh2[visit] = tE.cnt
                            visit += 1
                    E.wait("v", tE)
                    for mi in range(2):
                        for nd2 in range(2):
                            nd = 2 * half + nd2
                            bank = psD[2 * mi + nd2]
                            E.opi("v", lambda e, mi=mi, nd=nd, bank=bank:
                                  e.scalar_tensor_tensor(
                                      xt[mi][:, 512 * nd:512 * (nd + 1)],
                                      bank[:], 0.5,
                                      xt[mi][:, 512 * nd:512 * (nd + 1)],
                                      ALU.mult, ALU.add), vE, 1)
            b.f1_prev_dn_last = E.sem("tE").cnt
            E.wait("g", vE)
            for mi in range(2):
                m = 2 * p + mi
                E.opi("g", lambda e, mi=mi, m=m: e.dma_start(
                    out=xdst[128 * m:128 * (m + 1), :], in_=xt[mi][:]), qO, 16)
        E.join()
        E.flush_block("ffn1")



def _lattn(b, src, dst, final=False):
    nc, E, io, dr, g, st_ = b.nc, b.E, b.io, b.dr, b.g, b.st
    qW0, qW1 = E.sem("qW0"), E.sem("qW1")
    qX0, qX1 = E.sem("qX0"), E.sem("qX1")
    qA0, qA1 = E.sem("qA0"), E.sem("qA1")
    qS, qO, qG = E.sem("qS"), E.sem("qO"), E.sem("qG")
    tE, vE, sE, cc = E.sem("tE"), E.sem("vE"), E.sem("sE"), E.sem("cc")
    xsrc = dr[src].ap()
    xdst = io["y"] if final else dr[dst].ap()
    S = g["stat"]

    with ExitStack() as px:
        xt1 = px.enter_context(st_("la_xt", [128, D], F32))
        sq = px.enter_context(st_("la_sq", [128, D], F32))
        xqb = px.enter_context(st_("la_xqb", [128, D], BF16))
        xlnT = px.enter_context(st_("la_xlnT", [128, 8 * D], BF16))
        mr = px.enter_context(st_("la_mr", [128, 16], F32))
        psT = px.enter_context(_psum(nc, "la_psT", [128, 1024], BF16))
        for m in range(8):
            qA = qA0 if m % 2 == 0 else qA1
            if m >= 1:
                E.wait("y", vE)
            E.opi("y", lambda e, m=m: e.dma_start(
                out=xt1[:], in_=xsrc[128 * m:128 * (m + 1), :]), qA, 16)
            E.wait("v", qA)
            E.wait("s", qA)
            _ln_stats(b, xt1[:], sq, 0, quant=False)
            E.dr("v")
            E.opi("v", lambda e, m=m: e.tensor_scalar(
                xqb[:], xt1[:], S[:, 0:1], S[:, 1:2],
                ALU.subtract, ALU.mult), vE, 1)
            E.wait("t", vE)
            _transpose_128(b, lambda k: xqb[:, 128 * k:128 * (k + 1)],
                           16, psT, xlnT, D * m)
        with ExitStack() as pf:
            wsl = [pf.enter_context(st_(f"la_w{i}", [128, 16 * 512], BF16))
                   for i in range(2)]
            brow = pf.enter_context(st_("la_brow", [1, 3 * D], BF16))
            stg = [pf.enter_context(st_(f"la_stg{i}", [128, 512], BF16))
                   for i in range(4)]
            psP = [pf.enter_context(_psum(nc, f"la_psP{i}", [128, 512], F32))
                   for i in range(4)]
            E.opi("g", lambda e: e.dma_start(out=brow[:], in_=io["bqkv"]), qG, 16)
            tEh = {}
            vEh = {}
            qO_hist = {}
            for n in range(12):
                q = qX0 if n % 2 == 0 else qX1
                if n >= 2:
                    E.wait("g", tE, tEh[n - 2])
                for hh in range(2):
                    E.opi("g", lambda e, n=n, hh=hh: e.dma_start(
                        out=wsl[n % 2][:, 4096 * hh:4096 * (hh + 1)]
                        .rearrange("p (c o) -> p c o", o=512),
                        in_=io["wqkv"].rearrange("(c p) o -> p c o", p=128)
                        [:, 8 * hh:8 * (hh + 1), 512 * n:512 * (n + 1)]), q, 16)
                E.wait("t", q)
                E.wait("t", qG)
                for mh in range(2):
                    E.wait("t", vE)
                    for m4 in range(4):
                        m = 4 * mh + m4
                        for k in range(16):
                            E.opi("t", lambda e, m=m, k=k, n=n, m4=m4: e.matmul(
                                psP[m4][:],
                                xlnT[:, D * m + 128 * k:D * m + 128 * (k + 1)],
                                wsl[n % 2][:, 512 * k:512 * (k + 1)],
                                start=(k == 0), stop=False), tE, 0)
                        E.opi("t", lambda e, n=n, m4=m4: e.matmul(
                            psP[m4][:], g["ones_rbf"][0:1, :],
                            brow[0:1, 512 * n:512 * (n + 1)],
                            start=False, stop=True), tE, 1)
                    tEh[n] = tE.cnt
                    E.wait("v", tE)
                    for m4 in range(4):
                        m = 4 * mh + m4
                        si = m4
                        if (n, mh) > (0, 0):
                            E.wait("v", qO, qO_hist.get((n, mh, m4), 0))
                        E.opi("v", lambda e, m4=m4, si=si: e.tensor_copy(
                            stg[si][:], psP[m4][:]), vE, 1)
                        E.wait("g", vE)
                        E.opi("g", lambda e, m=m, n=n, si=si: e.dma_start(
                            out=dr["qkv0"].ap()[128 * m:128 * (m + 1),
                                                512 * n:512 * (n + 1)],
                            in_=stg[si][:]), qO, 16)
                        nxt = (n, mh, m4) if mh == 1 else (n, mh + 1, m4)
                        nxt = (n + 1, 0, m4) if mh == 1 else (n, 1, m4)
                        qO_hist[nxt] = qO.cnt
        E.join()
        E.flush_block("la_proj")

    with ExitStack() as pxo:
      oT = pxo.enter_context(st_("la_oT", [128, 16 * 1024], BF16))
      with ExitStack() as px:
          scol = px.enter_context(st_("la_scol", [128, 16], F32))
          nscol = px.enter_context(st_("la_nscol", [128, 16], F32))
          bdcol = px.enter_context(st_("la_bdcol", [128, 16], F32))
          kdcol = px.enter_context(st_("la_kdcol", [128, 16], F32))
          nsdiag = px.enter_context(st_("la_nsdiag", [16, 1], F32))
          k64c = px.enter_context(st_("la_k64c", [128, 1], F32))
          Dts = px.enter_context(st_("la_Dts", [64, 64], F32))
          Pcs = px.enter_context(st_("la_Pcs", [16, 8], F32))
          W16 = px.enter_context(st_("la_W16", [16, 8], F32))
          Wcols = px.enter_context(st_("la_Wcols", [128, 128], F32))
          decT = px.enter_context(st_("la_decT", [64, 16 * 64], BF16))
          rowq = px.enter_context(st_("la_rowq", [1, TC], F32))
          g["row"] = px.enter_context(st_("la_row", [1, TC], F32))
          g["row2"] = px.enter_context(st_("la_row2", [1, TC], F32))
          qd64 = px.enter_context(st_("la_qd64", [128, TC], F32))
          qgdec = px.enter_context(st_("la_qgdec", [128, TC], F32))
          slrow = px.enter_context(st_("la_slrow", [1, 16], F32))
          q_h = px.enter_context(st_("la_qh", [128, 1024], BF16))
          k_h = px.enter_context(st_("la_kh", [128, 1024], BF16))
          k64 = px.enter_context(st_("la_k64", [64, 2048], BF16))
          v64 = px.enter_context(st_("la_v64", [64, 2048], BF16))
          qT = px.enter_context(st_("la_qT", [128, 1024], BF16))
          kT = px.enter_context(st_("la_kT", [128, 1024], BF16))
          qd = px.enter_context(st_("la_qd", [128, 1024], BF16))
          qgdT = px.enter_context(st_("la_qgdT", [128, 16 * 1024], BF16))
          kd_st = px.enter_context(st_("la_kdst", [64, 128], BF16))
          PT = px.enter_context(st_("la_PT", [64, 64], BF16))
          Ssb = px.enter_context(st_("la_Ssb", [128, 128], F32))
          Sbf = px.enter_context(st_("la_Sbf", [128, 128], BF16))
          psT = px.enter_context(_psum(nc, "la_psT2", [128, 1024], BF16))
          psA = [px.enter_context(_psum(nc, f"la_psA{i}", [64, 64], F32))
                 for i in range(2)]
          psS = px.enter_context(_psum(nc, "la_psS", [128, 128], F32))
          psO = [px.enter_context(_psum(nc, f"la_psO{i}", [128, 64], F32))
                 for i in range(2)]
          psB = px.enter_context(_psum(nc, "la_psB", [128, 512], F32))
          E.opi("y", lambda e: e.dma_start(out=slrow[:], in_=io["slopes"]), qS, 16)
          E.opi("y", lambda e: e.dma_start(out=k64c[:], in_=io["k64m"]), qS, 16)
          E.opi("y", lambda e: e.dma_start(out=Dts[:], in_=io["Dt"]), qS, 16)
          E.opi("y", lambda e: e.dma_start(out=Pcs[:], in_=io["Pc"]), qS, 16)
          E.opi("y", lambda e: e.dma_start(out=g["row"][0:1, :TC],
                                           in_=io["iota_m64"]), qS, 16)
          E.opi("y", lambda e: e.dma_start(out=g["row2"][0:1, :TC],
                                           in_=io["iota_loc"]), qS, 16)
          E.wait("t", qS)
          E.opi("t", lambda e: e.matmul(psB[:, 0:16], g["ones_f"][0:1, :],
                                        slrow[0:1, :], start=True, stop=True), tE, 1)
          E.opi("t", lambda e: e.transpose(psS[0:16, 0:1], slrow[0:1, :],
                                           g["ident_f"][0:1, 0:1]), tE, 1)
          E.wait("v", tE)
          E.opi("v", lambda e: e.tensor_copy(scol[:], psB[:, 0:16]), vE, 1)
          E.dr("v")
          E.op("v", lambda e: e.tensor_scalar(nscol[:], scol[:], -1.0, None,
                                              ALU.mult))
          E.dr("v")
          E.opi("v", lambda e: e.tensor_scalar(nsdiag[:], psS[0:16, 0:1], -1.0,
                                               None, ALU.mult), vE, 1)
          E.dr("v")
          E.wait("s", vE)
          E.opi("s", lambda e: e.activation(bdcol[:], scol[:], AF.Exp,
                                            scale=-64.0), sE, 1)
          for h in range(NH):
              E.opi("s", lambda e, h=h: e.activation(
                  kdcol[:, h:h + 1], k64c[:], AF.Exp,
                  scale=nscol[:, h:h + 1]), sE, 1)
              E.opi("s", lambda e, h=h: e.activation(
                  decT[:, 64 * h:64 * (h + 1)], Dts[:], AF.Exp,
                  scale=nscol[0:64, h:h + 1]), sE, 1)
          E.opi("s", lambda e: e.activation(W16[:], Pcs[:], AF.Exp,
                                            scale=nsdiag[:]), sE, 1)
          E.wait("t", sE)
          for r in range(8):
              E.opi("t", lambda e, r=r: e.transpose(
                  psB[0:1, 16 * r:16 * (r + 1)], W16[:, r:r + 1],
                  g["ident_f"][0:16, 0:16]), tE, 1)
          E.wait("v", tE)
          E.opi("v", lambda e: e.tensor_copy(rowq[0:1, 0:128], psB[0:1, 0:128]),
                vE, 1)
          E.wait("t", vE)
          E.opi("t", lambda e: e.matmul(psB[:, 0:128], g["ones_f"][0:1, :],
                                        rowq[0:1, 0:128], start=True, stop=True),
                tE, 1)
          E.wait("v", tE)
          E.opi("v", lambda e: e.tensor_copy(Wcols[:], psB[:, 0:128]), vE, 1)
          E.join()

          for h in range(NH):
              qX = qX0 if h % 2 == 0 else qX1
              if h >= 1:
                  E.wait("g", tE)
                  E.wait("g", vE)
              E.opi("g", lambda e, h=h: e.dma_start(
                  out=q_h[:].rearrange("p (m c) -> p m c", c=128),
                  in_=dr["qkv0"].ap()[:, 128 * h:128 * (h + 1)]
                  .rearrange("(m p) c -> m p c", p=128).transpose([1, 0, 2])),
                  qX, 16)
              E.opi("g", lambda e, h=h: e.dma_start(
                  out=k_h[:].rearrange("p (m c) -> p m c", c=128),
                  in_=dr["qkv0"].ap()[:, D + 128 * h:D + 128 * (h + 1)]
                  .rearrange("(m p) c -> m p c", p=128).transpose([1, 0, 2])),
                  qX, 16)
              E.opi("g", lambda e, h=h: e.dma_start(
                  out=k64[:].rearrange("p (m c) -> p m c", c=128),
                  in_=dr["qkv0"].ap()[:, D + 128 * h:D + 128 * (h + 1)]
                  .rearrange("(m p) c -> m p c", p=64).transpose([1, 0, 2])),
                  qX, 16)
              E.opi("g", lambda e, h=h: e.dma_start(
                  out=v64[:].rearrange("p (m c) -> p m c", c=128),
                  in_=dr["qkv0"].ap()[:, 2 * D + 128 * h:2 * D + 128 * (h + 1)]
                  .rearrange("(m p) c -> m p c", p=64).transpose([1, 0, 2])),
                  qX, 16)
              E.wait("t", qX)
              _transpose_128(b, lambda k: q_h[:, 128 * k:128 * (k + 1)], 8, psT, qT, 0)
              _transpose_128(b, lambda k: k_h[:, 128 * k:128 * (k + 1)], 8, psT, kT, 0)
              E.wait("s", tE)
              E.opi("s", lambda e, h=h: e.activation(
                  rowq[:], g["row"][0:1, :TC], AF.Exp,
                  scale=nscol[0:1, h:h + 1]), sE, 1)
              E.wait("t", sE)
              _bcast(b, psB, lambda j, ww: rowq[0:1, 512 * j:512 * j + ww], qd64, TC)
              E.wait("s", tE)
              E.opi("s", lambda e, h=h: e.activation(
                  rowq[:], g["row2"][0:1, :TC], AF.Exp,
                  scale=nscol[0:1, h:h + 1]), sE, 1)
              E.wait("t", sE)
              _bcast(b, psB, lambda j, ww: rowq[0:1, 512 * j:512 * j + ww], qgdec, TC)
              E.wait("v", tE)
              E.dr("v")
              E.opi("v", lambda e: e.tensor_tensor(qd[:], qT[:], qd64[:], ALU.mult),
                    vE, 1)
              E.opi("v", lambda e, h=h: e.tensor_tensor(
                  qgdT[:, 1024 * h:1024 * (h + 1)], qT[:], qgdec[:], ALU.mult),
                  vE, 1)
              E.wait("v", E.sem("qO"))
              E.op("v", lambda e: e.memset(Ssb[:], 0.0))
              E.opi("v", lambda e: e.memset(Sbf[:], 0.0), vE, 1)
              E.wait("t", vE)
              sE_evac = {}
              for c_ in range(16):
                  pa = psA[c_ % 2]
                  po = psO[c_ % 2]
                  if c_ >= 2:
                      E.wait("t", sE, sE_evac[c_ - 2])
                  E.opi("t", lambda e, c_=c_, pa=pa: e.matmul(
                      pa[:], kT[:, 64 * c_:64 * (c_ + 1)],
                      qT[:, 64 * c_:64 * (c_ + 1)], start=True, stop=True), tE, 1)
                  E.wait("v", tE)
                  E.opi("v", lambda e, h=h, pa=pa: e.tensor_tensor(
                      PT[:], pa[:], decT[:, 64 * h:64 * (h + 1)], ALU.mult), vE, 1)
                  E.opi("v", lambda e, c_=c_, h=h: e.tensor_scalar(
                      kd_st[:], k64[:, 128 * c_:128 * (c_ + 1)],
                      kdcol[0:64, h:h + 1], None, ALU.mult), vE, 1)
                  E.wait("t", vE)
                  E.opi("t", lambda e, c_=c_, po=po: e.matmul(
                      po[:], v64[:, 128 * c_:128 * (c_ + 1)], PT[:],
                      start=True, stop=False), tE, 1)
                  E.opi("t", lambda e, c_=c_, po=po: e.matmul(
                      po[:], Sbf[:], qd[:, 64 * c_:64 * (c_ + 1)],
                      start=False, stop=True), tE, 1)
                  E.opi("t", lambda e, c_=c_: e.matmul(
                      psS[:], kd_st[:], v64[:, 128 * c_:128 * (c_ + 1)],
                      start=True, stop=True), tE, 1)
                  E.wait("v", tE)
                  E.opi("v", lambda e, h=h: e.scalar_tensor_tensor(
                      Ssb[:], Ssb[:], bdcol[:, h:h + 1], psS[:],
                      ALU.mult, ALU.add), vE, 1)
                  E.dr("v")
                  E.opi("v", lambda e: e.tensor_copy(Sbf[:], Ssb[:]), vE, 1)
                  E.wait("s", tE)
                  E.opi("s", lambda e, c_=c_, h=h, po=po: e.activation(
                      oT[:, 1024 * h + 64 * c_:1024 * h + 64 * (c_ + 1)], po[:],
                      AF.Copy), sE, 1)
                  sE_evac[c_] = sE.cnt
              E.wait("g", vE)
              E.wait("g", sE)
              E.opi("g", lambda e, h=h: e.dma_start(
                  out=dr["Tc_sh"].ap()[h], in_=Ssb[:]), qO, 16)
          _ag(b, "Tc_sh", "Tc")

          with ExitStack() as pf:
              tc0 = pf.enter_context(st_("la_tc0", [128, 128], F32))
              tc1 = pf.enter_context(st_("la_tc1", [128, 128], F32))
              Sin = pf.enter_context(st_("la_Sin", [128, 128], F32))
              Sinb = pf.enter_context(st_("la_Sinb", [128, 128], BF16))
              E.wait("g", cc)
              for h in range(NH):
                  E.op("v", lambda e: e.memset(Sin[:], 0.0))
                  for r in range(8):
                      buf = tc0 if r % 2 == 0 else tc1
                      q = qX0 if r % 2 == 0 else qX1
                      if r >= 2:
                          E.wait("g", vE)
                      E.opi("g", lambda e, r=r, h=h, bu=buf: e.dma_start(
                          out=bu[:], in_=dr["Tc"].ap()[r, h]), q, 16)
                      E.wait("v", q)
                      E.opi("v", lambda e, r=r, h=h, bu=buf: e.scalar_tensor_tensor(
                          Sin[:], bu[:], Wcols[:, 16 * r + h:16 * r + h + 1],
                          Sin[:], ALU.mult, ALU.add), vE, 1)
                  E.opi("v", lambda e: e.tensor_copy(Sinb[:], Sin[:]), vE, 1)
                  E.wait("t", vE)
                  for j in range(2):
                      E.opi("t", lambda e, h=h, j=j: e.matmul(
                          psB[:], Sinb[:],
                          qgdT[:, 1024 * h + 512 * j:1024 * h + 512 * (j + 1)],
                          start=True, stop=True), tE, 1)
                      E.wait("v", tE)
                      E.opi("v", lambda e, h=h, j=j: e.tensor_tensor(
                          oT[:, 1024 * h + 512 * j:1024 * h + 512 * (j + 1)],
                          oT[:, 1024 * h + 512 * j:1024 * h + 512 * (j + 1)],
                          psB[:], ALU.add), vE, 1)
                      E.wait("t", vE)
          E.join()

      with ExitStack() as pf:
          xts = [pf.enter_context(st_(f"la_x{i}", [128, D], F32))
                 for i in range(8)]
          wsl = [pf.enter_context(st_(f"la_wo{i}", [128, 16 * 512], BF16))
                 for i in range(2)]
          brow = pf.enter_context(st_("la_bo", [1, D], BF16))
          psP = [pf.enter_context(_psum(nc, f"la_psQ{i}", [128, 512], F32))
                 for i in range(2)]
          E.opi("g", lambda e: e.dma_start(out=brow[:], in_=io["bout"]), qG, 16)
          for m in range(8):
              E.opi("y", lambda e, m=m: e.dma_start(
                  out=xts[m][:], in_=xsrc[128 * m:128 * (m + 1), :]), qA0, 16)
          tEh = {}
          for n in range(4):
              q = qX0 if n % 2 == 0 else qX1
              if n >= 2:
                  E.wait("g", tE, tEh[n - 2])
              E.opi("g", lambda e, n=n: e.dma_start(
                  out=wsl[n % 2][:, :].rearrange("p (c o) -> p c o", o=512),
                  in_=io["wout"].rearrange("(c p) o -> p c o", p=128)
                  [:, :, 512 * n:512 * (n + 1)]), q, 16)
              E.wait("t", q)
              E.wait("t", qG)
              E.wait("t", qA0)
              for m in range(8):
                  pb = psP[m % 2]
                  if m >= 2:
                      E.wait("t", vE)
                  for k in range(16):
                      E.opi("t", lambda e, m=m, k=k, n=n, pb=pb: e.matmul(
                          pb[:], oT[:, 1024 * k + 128 * m:1024 * k + 128 * (m + 1)],
                          wsl[n % 2][:, 512 * k:512 * (k + 1)],
                          start=(k == 0), stop=False), tE, 0)
                  E.opi("t", lambda e, n=n, pb=pb: e.matmul(
                      pb[:], g["ones_rbf"][0:1, :],
                      brow[0:1, 512 * n:512 * (n + 1)],
                      start=False, stop=True), tE, 1)
                  E.wait("v", tE)
                  E.opi("v", lambda e, m=m, n=n, pb=pb: e.tensor_tensor(
                      xts[m][:, 512 * n:512 * (n + 1)],
                      xts[m][:, 512 * n:512 * (n + 1)], pb[:], ALU.add),
                      vE, 1)
              tEh[n] = tE.cnt
          E.wait("g", vE)
          for m in range(8):
              E.opi("g", lambda e, m=m: e.dma_start(
                  out=xdst[128 * m:128 * (m + 1), :], in_=xts[m][:]), qO, 16)
      E.join()
      E.flush_block("la_main")



def _sattn(b, src, dst, final=False):
    nc, E, io, dr, g, st_ = b.nc, b.E, b.io, b.dr, b.g, b.st
    qW0, qW1 = E.sem("qW0"), E.sem("qW1")
    qX0, qX1 = E.sem("qX0"), E.sem("qX1")
    qA0, qA1 = E.sem("qA0"), E.sem("qA1")
    qS, qO, qG = E.sem("qS"), E.sem("qO"), E.sem("qG")
    tE, vE, sE, cc = E.sem("tE"), E.sem("vE"), E.sem("sE"), E.sem("cc")
    xsrc = dr[src].ap()
    xdst = io["y"] if final else dr[dst].ap()
    S = g["stat"]

    def emit_qslab(dst, mat, n, q):
        for c in range(2):
            E.opi("y", lambda e, mat=mat, n=n, c=c: e.dma_start(
                out=dst[:].rearrange("p (r c o) -> p r c o", c=2, o=512)
                [:, :, c, :],
                in_=dr["qkvob"].ap()[:, mat, c, :, 512 * n:512 * (n + 1)]
                .transpose([1, 0, 2])), q, 16)

    E.wait("y", E.sem("cc"), b.cc_marks["qkvob"])
    with ExitStack() as px:
        xt1 = px.enter_context(st_("sa_xt", [128, D], F32))
        sq = px.enter_context(st_("sa_sq", [128, D], F32))
        xqb = px.enter_context(st_("sa_xqb", [128, D], BF16))
        xlnT = px.enter_context(st_("sa_xlnT", [128, 8 * D], BF16))
        psT = px.enter_context(_psum(nc, "sa_psT", [128, 1024], BF16))
        for m in range(8):
            qA = qA0 if m % 2 == 0 else qA1
            if m >= 1:
                E.wait("y", vE)
            E.opi("y", lambda e, m=m: e.dma_start(
                out=xt1[:], in_=xsrc[128 * m:128 * (m + 1), :]), qA, 16)
            E.wait("v", qA)
            E.wait("s", qA)
            _ln_stats(b, xt1[:], sq, 0, quant=False)
            E.dr("v")
            E.opi("v", lambda e, m=m: e.tensor_scalar(
                xqb[:], xt1[:], S[:, 0:1], S[:, 1:2],
                ALU.subtract, ALU.mult), vE, 1)
            E.wait("t", vE)
            _transpose_128(b, lambda k: xqb[:, 128 * k:128 * (k + 1)],
                           16, psT, xlnT, D * m)
        coss = px.enter_context(st_("sa_cos", [128, 8 * 64], F32))
        sins = px.enter_context(st_("sa_sin", [128, 8 * 64], F32))
        E.opi("y", lambda e: e.dma_start(
            out=coss[:].rearrange("p (m f) -> p m f", f=64),
            in_=io["cos"].rearrange("(m p) f -> m p f", p=128)
            .transpose([1, 0, 2])), qS, 16)
        E.opi("y", lambda e: e.dma_start(
            out=sins[:].rearrange("p (m f) -> p m f", f=64),
            in_=io["sin"].rearrange("(m p) f -> m p f", p=128)
            .transpose([1, 0, 2])), qS, 16)

        with ExitStack() as pf:
            wsl = [pf.enter_context(st_(f"sa_w{i}", [128, 16 * 512], BF16))
                   for i in range(2)]
            stg = [pf.enter_context(st_(f"sa_stg{i}", [128, 512], BF16))
                   for i in range(4)]
            qk_sb = pf.enter_context(st_("sa_qk", [128, 8 * D], BF16))
            rt = [pf.enter_context(st_(f"sa_rt{i}", [128, 1024], F32))
                  for i in range(4)]
            kstg = [pf.enter_context(st_(f"sa_kstg{i}", [128, 16 * 128], BF16))
                    for i in range(2)]
            psP = [pf.enter_context(_psum(nc, f"sa_psP{i}", [128, 512], F32))
                   for i in range(4)]

            vq_hist = {}
            vreg = (dr["qkv1_sh"].ap()[2].rearrange("h a t -> h (a t)")
                    .rearrange("h (t c) -> h t c", c=128))
            for n in range(4):
                q = qW0 if n % 2 == 0 else qW1
                if n >= 2:
                    E.wait("y", tE)
                emit_qslab(wsl[n % 2], 2, n, q)
                E.wait("t", q)
                for mh in range(2):
                    E.wait("t", vE)
                    for m4 in range(4):
                        m = 4 * mh + m4
                        for k in range(16):
                            E.opi("t", lambda e, m=m, k=k, n=n, m4=m4: e.matmul(
                                psP[m4][:],
                                xlnT[:, D * m + 128 * k:D * m + 128 * (k + 1)],
                                wsl[n % 2][:, 512 * k:512 * (k + 1)],
                                start=(k == 0), stop=(k == 15)),
                                tE, 1 if k == 15 else 0)
                    E.wait("v", tE)
                    for m4 in range(4):
                        m = 4 * mh + m4
                        vi = 2 * n + mh
                        if vi >= 1:
                            E.wait("v", qO, vq_hist.get(vi - 1, 0))
                        E.opi("v", lambda e, m4=m4: e.tensor_copy(
                            stg[m4][:], psP[m4][:]), vE, 1)
                        E.wait("g", vE)
                        E.opi("g", lambda e, m=m, n=n, m4=m4: e.dma_start(
                            out=vreg[4 * n:4 * (n + 1), 128 * m:128 * (m + 1), :]
                            .transpose([1, 0, 2]),
                            in_=stg[m4][:].rearrange("p (h c) -> p h c", c=128)),
                            qO, 16)
                        vq_hist[vi] = qO.cnt
            kq_hist = {}
            for mat in (1, 0):
                for n in range(4):
                    q = qW0 if n % 2 == 0 else qW1
                    E.wait("y", tE)
                    emit_qslab(wsl[n % 2], mat, n, q)
                    E.wait("t", q)
                    for mh in range(2):
                        E.wait("t", vE)
                        for m4 in range(4):
                            m = 4 * mh + m4
                            for k in range(16):
                                E.opi("t", lambda e, m=m, k=k, n=n, m4=m4:
                                      e.matmul(
                                          psP[m4][:],
                                          xlnT[:, D * m + 128 * k:
                                               D * m + 128 * (k + 1)],
                                          wsl[n % 2][:, 512 * k:512 * (k + 1)],
                                          start=(k == 0), stop=(k == 15)),
                                      tE, 1 if k == 15 else 0)
                        E.wait("v", tE)
                        for m4 in range(4):
                            m = 4 * mh + m4
                            if mat == 0:
                                E.opi("v", lambda e, m=m, n=n, m4=m4:
                                      e.tensor_scalar(
                                          qk_sb[:, D * m + 512 * n:
                                                D * m + 512 * (n + 1)],
                                          psP[m4][:], QSCL, None, ALU.mult),
                                      vE, 1)
                            else:
                                E.opi("v", lambda e, m=m, n=n, m4=m4:
                                      e.tensor_copy(
                                          qk_sb[:, D * m + 512 * n:
                                                D * m + 512 * (n + 1)],
                                          psP[m4][:]), vE, 1)
                dst_mat = dr["qkv1_sh"].ap()[mat]
                for m in range(8):
                    base = D * m
                    k1 = qk_sb[:, base:base + D].rearrange(
                        "p (h c) -> p h c", c=128)[:, :, 0:64]
                    k2 = qk_sb[:, base:base + D].rearrange(
                        "p (h c) -> p h c", c=128)[:, :, 64:128]
                    cosb = (coss[:, 64 * m:64 * (m + 1)]
                            .rearrange("p (a f) -> p a f", a=1)
                            .broadcast_to([128, 16, 64]))
                    sinb = (sins[:, 64 * m:64 * (m + 1)]
                            .rearrange("p (a f) -> p a f", a=1)
                            .broadcast_to([128, 16, 64]))
                    rt3 = [r[:].rearrange("p (h f) -> p h f", f=64) for r in rt]
                    E.op("v", lambda e, k1=k1, cosb=cosb, r=rt3[0]:
                         e.tensor_tensor(r, k1, cosb, ALU.mult))
                    E.op("v", lambda e, k1=k1, sinb=sinb, r=rt3[1]:
                         e.tensor_tensor(r, k1, sinb, ALU.mult))
                    E.op("v", lambda e, k2=k2, sinb=sinb, r=rt3[2]:
                         e.tensor_tensor(r, k2, sinb, ALU.mult))
                    E.op("v", lambda e, k2=k2, cosb=cosb, r=rt3[3]:
                         e.tensor_tensor(r, k2, cosb, ALU.mult))
                    E.op("v", lambda e, k1=k1, a=rt3[0], c=rt3[2]:
                         e.tensor_tensor(k1, a, c, ALU.subtract))
                    E.opi("v", lambda e, k2=k2, a=rt3[1], c=rt3[3]:
                          e.tensor_tensor(k2, a, c, ALU.add), vE, 1)
                    E.wait("t", vE)
                    if m >= 1:
                        E.wait("v", qO, kq_hist.get((mat, m - 1), 0))
                    _transpose_128(b, lambda k, base=base:
                                   qk_sb[:, base + 128 * k:base + 128 * (k + 1)],
                                   16, psT, kstg[m % 2], 0)
                    E.wait("g", vE)
                    E.opi("g", lambda e, m=m, dm=dst_mat: e.dma_start(
                        out=dm[:, :, 128 * m:128 * (m + 1)].transpose([1, 0, 2]),
                        in_=kstg[m % 2][:].rearrange("p (h t) -> p h t", t=128)),
                        qO, 16)
                    kq_hist[(mat, m)] = qO.cnt
        _ag(b, "qkv1_sh", "qkv1")
        E.join()
        E.flush_block("sa_proj")

    from concourse.bass import ds
    with ExitStack() as px:
        kT = [px.enter_context(st_(f"sa_kT{j}", [128, 8192], BF16))
              for j in range(2)]
        vsb = [px.enter_context(st_(f"sa_v{j}", [128, 8192], BF16))
               for j in range(2)]
        qTf = [px.enter_context(st_(f"sa_qTf{j}", [128, 8192], BF16))
               for j in range(2)]
        PT = [px.enter_context(st_(f"sa_PT{i}", [128, 512], BF16))
              for i in range(4)]
        obuf = [px.enter_context(st_(f"sa_ob{i}", [128, 512], BF16))
                for i in range(2)]
        rlb = [px.enter_context(st_(f"sa_rlb{i}", [128, 512], F32))
               for i in range(2)]
        lrow = px.enter_context(st_("sa_lrow", [1, 1024], F32))
        msk = px.enter_context(st_("sa_msk", [128, 4 * 512], BF16))
        psA = [px.enter_context(_psum(nc, f"sa_psA{i}", [128, 512], F32))
               for i in range(4)]
        psO = [px.enter_context(_psum(nc, f"sa_psO{i}", [128, 512], F32))
               for i in range(2)]
        psL = [px.enter_context(_psum(nc, f"sa_psL{i}", [1, 512], F32))
               for i in range(2)]
        E.wait("y", cc)
        E.opi("y", lambda e: e.dma_start(
            out=msk[:].rearrange("p (j q) -> p j q", q=512),
            in_=io["fmask"].transpose([1, 0, 2])), qS, 16)
        E.opi("v", lambda e: e.memset(S[:, 30:31], -SOFF), vE, 1)
        E.wait("s", vE)
        hpbox = {}

        def snap_heads(e):
            pid = nc.partition_id(engines=[e.engine])
            for j in range(2):
                hpbox[("hp", j)] = e.snap((pid * 2 + j) * 128,
                                          min_val=0, max_val=15 * 128)
                hpbox[("hf", j)] = e.snap((pid * 2 + j) * (128 * TC),
                                          min_val=0, max_val=15 * 128 * TC)
            return None
        E.op("y", snap_heads)

        def load_kv(e):
            for j in range(2):
                hp = hpbox[("hp", j)]
                hf = hpbox[("hf", j)]
                i1 = (dr["qkv1"].ap()[:, 1].rearrange("r h p t -> r (h p) t")
                      [:, ds(hp, 128), :].transpose([1, 0, 2]))
                e.dma_start(out=kT[j][:].rearrange("p (r t) -> p r t", t=TC),
                            in_=i1).then_inc(E.sems["qD0"].h, 16)
                i1q = (dr["qkv1"].ap()[:, 0].rearrange("r h p t -> r (h p) t")
                       [:, ds(hp, 128), :].transpose([1, 0, 2]))
                e.dma_start(out=qTf[j][:].rearrange("p (r t) -> p r t", t=TC),
                            in_=i1q).then_inc(E.sems["qD0"].h, 16)
                for r in range(8):
                    i2 = (dr["qkv1"].ap()[r, 2]
                          .rearrange("h a t -> (h a t)")[ds(hf, 128 * TC)]
                          .rearrange("(kt p c) -> kt p c", p=128, c=128)
                          .transpose([1, 0, 2]))
                    e.dma_start(
                        out=vsb[j][:, 1024 * r:1024 * (r + 1)]
                        .rearrange("p (kt c) -> p kt c", c=128),
                        in_=i2).then_inc(E.sems["qD0"].h, 16)
            return None
        E.sems["qD0"].cnt += 16 * 20
        E.op("y", load_kv)
        E.wait("t", E.sems["qD0"])
        sE_qt_last = 0
        tE_qt_last = 0
        obuf_hist = {}
        rlb_hist = {}
        for qt in range(16):
            nkt = 4 * qt + 4
            sEh, tEh = {}, {}
            for kt in range(nkt):
                for j in range(2):
                    pa = psA[2 * j + kt % 2]
                    pt = PT[2 * j + kt % 2]
                    qb = qTf[j][:, 512 * qt:512 * (qt + 1)]
                    po = psO[j]
                    pl = psL[j]
                    if kt >= 2:
                        E.wait("t", sE, sEh[(j, kt - 2)])
                    elif sE_qt_last:
                        E.wait("t", sE, sE_qt_last)
                    if kt == 0 and j == 0:
                        if qt >= 2:
                            E.wait("t", vE, obuf_hist[qt - 2])
                        if qt >= 1:
                            E.wait("t", vE, rlb_hist[qt - 1])
                    E.opi("t", lambda e, kt=kt, pa=pa, qb=qb, j=j: e.matmul(
                        pa[:], kT[j][:, 128 * kt:128 * (kt + 1)], qb,
                        start=True, stop=True), tE, 1)
                    if kt >= 2:
                        E.wait("s", tE, tEh[(j, kt - 2)])
                    elif tE_qt_last:
                        E.wait("s", tE, tE_qt_last)
                    E.wait("s", tE)
                    E.opi("s", lambda e, pa=pa, pt=pt: e.activation(
                        pt[:], pa[:], AF.Exp, bias=S[:, 30:31]), sE, 1)
                    diag = kt - 4 * qt
                    if diag >= 0:
                        E.wait("v", sE)
                        E.opi("v", lambda e, pt=pt, diag=diag: e.tensor_tensor(
                            pt[:], pt[:], msk[:, 512 * diag:512 * (diag + 1)],
                            ALU.mult), vE, 1)
                        E.wait("t", vE)
                    else:
                        E.wait("t", sE)
                    sEh[(j, kt)] = sE.cnt
                    E.opi("t", lambda e, kt=kt, pt=pt, nkt=nkt, po=po, j=j:
                          e.matmul(po[:], vsb[j][:, 128 * kt:128 * (kt + 1)],
                                   pt[:], start=(kt == 0),
                                   stop=(kt == nkt - 1)), tE, 1)
                    E.opi("t", lambda e, kt=kt, pt=pt, nkt=nkt, pl=pl:
                          e.matmul(pl[:], g["ones_bf"][:], pt[:],
                                   start=(kt == 0), stop=(kt == nkt - 1)),
                          tE, 1)
                    tEh[(j, kt)] = tE.cnt
            sE_qt_last = sE.cnt
            tE_qt_last = tE.cnt
            E.wait("v", tE)
            for j in range(2):
                E.opi("v", lambda e, j=j: e.tensor_copy(
                    lrow[0:1, 512 * j:512 * (j + 1)], psL[j][:]), vE, 1)
            E.dr("v")
            E.opi("v", lambda e: e.reciprocal(lrow[:], lrow[:]), vE, 1)
            E.dr("v")
            E.wait("t", vE)
            for j in range(2):
                E.opi("t", lambda e, j=j: e.matmul(
                    psA[j][:], g["ones_f"][0:1, :],
                    lrow[0:1, 512 * j:512 * (j + 1)],
                    start=True, stop=True), tE, 1)
            E.wait("v", tE)
            for j in range(2):
                E.opi("v", lambda e, j=j: e.tensor_copy(rlb[j][:], psA[j][:]),
                      vE, 1)
            rlb_hist[qt] = vE.cnt
            E.dr("v")
            if qt >= 2:
                E.wait("v", qO)
            for j in range(2):
                E.opi("v", lambda e, j=j: e.tensor_tensor(
                    obuf[j][:], psO[j][:], rlb[j][:], ALU.mult), vE, 1)
            obuf_hist[qt] = vE.cnt
            E.wait("g", vE)
            for j in range(2):
                E.opi("g", lambda e, j=j, qt=qt: e.dma_start(
                    out=dr["o1_sh"].ap()[j][:, 512 * qt:512 * (qt + 1)],
                    in_=obuf[j][:]), qO, 16)
        _ag(b, "o1_sh", "o1")
        E.join()
        E.flush_block("sa_flash")

    with ExitStack() as px:
        xts = [px.enter_context(st_(f"sa_x{i}", [128, D], F32)) for i in range(8)]
        oTa = px.enter_context(st_("sa_oTa", [128, 16 * 1024], BF16))
        wsl = [px.enter_context(st_(f"sa_wo{i}", [128, 16 * 512], BF16))
               for i in range(2)]
        psP = [px.enter_context(_psum(nc, f"sa_psQ{i}", [128, 512], F32))
               for i in range(2)]
        E.wait("g", cc)
        import os as _os
        if _os.environ.get("MK_DBG5"):
            E.join()
            E.opi("g", lambda e: e.dma_start(out=io["dbg0"],
                                             in_=dr["o1"].ap()[0, 0]),
                  E.sem("qO"), 16)
            E.opi("g", lambda e: e.dma_start(out=io["dbg2"],
                                             in_=dr["o1"].ap()[0, 1][:, 0:D]),
                  E.sem("qO"), 16)
            E.join()

        E.wait("s", cc)
        def load_oT(e):
            pid = nc.partition_id(engines=[e.engine])
            qoff = e.snap(pid * 1024, min_val=0, max_val=7 * 1024)
            i4 = (dr["o1"].ap().rearrange("r j p q -> (r j) p q")
                  [:, :, ds(qoff, 1024)].transpose([1, 0, 2]))
            e.dma_start(out=oTa[:].rearrange("p (h t) -> p h t", t=1024),
                        in_=i4).then_inc(E.sems["qD1"].h, 16)
            return None
        E.sems["qD1"].cnt += 16
        E.op("s", load_oT)
        for m in range(8):
            E.opi("y", lambda e, m=m: e.dma_start(
                out=xts[m][:], in_=xsrc[128 * m:128 * (m + 1), :]), qA0, 16)
        E.wait("t", E.sems["qD1"])
        E.wait("t", qA0)
        tEh = {}
        for n in range(4):
            q = qW0 if n % 2 == 0 else qW1
            if n >= 2:
                E.wait("y", tE, tEh[n - 2])
            emit_qslab(wsl[n % 2], 3, n, q)
            E.wait("t", q)
            for m in range(8):
                pb = psP[m % 2]
                if (n, m) >= (0, 2):
                    E.wait("t", vE)
                for k in range(16):
                    E.opi("t", lambda e, m=m, k=k, n=n, pb=pb: e.matmul(
                        pb[:], oTa[:, 1024 * k + 128 * m:1024 * k + 128 * (m + 1)],
                        wsl[n % 2][:, 512 * k:512 * (k + 1)],
                        start=(k == 0), stop=(k == 15)), tE,
                        1 if k == 15 else 0)
                E.wait("v", tE)
                E.opi("v", lambda e, m=m, n=n, pb=pb: e.tensor_tensor(
                    xts[m][:, 512 * n:512 * (n + 1)],
                    xts[m][:, 512 * n:512 * (n + 1)], pb[:], ALU.add), vE, 1)
            tEh[n] = tE.cnt
        E.wait("g", vE)
        for m in range(8):
            E.opi("g", lambda e, m=m: e.dma_start(
                out=xdst[128 * m:128 * (m + 1), :], in_=xts[m][:]), qO, 16)
        E.join()
        E.flush_block("sa_out")


def kernel(**inputs):
    maps = _shard_inputs(inputs)
    import os
    upto = int(os.environ.get("MK_UPTO", "6"))
    nc = build(upto=upto)
    res = run_bass_kernel_spmd(nc, maps, core_ids=list(range(NCORES)))
    out = np.concatenate([np.asarray(r["y"]) for r in res.results], axis=0)
    return out.reshape(1, 8192, D).astype(np.float32)


# version 08
# speedup vs baseline: 1.1101x; 1.1101x over previous
import numpy as np
import ml_dtypes
from contextlib import ExitStack
import concourse.bass as bass
import concourse.mybir as mybir
from concourse.bass_utils import run_bass_kernel_spmd

F32 = mybir.dt.float32
BF16 = mybir.dt.bfloat16
I32 = mybir.dt.int32
AF = mybir.ActivationFunctionType
ALU = mybir.AluOpType
AX = mybir.AxisListType
BF16NP = ml_dtypes.bfloat16

NCORES = 8
TC = 1024
D = 2048
H0 = 8192
H1 = 5632
NH = 16
HD = 128
MAGIC = 12582912.0
BIGNEG = -30000.0
SOFF = 32.0
LNEPS = 1e-5
QSCL = 0.08838834764831845

ENGS = ("t", "v", "s", "g", "y")
QUEUE_ENG = {"qS": "y", "qW0": "y", "qW1": "y", "qA0": "y", "qA1": "y",
             "qD0": "y", "qD1": "s",
             "qG": "g", "qX0": "g", "qX1": "g", "qO": "g"}


class Sem:
    __slots__ = ("h", "cnt", "name", "joined")

    def __init__(self, h, name):
        self.h, self.cnt, self.name, self.joined = h, 0, name, 0


class Emit:
    def __init__(self, nc, ctx):
        self.nc, self.ctx = nc, ctx
        self.bufs = {e: [] for e in ENGS}
        self.sems = {}

    def sem(self, name):
        if name not in self.sems:
            self.sems[name] = Sem(self.ctx.enter_context(self.nc.semaphore(name)),
                                  name)
        return self.sems[name]

    def op(self, eng, fn):
        self.bufs[eng].append(fn)

    def wait(self, eng, sem, cnt=None):
        n = sem.cnt if cnt is None else cnt
        if n > 0:
            self.bufs[eng].append(lambda e, h=sem.h, n=n: e.wait_ge(h, n))

    def dr(self, eng):
        self.bufs[eng].append(lambda e: e.drain())

    def opi(self, eng, fn, sem, k=1):
        if k == 0:
            self.bufs[eng].append(fn)
            return
        sem.cnt += k
        self.bufs[eng].append(lambda e, f=fn, h=sem.h, k=k: f(e).then_inc(h, k))

    def join(self):
        for qn, eng in QUEUE_ENG.items():
            if qn in self.sems:
                s = self.sems[qn]
                if s.cnt > s.joined:
                    self.wait(eng, s)
                    s.joined = s.cnt
        ph = self.sem("ph")
        ph.cnt += 5
        tgt = ph.cnt
        for e in ENGS:
            if e in ("t", "v", "s"):
                self.bufs[e].append(lambda eng: eng.drain())
            self.bufs[e].append(lambda eng, h=ph.h: eng.sem_inc(h, 1))
            self.bufs[e].append(lambda eng, h=ph.h, n=tgt: eng.wait_ge(h, n))

    _blk_n = 0

    def flush_block(self, name=None):
        Emit._blk_n += 1
        name = f"{name or 'blk'}_{Emit._blk_n}"
        bufs = self.bufs
        self.bufs = {e: [] for e in ENGS}
        with self.nc.Block(name) as blk:
            for key, reg in (("t", blk.tensor), ("v", blk.vector),
                             ("s", blk.scalar), ("g", blk.gpsimd),
                             ("y", blk.sync)):
                def body(e, ops=bufs[key]):
                    for f in ops:
                        f(e)
                reg(body)



def _consts_per_core(c):
    i_ = np.arange(64)
    Dt = (i_[None, :] - i_[:, None]).astype(np.float32)
    Dt[i_[None, :] < i_[:, None]] = 1e9
    Pc = np.full((16, 8), 1e9, np.float32)
    for r in range(c):
        Pc[:, r] = 1024.0 * (c - 1 - r)
    pos = (np.arange(TC, dtype=np.float64) + TC * c)
    half = HD // 2
    inv_freq = 1.0 / (10000.0 ** (np.arange(half, dtype=np.float64) / half))
    freqs = np.outer(pos, inv_freq)
    masks = np.zeros((4, 128, 512), np.float32)
    kk = np.arange(128)[:, None]
    qq = np.arange(512)[None, :]
    for j in range(4):
        masks[j] = np.where(128 * j + kk <= qq, 1.0, 0.0)
    masks = masks.astype(BF16NP)
    return dict(
        ident=np.eye(128, dtype=BF16NP),
        ident_f=np.eye(128, dtype=np.float32),
        ones_f=np.ones((1, 128), np.float32),
        ones_rbf=np.ones((1, 128), BF16NP),
        ones_bf=np.ones((128, 1), BF16NP),
        iota_loc=np.arange(TC, dtype=np.float32).reshape(1, TC),
        iota_m64=(np.arange(TC) % 64).astype(np.float32).reshape(1, TC),
        iota64=np.arange(64, dtype=np.float32).reshape(1, 64),
        k64m=(64.0 - (np.arange(128) % 64)).astype(np.float32).reshape(128, 1),
        Dt=Dt, Pc=Pc,
        cos=np.cos(freqs).astype(np.float32),
        sin=np.sin(freqs).astype(np.float32),
        fmask=masks,
    )


def _shard_inputs(inp):
    x = np.ascontiguousarray(np.asarray(inp["L0_attn_x"], np.float32).reshape(8192, D))
    w1 = np.asarray(inp["L0_ffn_w1"], np.float32)
    w2 = np.asarray(inp["L0_ffn_w2"], np.float32)
    w1T = np.ascontiguousarray(w1.T)
    w2T = np.ascontiguousarray(w2.T)
    qkvoT = np.stack([np.ascontiguousarray(np.asarray(inp[k], np.float32).T)
                      for k in ("L1_attn_w_q", "L1_attn_w_k",
                                "L1_attn_w_v", "L1_attn_w_o")])
    gT = np.ascontiguousarray(np.asarray(inp["L1_ffn_wq_gate"], np.int32).T)
    uT = np.ascontiguousarray(np.asarray(inp["L1_ffn_wq_up"], np.int32).T)
    dT_raw = np.ascontiguousarray(np.asarray(inp["L1_ffn_wq_down"], np.int32).T)
    dT = np.zeros((48 * 128, D), np.int32)
    dT[:H1] = dT_raw
    base = dict(
        wqkv=np.ascontiguousarray(np.asarray(inp["L0_attn_Wqkv"], np.float32)),
        bqkv=np.asarray(inp["L0_attn_bqkv"], np.float32).reshape(1, 3 * D),
        wout=np.ascontiguousarray(np.asarray(inp["L0_attn_Wout"], np.float32)),
        bout=np.asarray(inp["L0_attn_bout"], np.float32).reshape(1, D),
        slopes=np.asarray(inp["L0_attn_slopes"], np.float32).reshape(1, 16),
        sg=np.asarray(inp["L1_ffn_s_gate"], np.float32).reshape(1, H1),
        su=np.asarray(inp["L1_ffn_s_up"], np.float32).reshape(1, H1),
        sd=np.asarray(inp["L1_ffn_s_down"], np.float32).reshape(1, D),
    )
    maps = []
    for c in range(NCORES):
        m = dict(base)
        m.update(
            x0=np.ascontiguousarray(x[TC * c:TC * (c + 1)]),
            w1_rows=np.ascontiguousarray(w1[1024 * c:1024 * (c + 1)]),
            w1T_sh=np.ascontiguousarray(w1T[256 * c:256 * (c + 1)]),
            w2_rows=np.ascontiguousarray(w2[256 * c:256 * (c + 1)]),
            w2T_sh=np.ascontiguousarray(w2T[1024 * c:1024 * (c + 1)]),
            qkvoT_sh=np.ascontiguousarray(qkvoT[:, 256 * c:256 * (c + 1), :]),
            gT_sh=np.ascontiguousarray(gT[256 * c:256 * (c + 1)]),
            uT_sh=np.ascontiguousarray(uT[256 * c:256 * (c + 1)]),
            dT_sh=np.ascontiguousarray(dT[768 * c:768 * (c + 1)]),
        )
        m.update(_consts_per_core(c))
        maps.append(m)
    return maps



class B:
    pass


def _psum(nc, *a, **k):
    return nc.psum_tensor(*a, allow_name_mangling=True, **k)


def build(upto=6):
    nc = bass.Bass("TRN2", target_bir_lowering=False, debug=False,
                   detect_race_conditions=False)
    b = B()
    b.nc = nc
    dt = nc.dram_tensor

    def st_(*a, **k):
        return nc.sbuf_tensor(*a, allow_name_mangling=True, **k)
    b.st = st_
    io = {}
    io["x0"] = dt("x0", [TC, D], F32, kind="ExternalInput").ap()
    io["y"] = dt("y", [TC, D], F32, kind="ExternalOutput").ap()
    io["dbg0"] = dt("dbg0", [128, H0], F32, kind="ExternalOutput").ap()
    io["dbg1"] = dt("dbg1", [128, 64], F32, kind="ExternalOutput").ap()
    io["dbg2"] = dt("dbg2", [128, D], F32, kind="ExternalOutput").ap()
    for name, shape, dty in [
        ("w1_rows", [1024, D], F32), ("w1T_sh", [256, H0], F32),
        ("w2_rows", [256, H0], F32), ("w2T_sh", [1024, D], F32),
        ("wqkv", [D, 3 * D], F32), ("bqkv", [1, 3 * D], F32),
        ("wout", [D, D], F32), ("bout", [1, D], F32), ("slopes", [1, 16], F32),
        ("qkvoT_sh", [4, 256, D], F32),
        ("gT_sh", [256, H1], I32), ("uT_sh", [256, H1], I32),
        ("dT_sh", [768, D], I32),
        ("sg", [1, H1], F32), ("su", [1, H1], F32), ("sd", [1, D], F32),
        ("ident", [128, 128], BF16), ("ident_f", [128, 128], F32),
        ("ones_f", [1, 128], F32),
        ("ones_rbf", [1, 128], BF16), ("ones_bf", [128, 1], BF16),
        ("iota_loc", [1, TC], F32), ("iota_m64", [1, TC], F32),
        ("iota64", [1, 64], F32), ("k64m", [128, 1], F32),
        ("Dt", [64, 64], F32), ("Pc", [16, 8], F32),
        ("cos", [TC, 64], F32), ("sin", [TC, 64], F32),
        ("fmask", [4, 128, 512], BF16),
    ]:
        io[name] = dt(name, shape, dty, kind="ExternalInput").ap()
    b.io = io

    dr = {}
    dr["xa"] = dt("xa", [TC, D], F32)
    dr["xb"] = dt("xb", [TC, D], F32)
    dr["st_sh"] = dt("st_sh", [1, 1280], F32)
    dr["st_all"] = dt("st_all", [8, 1280], F32)
    dr["w1b_sh"] = dt("w1b_sh", [2, 128, H0], BF16)
    dr["w1b"] = dt("w1b", [16, 128, H0], BF16)
    dr["w2b_sh"] = dt("w2b_sh", [8, 128, D], BF16)
    dr["w2b"] = dt("w2b", [64, 128, D], BF16)
    dr["qkvob_sh"] = dt("qkvob_sh", [4, 2, 128, D], BF16)
    dr["qkvob"] = dt("qkvob", [8, 4, 2, 128, D], BF16)
    dr["gb_sh"] = dt("gb_sh", [2, 128, H1], BF16)
    dr["ub_sh"] = dt("ub_sh", [2, 128, H1], BF16)
    dr["db_sh"] = dt("db_sh", [6, 128, D], BF16)
    dr["gb"] = dt("gb", [16, 128, H1], BF16)
    dr["ub"] = dt("ub", [16, 128, H1], BF16)
    dr["db"] = dt("db", [48, 128, D], BF16)
    dr["qkv0"] = dt("qkv0", [TC, 3 * D], BF16)
    dr["Tc_sh"] = dt("Tc_sh", [16, 128, 128], F32)
    dr["Tc"] = dt("Tc", [8, 16, 128, 128], F32)
    dr["qkv1_sh"] = dt("qkv1_sh", [3, NH, 128, TC], BF16)
    dr["qkv1"] = dt("qkv1", [8, 3, NH, 128, TC], BF16)
    dr["o1_sh"] = dt("o1_sh", [2, 128, 8192], BF16)
    dr["o1"] = dt("o1", [8, 2, 128, 8192], BF16)
    b.dr = dr

    ctx = ExitStack()
    b.ctx = ctx
    E = Emit(nc, ctx)
    b.E = E
    g = {}
    g["ident"] = ctx.enter_context(st_("c_ident", [128, 128], BF16))
    g["ident_f"] = ctx.enter_context(st_("c_identf", [128, 128], F32))
    g["ones_f"] = ctx.enter_context(st_("c_ones_f", [1, 128], F32))
    g["ones_rbf"] = ctx.enter_context(st_("c_ones_rbf", [1, 128], BF16))
    g["ones_bf"] = ctx.enter_context(st_("c_ones_bf", [128, 1], BF16))
    g["stat"] = ctx.enter_context(st_("g_stat", [128, 64], F32))
    b.g = g

    for s in ("qS", "qW0", "qW1", "qA0", "qA1", "qD0", "qD1",
              "qG", "qX0", "qX1", "qO", "tE", "vE", "sE", "cc"):
        E.sem(s)

    qS = E.sem("qS")
    for name, dst in (("ident", g["ident"]), ("ident_f", g["ident_f"]),
                      ("ones_f", g["ones_f"]),
                      ("ones_rbf", g["ones_rbf"]), ("ones_bf", g["ones_bf"])):
        E.opi("y", lambda e, s=dst, i=io[name]: e.dma_start(out=s[:], in_=i), qS, 16)
    E.opi("y", lambda e: e.dma_start(out=dr["xa"].ap(), in_=io["x0"]), qS, 16)
    E.join()
    E.flush_block("p0")

    _prep(b)

    xc = "xa"
    seq = [("f0", "_ffn0"), ("la", "_lattn"), ("f0", "_ffn0"),
           ("f1", "_ffn1"), ("sa", "_sattn"), ("f1", "_ffn1")]
    import os as _os
    only = _os.environ.get("MK_ONLY")
    run_seq = [seq[int(only) - 1]] if only else seq[:upto]
    for i, (tag, fn) in enumerate(run_seq):
        phase_i = i + 1
        last = (phase_i == 6) and not only
        dst = None if last else ("xb" if xc == "xa" else "xa")
        globals()[fn](b, xc, dst, final=last)
        if dst is not None:
            xc = dst
    upto = 0 if only else upto
    if upto < 6:
        E.opi("y", lambda e, x=xc: e.dma_start(out=io["y"], in_=dr[x].ap()), qS, 16)
        E.join()
        E.flush_block("fincopy")
    ctx.close()
    return nc



def _bcast(b, psb, src_ap_fn, dst, w, f32=True):
    E, g = b.E, b.g
    tE, vE = E.sem("tE"), E.sem("vE")
    E.wait("t", vE)
    for j in range((w + 511) // 512):
        ww = min(512, w - 512 * j)
        E.opi("t", lambda e, j=j, ww=ww: e.matmul(
            psb[:, :ww], g["ones_f"][0:1, :], src_ap_fn(j, ww),
            start=True, stop=True), tE, 1)
        E.wait("v", tE)
        E.opi("v", lambda e, j=j, ww=ww: e.tensor_copy(
            dst[:, 512 * j:512 * j + ww], psb[:, :ww]), vE, 1)
        E.wait("t", vE)


def _ag(b, name_in, name_out):
    E, dr = b.E, b.dr
    cc, qG = E.sem("cc"), E.sem("qG")
    E.wait("g", qG)
    E.wait("g", E.sem("qO"))
    E.opi("g", lambda e: e.collective_compute(
        "AllGather", ALU.bypass, replica_groups=[list(range(NCORES))],
        ins=[dr[name_in].ap()], outs=[dr[name_out].ap()]), cc, 1)
    if not hasattr(b, "cc_marks"):
        b.cc_marks = {}
    b.cc_marks[name_out] = cc.cnt


def _ternary_quant_ops(b, wt_ap, rc, sc, tq, out_ap, w):
    E = b.E
    vE = E.sem("vE")
    E.wait("v", E.sem("qG"))
    E.op("v", lambda e: e.tensor_tensor(tq[:, :w], wt_ap, rc[:, :w], ALU.mult))
    E.op("v", lambda e: e.tensor_scalar(tq[:, :w], tq[:, :w], MAGIC, MAGIC,
                                        ALU.add, ALU.subtract))
    E.op("v", lambda e: e.tensor_scalar(tq[:, :w], tq[:, :w], 1.0, -1.0,
                                        ALU.min, ALU.max))
    E.opi("v", lambda e: e.tensor_tensor(out_ap, tq[:, :w], sc[:, :w], ALU.mult),
          vE, 1)



def _prep(b):
    import os
    pstop = int(os.environ.get("MK_PSTOP", "9"))
    nc, E, io, dr, g, st_ = b.nc, b.E, b.io, b.dr, b.g, b.st
    if pstop <= 0:
        return
    qS, qG = E.sem("qS"), E.sem("qG")
    qW0, qW1 = E.sem("qW0"), E.sem("qW1")
    qX0, qX1 = E.sem("qX0"), E.sem("qX1")
    tE, vE, sE, cc = E.sem("tE"), E.sem("vE"), E.sem("sE"), E.sem("cc")

    with ExitStack() as px:
        wrow = px.enter_context(st_("p_wrow", [128, H0], F32))
        wrow2 = px.enter_context(st_("p_wrow2", [128, H0], F32))
        stt = px.enter_context(st_("p_stats", [128, 10], F32))
        srow = px.enter_context(st_("p_srow", [1, 1280], F32))
        pT = px.enter_context(_psum(nc, "pp_T", [1, 512], F32))
        vh = []
        for j in range(10):
            buf = wrow if j % 2 == 0 else wrow2
            q = qW0 if j % 2 == 0 else qW1
            if j >= 2:
                E.wait("y", vE, vh[j - 2])
            if j < 8:
                E.opi("y", lambda e, bu=buf, j=j: e.dma_start(
                    out=bu[:, :D], in_=io["w1_rows"][128 * j:128 * (j + 1), :]),
                    q, 16)
                E.wait("v", q)
                E.opi("v", lambda e, bu=buf, j=j: e.tensor_reduce(
                    stt[:, j:j + 1], bu[:, :D], axis=AX.X, op=ALU.add,
                    apply_absolute_value=True), vE, 1)
            else:
                jj = j - 8
                E.opi("y", lambda e, bu=buf, jj=jj: e.dma_start(
                    out=bu[:, :], in_=io["w2_rows"][128 * jj:128 * (jj + 1), :]),
                    q, 16)
                E.wait("v", q)
                E.opi("v", lambda e, bu=buf, j=j: e.tensor_reduce(
                    stt[:, j:j + 1], bu[:, :], axis=AX.X, op=ALU.add,
                    apply_absolute_value=True), vE, 1)
            vh.append(vE.cnt)
        pamode = int(os.environ.get("MK_PAMODE", "4"))
        if pamode >= 2:
            E.wait("t", vE)
            for j0 in range(0, 10, 4):
                nb = min(4, 10 - j0)
                for k in range(nb):
                    E.opi("t", lambda e, j=j0 + k, s=k: e.transpose(
                        pT[0:1, 128 * s:128 * s + 128], stt[:, j:j + 1],
                        g["ident_f"][:]), tE, 1)
                E.wait("v", tE)
                E.opi("v", lambda e, j0=j0, nb=nb: e.tensor_copy(
                    srow[0:1, 128 * j0:128 * (j0 + nb)],
                    pT[0:1, 0:128 * nb]), vE, 1)
                E.wait("t", vE)
        if pamode >= 3:
            E.wait("g", vE)
            E.opi("g", lambda e: e.dma_start(out=dr["st_sh"].ap(), in_=srow[:]),
                  qG, 16)
        if pamode >= 4:
            _ag(b, "st_sh", "st_all")
        E.join()
        E.flush_block("prepA")
    if pstop <= 1:
        return

    with ExitStack() as px:
        wt = px.enter_context(st_("pb_wt", [128, H0], F32))
        wt2 = px.enter_context(st_("pb_wt2", [128, H0], F32))
        tq = px.enter_context(st_("pb_tq", [128, 2048], F32))
        ob = px.enter_context(st_("pb_ob", [128, H0], BF16))
        rc = px.enter_context(st_("pb_rc", [128, 2048], F32))
        sc = px.enter_context(st_("pb_sc", [128, 2048], F32))
        psb = px.enter_context(_psum(nc, "pp_b", [128, 512], F32))
        g["row"] = px.enter_context(st_("pb_row", [1, H0], F32))
        g["row2"] = px.enter_context(st_("pb_row2", [1, H0], F32))
        E.wait("y", cc)
        E.opi("y", lambda e: e.dma_start(
            out=g["row"][0:1, :H0].rearrange("a (r n) -> a r n", r=8),
            in_=dr["st_all"].ap()[:, 0:1024].rearrange("r n -> () r n")), qS, 16)
        E.wait("v", qS)
        E.op("v", lambda e: e.tensor_scalar(
            g["row"][0:1, :H0], g["row"][0:1, :H0], 1.0 / D, 1e-5,
            ALU.mult, ALU.max))
        E.dr("v")
        E.opi("v", lambda e: e.reciprocal(g["row2"][0:1, :H0], g["row"][0:1, :H0]),
              vE, 1)
        E.dr("v")
        for i in range(2):
            buf = wt if i % 2 == 0 else wt2
            q = qW0 if i % 2 == 0 else qW1
            E.opi("y", lambda e, bu=buf, i=i: e.dma_start(
                out=bu[:], in_=io["w1T_sh"][128 * i:128 * (i + 1), :]), q, 16)
            E.wait("v", q)
            for sl in range(4):
                lo = 2048 * sl
                _bcast(b, psb, lambda j, ww, lo=lo: g["row2"][0:1, lo + 512 * j:lo + 512 * j + ww],
                       rc, 2048)
                _bcast(b, psb, lambda j, ww, lo=lo: g["row"][0:1, lo + 512 * j:lo + 512 * j + ww],
                       sc, 2048)
                _ternary_quant_ops(b, buf[:, lo:lo + 2048], rc, sc, tq,
                                   ob[:, lo:lo + 2048], 2048)
            E.wait("g", vE)
            E.opi("g", lambda e, i=i: e.dma_start(
                out=dr["w1b_sh"].ap()[i], in_=ob[:]), qG, 16)
        _ag(b, "w1b_sh", "w1b")
        E.join()
        E.flush_block("prepB")
    if pstop <= 2:
        return

    with ExitStack() as px:
        wt = px.enter_context(st_("pc_wt", [128, D], F32))
        wt2 = px.enter_context(st_("pc_wt2", [128, D], F32))
        tq = px.enter_context(st_("pc_tq", [128, D], F32))
        ob = px.enter_context(st_("pc_ob", [128, D], BF16))
        rc = px.enter_context(st_("pc_rc", [128, D], F32))
        sc = px.enter_context(st_("pc_sc", [128, D], F32))
        psb = px.enter_context(_psum(nc, "pp_c", [128, 512], F32))
        g["row"] = px.enter_context(st_("pc_row", [1, D], F32))
        g["row2"] = px.enter_context(st_("pc_row2", [1, D], F32))
        E.opi("y", lambda e: e.dma_start(
            out=g["row"][0:1, :D].rearrange("a (r n) -> a r n", r=8),
            in_=dr["st_all"].ap()[:, 1024:1280].rearrange("r n -> () r n")), qS, 16)
        E.wait("v", qS)
        E.op("v", lambda e: e.tensor_scalar(
            g["row"][0:1, :D], g["row"][0:1, :D], 1.0 / H0, 1e-5,
            ALU.mult, ALU.max))
        E.dr("v")
        E.opi("v", lambda e: e.reciprocal(g["row2"][0:1, :D], g["row"][0:1, :D]),
              vE, 1)
        E.dr("v")
        _bcast(b, psb, lambda j, ww: g["row2"][0:1, 512 * j:512 * j + ww], rc, D)
        _bcast(b, psb, lambda j, ww: g["row"][0:1, 512 * j:512 * j + ww], sc, D)
        obh = []
        for i in range(8):
            buf = wt if i % 2 == 0 else wt2
            q = qW0 if i % 2 == 0 else qW1
            if i >= 1:
                E.wait("y", vE)
                if i >= 1:
                    E.wait("y", qG)
            E.opi("y", lambda e, bu=buf, i=i: e.dma_start(
                out=bu[:], in_=io["w2T_sh"][128 * i:128 * (i + 1), :]), q, 16)
            E.wait("v", q)
            _ternary_quant_ops(b, buf[:], rc, sc, tq, ob[:], D)
            E.wait("g", vE)
            E.opi("g", lambda e, i=i: e.dma_start(
                out=dr["w2b_sh"].ap()[i], in_=ob[:]), qG, 16)
        _ag(b, "w2b_sh", "w2b")
        E.join()
        E.flush_block("prepC")
    if pstop <= 3:
        return

    with ExitStack() as px:
        wt = px.enter_context(st_("pd_wt", [128, D], F32))
        awt = px.enter_context(st_("pd_awt", [128, D], F32))
        tq = px.enter_context(st_("pd_tq", [128, D], F32))
        ob = px.enter_context(st_("pd_ob", [128, D], BF16))
        rc = px.enter_context(st_("pd_rc", [128, D], F32))
        sc = px.enter_context(st_("pd_sc", [128, D], F32))
        psb = px.enter_context(_psum(nc, "pp_d", [128, 512], F32))
        pss = px.enter_context(_psum(nc, "pp_ds", [1, 2048], F32))
        onesc = px.enter_context(st_("pd_ones", [128, 1], F32))
        g["row"] = px.enter_context(st_("pd_row", [1, D], F32))
        g["row2"] = px.enter_context(st_("pd_row2", [1, D], F32))
        E.op("v", lambda e: e.memset(onesc[:], 1.0))
        for mat in range(4):
            for i in range(2):
                t_i = mat * 2 + i
                q = qW0 if t_i % 2 == 0 else qW1
                if t_i >= 1:
                    E.wait("y", vE)
                    E.wait("y", qG)
                E.opi("y", lambda e, mat=mat, i=i: e.dma_start(
                    out=wt[:], in_=io["qkvoT_sh"][mat, 128 * i:128 * (i + 1), :]),
                    q, 16)
                E.wait("s", q)
                E.opi("s", lambda e: e.activation(awt[:], wt[:], AF.Abs), sE, 1)
                E.wait("t", sE)
                E.wait("t", vE)
                for j in range(4):
                    E.opi("t", lambda e, j=j: e.matmul(
                        pss[0:1, 512 * j:512 * (j + 1)], onesc[:],
                        awt[:, 512 * j:512 * (j + 1)], start=True, stop=True),
                        tE, 1)
                E.wait("v", tE)
                E.op("v", lambda e: e.tensor_scalar(
                    g["row"][0:1, :D], pss[0:1, :], 1.0 / 128.0, 1e-8,
                    ALU.mult, ALU.max))
                E.dr("v")
                E.opi("v", lambda e: e.reciprocal(
                    g["row2"][0:1, :D], g["row"][0:1, :D]), vE, 1)
                E.dr("v")
                _bcast(b, psb, lambda j, ww: g["row2"][0:1, 512 * j:512 * j + ww],
                       rc, D)
                _bcast(b, psb, lambda j, ww: g["row"][0:1, 512 * j:512 * j + ww],
                       sc, D)
                _ternary_quant_ops(b, wt[:], rc, sc, tq, ob[:], D)
                E.wait("g", vE)
                E.opi("g", lambda e, mat=mat, i=i: e.dma_start(
                    out=dr["qkvob_sh"].ap()[mat, i], in_=ob[:]), qG, 16)
        _ag(b, "qkvob_sh", "qkvob")
        E.join()
        E.flush_block("prepD")
    if pstop <= 4:
        return

    with ExitStack() as px:
        wt = px.enter_context(st_("pe_wt", [128, H1], F32))
        tq = px.enter_context(st_("pe_tq", [128, H1], F32))
        ob = px.enter_context(st_("pe_ob", [128, H1], BF16))
        scl = px.enter_context(st_("pe_scl", [128, H1], F32))
        psb = px.enter_context(_psum(nc, "pp_e", [128, 512], F32))
        g["row"] = px.enter_context(st_("pe_row", [1, H1], F32))
        for srow_name, wsh, osh, width, nt in (
                ("sg", "gT_sh", "gb_sh", H1, 2), ("su", "uT_sh", "ub_sh", H1, 2),
                ("sd", "dT_sh", "db_sh", D, 6)):
            E.wait("y", tE)
            E.opi("y", lambda e, n=srow_name, w=width: e.dma_start(
                out=g["row"][0:1, :w], in_=io[n]), qS, 16)
            E.wait("t", qS)
            _bcast(b, psb, lambda j, ww: g["row"][0:1, 512 * j:512 * j + ww],
                   scl, width)
            for i in range(nt):
                q = qX0 if i % 2 == 0 else qX1
                if i >= 1:
                    E.wait("g", vE)
                    E.wait("g", qG)
                E.opi("g", lambda e, wsh=wsh, i=i, w=width: e.dma_start(
                    out=wt[:, :w], in_=io[wsh][128 * i:128 * (i + 1), :]), q, 16)
                E.wait("v", q)
                E.wait("v", qG)
                E.op("v", lambda e, w=width: e.tensor_scalar(
                    tq[:, :w], wt[:, :w], -128.0, 0.0, ALU.add, ALU.add))
                E.opi("v", lambda e, w=width: e.tensor_tensor(
                    ob[:, :w], tq[:, :w], scl[:, :w], ALU.mult), vE, 1)
                E.wait("g", vE)
                E.opi("g", lambda e, osh=osh, i=i: e.dma_start(
                    out=dr[osh].ap()[i], in_=ob[:, :dr[osh].shape[2]]), qG, 16)
        _ag(b, "gb_sh", "gb")
        _ag(b, "ub_sh", "ub")
        _ag(b, "db_sh", "db")
        E.join()
        E.flush_block("prepE")



def _ln_stats(b, xt_ap, sq, mi, quant):
    E, g = b.E, b.g
    vE, sE = E.sem("vE"), E.sem("sE")
    S = g["stat"]
    c0 = 16 * mi

    def col(j):
        return S[:, c0 + j:c0 + j + 1]
    E.wait("s", vE)
    E.opi("s", lambda e: e.activation(sq[:], xt_ap, AF.Square), sE, 1)
    E.op("v", lambda e: e.tensor_reduce(col(5), xt_ap, axis=AX.X, op=ALU.add))
    E.wait("v", sE)
    E.op("v", lambda e: e.tensor_reduce(col(6), sq[:], axis=AX.X, op=ALU.add))
    E.dr("v")
    E.op("v", lambda e: e.tensor_scalar(col(0), col(5), 1.0 / D, None, ALU.mult))
    E.op("v", lambda e: e.tensor_scalar(col(6), col(6), 1.0 / D, None, ALU.mult))
    E.dr("v")
    E.op("v", lambda e: e.tensor_tensor(col(7), col(0), col(0), ALU.mult))
    E.dr("v")
    E.op("v", lambda e: e.tensor_tensor(col(6), col(6), col(7), ALU.subtract))
    E.dr("v")
    E.opi("v", lambda e: e.tensor_scalar(col(6), col(6), 1.0, LNEPS,
                                         ALU.mult, ALU.add), vE, 1)
    E.dr("v")
    E.wait("s", vE)
    E.opi("s", lambda e: e.activation(col(7), col(6), AF.Sqrt), sE, 1)
    E.dr("s")
    E.wait("v", sE)
    E.opi("v", lambda e: e.reciprocal(col(1), col(7)), vE, 1)
    if quant:
        E.op("v", lambda e: e.tensor_reduce(col(5), xt_ap, axis=AX.X, op=ALU.max))
        E.op("v", lambda e: e.tensor_reduce(col(6), xt_ap, axis=AX.X, op=ALU.min))
        E.dr("v")
        E.op("v", lambda e: e.tensor_tensor(col(5), col(5), col(0), ALU.subtract))
        E.op("v", lambda e: e.tensor_tensor(col(6), col(0), col(6), ALU.subtract))
        E.dr("v")
        E.op("v", lambda e: e.tensor_tensor(col(5), col(5), col(6), ALU.max))
        E.dr("v")
        E.op("v", lambda e: e.tensor_tensor(col(5), col(5), col(1), ALU.mult))
        E.dr("v")
        E.op("v", lambda e: e.tensor_scalar(col(5), col(5), 1.0, 1e-5,
                                            ALU.mult, ALU.max))
        E.dr("v")
        E.op("v", lambda e: e.tensor_scalar(col(2), col(5), 1.0 / 127.0, None,
                                            ALU.mult))
        E.op("v", lambda e: e.reciprocal(col(6), col(5)))
        E.dr("v")
        E.op("v", lambda e: e.tensor_scalar(col(3), col(6), 127.0, None, ALU.mult))
        E.dr("v")
        E.opi("v", lambda e: e.tensor_tensor(col(4), col(1), col(3), ALU.mult),
              vE, 1)
        E.dr("v")


def _transpose_128(b, src_bf_ap_fn, nchunk, psT, dst, dst_off):
    E, g = b.E, b.g
    tE, vE = E.sem("tE"), E.sem("vE")
    for b0 in range(0, nchunk, 8):
        nb = min(8, nchunk - b0)
        for k in range(nb):
            E.opi("t", lambda e, k=k, b0=b0: e.transpose(
                psT[:, 128 * k:128 * (k + 1)], src_bf_ap_fn(b0 + k),
                g["ident"][:]), tE, 1)
        E.wait("v", tE)
        E.opi("v", lambda e, b0=b0, nb=nb: e.tensor_copy(
            dst[:, dst_off + 128 * b0:dst_off + 128 * (b0 + nb)],
            psT[:, :128 * nb]), vE, 1)
        E.wait("t", vE)



def _ffn0(b, src, dst, final=False):
    nc, E, io, dr, g, st_ = b.nc, b.E, b.io, b.dr, b.g, b.st
    qW0, qW1 = E.sem("qW0"), E.sem("qW1")
    qX0, qX1 = E.sem("qX0"), E.sem("qX1")
    qA0, qA1 = E.sem("qA0"), E.sem("qA1")
    qO = E.sem("qO")
    tE, vE, sE = E.sem("tE"), E.sem("vE"), E.sem("sE")
    xsrc = dr[src].ap()
    xdst = io["y"] if final else dr[dst].ap()
    S = g["stat"]

    E.wait("y", E.sem("cc"), b.cc_marks["w2b"])
    with ExitStack() as px:
        xt = [px.enter_context(st_(f"f0_xt{i}", [128, D], F32)) for i in range(2)]
        sq = px.enter_context(st_("f0_sq", [128, H0 // 4], F32))
        xqb = px.enter_context(st_("f0_xqb", [128, D], BF16))
        xqT = [px.enter_context(st_(f"f0_xqT{i}", [128, D], BF16)) for i in range(2)]
        hb = [px.enter_context(st_(f"f0_hb{i}", [128, H0], F32)) for i in range(2)]
        hqT = [px.enter_context(st_(f"f0_hqT{i}", [128, H0], BF16)) for i in range(2)]
        w1s = [px.enter_context(st_(f"f0_w1s{i}", [128, 16 * 512], BF16))
               for i in range(2)]
        w2s = [px.enter_context(st_(f"f0_w2s{i}", [128, 16 * 512], BF16))
               for i in range(2)]
        psT = px.enter_context(_psum(nc, "f0_psT", [128, 1024], BF16))
        qO_hist = []
        for p in range(4):
            qA = qA0 if p % 2 == 0 else qA1
            if p >= 1:
                E.wait("y", qO)
            for mi in range(2):
                m = 2 * p + mi
                E.opi("y", lambda e, mi=mi, m=m: e.dma_start(
                    out=xt[mi][:], in_=xsrc[128 * m:128 * (m + 1), :]), qA, 16)
            for mi in range(2):
                E.wait("v", qA)
                E.wait("s", qA)
                _ln_stats(b, xt[mi][:], sq, mi, quant=True)
                c0 = 16 * mi
                E.dr("v")
                E.op("v", lambda e, mi=mi, c0=c0: e.tensor_scalar(
                    sq[:, :D], xt[mi][:], S[:, c0:c0 + 1], S[:, c0 + 4:c0 + 5],
                    ALU.subtract, ALU.mult))
                E.opi("v", lambda e: e.tensor_scalar(
                    xqb[:], sq[:, :D], MAGIC, MAGIC, ALU.add, ALU.subtract),
                    vE, 1)
                E.wait("t", vE)
                _transpose_128(b, lambda k, mi=mi: xqb[:, 128 * k:128 * (k + 1)],
                               16, psT, xqT[mi], 0)
            with ExitStack() as pf:
                psF = [pf.enter_context(_psum(nc, f"f0_psF{i}", [128, 512], F32))
                       for i in range(4)]
                tEh, sEh = {}, {}
                tEh[-2] = tEh[-1] = b.f0_prev_ffn1_last if hasattr(b, "f0_prev_ffn1_last") else 0
                for mi in range(2):
                    E.op("v", lambda e, mi=mi: e.memset(
                        S[:, 16 * mi + 11:16 * mi + 12], 0.0))
                E.wait("s", vE)
                for n in range(16):
                    q = qW0 if n % 2 == 0 else qW1
                    if tEh[n - 2]:
                        E.wait("y", tE, tEh[n - 2])
                    for half in range(2):
                        E.opi("y", lambda e, n=n, half=half: e.dma_start(
                            out=w1s[n % 2][:, 4096 * half:4096 * (half + 1)]
                            .rearrange("p (c h) -> p c h", h=512),
                            in_=dr["w1b"].ap()[8 * half:8 * (half + 1), :,
                                               512 * n:512 * (n + 1)]
                            .transpose([1, 0, 2])), q, 16)
                    E.wait("t", q)
                    if n >= 2:
                        E.wait("t", sE, sEh[n - 2])
                    for mi in range(2):
                        bank = psF[2 * mi + n % 2]
                        for k in range(16):
                            E.opi("t", lambda e, mi=mi, k=k, n=n, bank=bank:
                                  e.matmul(bank[:],
                                           xqT[mi][:, 128 * k:128 * (k + 1)],
                                           w1s[n % 2][:, 512 * k:512 * (k + 1)],
                                           start=(k == 0), stop=(k == 15)),
                                  tE, 1 if k == 15 else 0)
                    tEh[n] = tE.cnt
                    E.wait("s", tE)
                    for mi in range(2):
                        c0 = 16 * mi
                        bank = psF[2 * mi + n % 2]
                        E.opi("s", lambda e, mi=mi, n=n, c0=c0, bank=bank:
                              e.activation(hb[mi][:, 512 * n:512 * (n + 1)],
                                           bank[:], AF.Gelu_apprx_tanh,
                                           scale=S[:, c0 + 2:c0 + 3]), sE, 1)
                    sEh[n] = sE.cnt
                    E.wait("v", sE)
                    for mi in range(2):
                        c0 = 16 * mi
                        E.op("v", lambda e, mi=mi, n=n, c0=c0: e.tensor_reduce(
                            S[:, c0 + 8:c0 + 9], hb[mi][:, 512 * n:512 * (n + 1)],
                            axis=AX.X, op=ALU.max, apply_absolute_value=True))
                        E.dr("v")
                        E.op("v", lambda e, c0=c0: e.tensor_tensor(
                            S[:, c0 + 11:c0 + 12], S[:, c0 + 11:c0 + 12],
                            S[:, c0 + 8:c0 + 9], ALU.max))
            b.f0_prev_ffn1_last = E.sem("tE").cnt
            import os as _os
            if p == 0 and _os.environ.get("MK_DBG"):
                E.join()
                E.opi("g", lambda e: e.dma_start(out=io["dbg0"], in_=hb[0][:]),
                      E.sem("qO"), 16)
                E.opi("g", lambda e: e.dma_start(out=io["dbg1"], in_=S[:]),
                      E.sem("qO"), 16)
                E.opi("g", lambda e: e.dma_start(out=io["dbg2"], in_=xqT[0][:]),
                      E.sem("qO"), 16)
                E.join()
            for mi in range(2):
                c0 = 16 * mi
                E.dr("v")
                E.op("v", lambda e, c0=c0: e.tensor_scalar(
                    S[:, c0 + 8:c0 + 9], S[:, c0 + 11:c0 + 12], 1.0, 1e-5,
                    ALU.mult, ALU.max))
                E.dr("v")
                E.op("v", lambda e, c0=c0: e.reciprocal(
                    S[:, c0 + 7:c0 + 8], S[:, c0 + 8:c0 + 9]))
                E.dr("v")
                E.op("v", lambda e, c0=c0: e.tensor_scalar(
                    S[:, c0 + 9:c0 + 10], S[:, c0 + 7:c0 + 8], 127.0, None,
                    ALU.mult))
                E.opi("v", lambda e, c0=c0: e.tensor_scalar(
                    S[:, c0 + 10:c0 + 11], S[:, c0 + 8:c0 + 9], 0.5 / 127.0,
                    None, ALU.mult), vE, 1)
                E.dr("v")
            for mi in range(2):
                c0 = 16 * mi
                for kq in range(4):
                    E.op("v", lambda e, mi=mi, kq=kq, c0=c0: e.tensor_scalar(
                        sq[:], hb[mi][:, 2048 * kq:2048 * (kq + 1)],
                        S[:, c0 + 9:c0 + 10], MAGIC, ALU.mult, ALU.add))
                    E.opi("v", lambda e: e.tensor_scalar(
                        xqb[:], sq[:], MAGIC, None, ALU.subtract), vE, 1)
                    E.wait("t", vE)
                    _transpose_128(b, lambda k: xqb[:, 128 * k:128 * (k + 1)],
                                   16, psT, hqT[mi], 2048 * kq)
            with ExitStack() as pf:
                psD = [pf.enter_context(_psum(nc, f"f0_psD{i}", [128, 512], F32))
                       for i in range(4)]
                visit = 0
                tEh2 = {}
                tEh2[-2] = tEh2[-1] = getattr(b, "f0_prev_ffn2_last", 0)
                for half in range(2):
                    for kq in range(4):
                        for nd2 in range(2):
                            nd = 2 * half + nd2
                            q = qW0 if visit % 2 == 0 else qW1
                            if tEh2[visit - 2]:
                                E.wait("y", tE, tEh2[visit - 2])
                            for hh in range(2):
                                E.opi("y", lambda e, kq=kq, nd=nd, hh=hh, v2=visit:
                                      e.dma_start(
                                          out=w2s[v2 % 2][:, 4096 * hh:4096 * (hh + 1)]
                                          .rearrange("p (c d) -> p c d", d=512),
                                          in_=dr["w2b"].ap()[16 * kq + 8 * hh:
                                                             16 * kq + 8 * (hh + 1),
                                                             :, 512 * nd:512 * (nd + 1)]
                                          .transpose([1, 0, 2])), q, 16)
                            E.wait("t", q)
                            if kq == 0:
                                E.wait("t", vE)
                            for mi in range(2):
                                bank = psD[2 * mi + nd2]
                                for k in range(16):
                                    E.opi("t", lambda e, mi=mi, k=k, kq=kq, v2=visit,
                                          bank=bank: e.matmul(
                                              bank[:],
                                              hqT[mi][:, 2048 * kq + 128 * k:
                                                      2048 * kq + 128 * (k + 1)],
                                              w2s[v2 % 2][:, 512 * k:512 * (k + 1)],
                                              start=(kq == 0 and k == 0),
                                              stop=(kq == 3 and k == 15)),
                                          tE, 1 if k == 15 else 0)
                            tEh2[visit] = tE.cnt
                            visit += 1
                    E.wait("v", tE)
                    for mi in range(2):
                        c0 = 16 * mi
                        for nd2 in range(2):
                            nd = 2 * half + nd2
                            bank = psD[2 * mi + nd2]
                            E.opi("v", lambda e, mi=mi, nd=nd, c0=c0, bank=bank:
                                  e.scalar_tensor_tensor(
                                      xt[mi][:, 512 * nd:512 * (nd + 1)],
                                      bank[:], S[:, c0 + 10:c0 + 11],
                                      xt[mi][:, 512 * nd:512 * (nd + 1)],
                                      ALU.mult, ALU.add), vE, 1)
            b.f0_prev_ffn2_last = E.sem("tE").cnt
            E.wait("g", vE)
            for mi in range(2):
                m = 2 * p + mi
                E.opi("g", lambda e, mi=mi, m=m: e.dma_start(
                    out=xdst[128 * m:128 * (m + 1), :], in_=xt[mi][:]), qO, 16)
            qO_hist.append(qO.cnt)
        E.join()
        E.flush_block("ffn0")



def _ffn1(b, src, dst, final=False):
    nc, E, io, dr, g, st_ = b.nc, b.E, b.io, b.dr, b.g, b.st
    qW0, qW1 = E.sem("qW0"), E.sem("qW1")
    qX0, qX1 = E.sem("qX0"), E.sem("qX1")
    qA0, qA1 = E.sem("qA0"), E.sem("qA1")
    qO = E.sem("qO")
    tE, vE, sE = E.sem("tE"), E.sem("vE"), E.sem("sE")
    xsrc = dr[src].ap()
    xdst = io["y"] if final else dr[dst].ap()
    S = g["stat"]

    E.wait("y", E.sem("cc"), b.cc_marks["db"])
    with ExitStack() as px:
        xt = [px.enter_context(st_(f"f1_xt{i}", [128, D], F32)) for i in range(2)]
        sq = px.enter_context(st_("f1_sq", [128, D], F32))
        xqb = px.enter_context(st_("f1_xqb", [128, D], BF16))
        xqT = [px.enter_context(st_(f"f1_xqT{i}", [128, D], BF16)) for i in range(2)]
        gst = [px.enter_context(st_(f"f1_gst{i}", [128, 512], F32)) for i in range(2)]
        mid = px.enter_context(st_("f1_mid", [128, 512], BF16))
        midT = [px.enter_context(st_(f"f1_midT{i}", [128, 44 * 128], BF16))
                for i in range(2)]
        wgs = [px.enter_context(st_(f"f1_wgs{i}", [128, 16 * 512], BF16))
               for i in range(2)]
        wus = [px.enter_context(st_(f"f1_wus{i}", [128, 16 * 512], BF16))
               for i in range(2)]
        wds = [px.enter_context(st_(f"f1_wds{i}", [128, 11 * 512], BF16))
               for i in range(2)]
        psT = px.enter_context(_psum(nc, "f1_psT", [128, 1024], BF16))
        for p in range(4):
            qA = qA0 if p % 2 == 0 else qA1
            if p >= 1:
                E.wait("y", qO)
            for mi in range(2):
                m = 2 * p + mi
                E.opi("y", lambda e, mi=mi, m=m: e.dma_start(
                    out=xt[mi][:], in_=xsrc[128 * m:128 * (m + 1), :]), qA, 16)
            for mi in range(2):
                E.wait("v", qA)
                E.wait("s", qA)
                _ln_stats(b, xt[mi][:], sq, mi, quant=False)
                c0 = 16 * mi
                E.dr("v")
                E.opi("v", lambda e, mi=mi, c0=c0: e.tensor_scalar(
                    xqb[:], xt[mi][:], S[:, c0:c0 + 1], S[:, c0 + 1:c0 + 2],
                    ALU.subtract, ALU.mult), vE, 1)
                E.wait("t", vE)
                _transpose_128(b, lambda k: xqb[:, 128 * k:128 * (k + 1)],
                               16, psT, xqT[mi], 0)
            with ExitStack() as pf:
                psG = [pf.enter_context(_psum(nc, f"f1_psG{i}", [128, 512], F32))
                       for i in range(2)]
                psU = [pf.enter_context(_psum(nc, f"f1_psU{i}", [128, 512], F32))
                       for i in range(2)]
                tEh, vEh, sEh = {}, {}, {}
                tEh[-2] = tEh[-1] = getattr(b, "f1_prev_gu_last", 0)
                for n in range(11):
                    q = qW0 if n % 2 == 0 else qW1
                    qg = qX0 if n % 2 == 0 else qX1
                    if tEh[n - 2]:
                        E.wait("y", tE, tEh[n - 2])
                    for hh in range(2):
                        E.opi("y", lambda e, n=n, hh=hh: e.dma_start(
                            out=wgs[n % 2][:, 4096 * hh:4096 * (hh + 1)]
                            .rearrange("p (c h) -> p c h", h=512),
                            in_=dr["gb"].ap()[8 * hh:8 * (hh + 1), :,
                                              512 * n:512 * (n + 1)]
                            .transpose([1, 0, 2])), q, 16)
                        E.opi("y", lambda e, n=n, hh=hh: e.dma_start(
                            out=wus[n % 2][:, 4096 * hh:4096 * (hh + 1)]
                            .rearrange("p (c h) -> p c h", h=512),
                            in_=dr["ub"].ap()[8 * hh:8 * (hh + 1), :,
                                              512 * n:512 * (n + 1)]
                            .transpose([1, 0, 2])), q, 16)
                    E.wait("t", q)
                    if n >= 1:
                        E.wait("t", sE, sEh[n - 1])
                        E.wait("t", vE, vEh[n - 1])
                    for mi in range(2):
                        for k in range(16):
                            E.opi("t", lambda e, mi=mi, k=k, n=n: e.matmul(
                                psG[mi][:], xqT[mi][:, 128 * k:128 * (k + 1)],
                                wgs[n % 2][:, 512 * k:512 * (k + 1)],
                                start=(k == 0), stop=(k == 15)),
                                tE, 1 if k == 15 else 0)
                        for k in range(16):
                            E.opi("t", lambda e, mi=mi, k=k, n=n: e.matmul(
                                psU[mi][:], xqT[mi][:, 128 * k:128 * (k + 1)],
                                wus[n % 2][:, 512 * k:512 * (k + 1)],
                                start=(k == 0), stop=(k == 15)),
                                tE, 1 if k == 15 else 0)
                    tEh[n] = tE.cnt
                    E.wait("s", tE)
                    for mi in range(2):
                        E.opi("s", lambda e, mi=mi: e.activation(
                            gst[mi][:], psG[mi][:], AF.Silu), sE, 1)
                    sEh[n] = sE.cnt
                    E.wait("v", sE)
                    for mi in range(2):
                        E.opi("v", lambda e, mi=mi: e.tensor_tensor(
                            mid[:], gst[mi][:], psU[mi][:], ALU.mult), vE, 1)
                        E.wait("t", vE)
                        nch = 4 if n < 10 else 4
                        _transpose_128(b, lambda k: mid[:, 128 * k:128 * (k + 1)],
                                       4, psT, midT[mi], 512 * n)
                    vEh[n] = vE.cnt
            b.f1_prev_gu_last = E.sem("tE").cnt
            with ExitStack() as pf:
                psD = [pf.enter_context(_psum(nc, f"f1_psD{i}", [128, 512], F32))
                       for i in range(4)]
                visit = 0
                tEh2 = {}
                tEh2[-2] = tEh2[-1] = getattr(b, "f1_prev_dn_last", 0)
                for half in range(2):
                    for kdq in range(4):
                        nk = 11
                        for nd2 in range(2):
                            nd = 2 * half + nd2
                            q = qW0 if visit % 2 == 0 else qW1
                            if tEh2[visit - 2]:
                                E.wait("y", tE, tEh2[visit - 2])
                            E.opi("y", lambda e, kdq=kdq, nd=nd, v2=visit:
                                  e.dma_start(
                                      out=wds[v2 % 2][:, :11 * 512]
                                      .rearrange("p (c d) -> p c d", d=512),
                                      in_=dr["db"].ap()[11 * kdq:11 * (kdq + 1),
                                                        :, 512 * nd:512 * (nd + 1)]
                                      .transpose([1, 0, 2])), q, 16)
                            E.wait("t", q)
                            if kdq == 0:
                                E.wait("t", vE)
                            for mi in range(2):
                                bank = psD[2 * mi + nd2]
                                for k in range(11):
                                    kk = 11 * kdq + k
                                    E.opi("t", lambda e, mi=mi, k=k, kk=kk, v2=visit,
                                          bank=bank: e.matmul(
                                              bank[:],
                                              midT[mi][:, 128 * kk:128 * (kk + 1)],
                                              wds[v2 % 2][:, 512 * k:512 * (k + 1)],
                                              start=(kk == 0), stop=(kk == 43)),
                                          tE, 1 if k == 10 else 0)
                            tEh2[visit] = tE.cnt
                            visit += 1
                    E.wait("v", tE)
                    for mi in range(2):
                        for nd2 in range(2):
                            nd = 2 * half + nd2
                            bank = psD[2 * mi + nd2]
                            E.opi("v", lambda e, mi=mi, nd=nd, bank=bank:
                                  e.scalar_tensor_tensor(
                                      xt[mi][:, 512 * nd:512 * (nd + 1)],
                                      bank[:], 0.5,
                                      xt[mi][:, 512 * nd:512 * (nd + 1)],
                                      ALU.mult, ALU.add), vE, 1)
            b.f1_prev_dn_last = E.sem("tE").cnt
            E.wait("g", vE)
            for mi in range(2):
                m = 2 * p + mi
                E.opi("g", lambda e, mi=mi, m=m: e.dma_start(
                    out=xdst[128 * m:128 * (m + 1), :], in_=xt[mi][:]), qO, 16)
        E.join()
        E.flush_block("ffn1")



def _lattn(b, src, dst, final=False):
    nc, E, io, dr, g, st_ = b.nc, b.E, b.io, b.dr, b.g, b.st
    qW0, qW1 = E.sem("qW0"), E.sem("qW1")
    qX0, qX1 = E.sem("qX0"), E.sem("qX1")
    qA0, qA1 = E.sem("qA0"), E.sem("qA1")
    qS, qO, qG = E.sem("qS"), E.sem("qO"), E.sem("qG")
    tE, vE, sE, cc = E.sem("tE"), E.sem("vE"), E.sem("sE"), E.sem("cc")
    xsrc = dr[src].ap()
    xdst = io["y"] if final else dr[dst].ap()
    S = g["stat"]

    with ExitStack() as px:
        xt1 = px.enter_context(st_("la_xt", [128, D], F32))
        sq = px.enter_context(st_("la_sq", [128, D], F32))
        xqb = px.enter_context(st_("la_xqb", [128, D], BF16))
        xlnT = px.enter_context(st_("la_xlnT", [128, 8 * D], BF16))
        mr = px.enter_context(st_("la_mr", [128, 16], F32))
        psT = px.enter_context(_psum(nc, "la_psT", [128, 1024], BF16))
        for m in range(8):
            qA = qA0 if m % 2 == 0 else qA1
            if m >= 1:
                E.wait("y", vE)
            E.opi("y", lambda e, m=m: e.dma_start(
                out=xt1[:], in_=xsrc[128 * m:128 * (m + 1), :]), qA, 16)
            E.wait("v", qA)
            E.wait("s", qA)
            _ln_stats(b, xt1[:], sq, 0, quant=False)
            E.dr("v")
            E.opi("v", lambda e, m=m: e.tensor_scalar(
                xqb[:], xt1[:], S[:, 0:1], S[:, 1:2],
                ALU.subtract, ALU.mult), vE, 1)
            E.wait("t", vE)
            _transpose_128(b, lambda k: xqb[:, 128 * k:128 * (k + 1)],
                           16, psT, xlnT, D * m)
        with ExitStack() as pf:
            wsl = [pf.enter_context(st_(f"la_w{i}", [128, 16 * 512], BF16))
                   for i in range(2)]
            brow = pf.enter_context(st_("la_brow", [1, 3 * D], BF16))
            stg = [pf.enter_context(st_(f"la_stg{i}", [128, 512], BF16))
                   for i in range(4)]
            psP = [pf.enter_context(_psum(nc, f"la_psP{i}", [128, 512], F32))
                   for i in range(4)]
            E.opi("g", lambda e: e.dma_start(out=brow[:], in_=io["bqkv"]), qG, 16)
            tEh = {}
            vEh = {}
            qO_hist = {}
            for n in range(12):
                q = qX0 if n % 2 == 0 else qX1
                if n >= 2:
                    E.wait("g", tE, tEh[n - 2])
                for hh in range(2):
                    E.opi("g", lambda e, n=n, hh=hh: e.dma_start(
                        out=wsl[n % 2][:, 4096 * hh:4096 * (hh + 1)]
                        .rearrange("p (c o) -> p c o", o=512),
                        in_=io["wqkv"].rearrange("(c p) o -> p c o", p=128)
                        [:, 8 * hh:8 * (hh + 1), 512 * n:512 * (n + 1)]), q, 16)
                E.wait("t", q)
                E.wait("t", qG)
                for mh in range(2):
                    E.wait("t", vE)
                    for m4 in range(4):
                        m = 4 * mh + m4
                        for k in range(16):
                            E.opi("t", lambda e, m=m, k=k, n=n, m4=m4: e.matmul(
                                psP[m4][:],
                                xlnT[:, D * m + 128 * k:D * m + 128 * (k + 1)],
                                wsl[n % 2][:, 512 * k:512 * (k + 1)],
                                start=(k == 0), stop=False), tE, 0)
                        E.opi("t", lambda e, n=n, m4=m4: e.matmul(
                            psP[m4][:], g["ones_rbf"][0:1, :],
                            brow[0:1, 512 * n:512 * (n + 1)],
                            start=False, stop=True), tE, 1)
                    tEh[n] = tE.cnt
                    E.wait("v", tE)
                    for m4 in range(4):
                        m = 4 * mh + m4
                        si = m4
                        if (n, mh) > (0, 0):
                            E.wait("v", qO, qO_hist.get((n, mh, m4), 0))
                        E.opi("v", lambda e, m4=m4, si=si: e.tensor_copy(
                            stg[si][:], psP[m4][:]), vE, 1)
                        E.wait("g", vE)
                        E.opi("g", lambda e, m=m, n=n, si=si: e.dma_start(
                            out=dr["qkv0"].ap()[128 * m:128 * (m + 1),
                                                512 * n:512 * (n + 1)],
                            in_=stg[si][:]), qO, 16)
                        nxt = (n, mh, m4) if mh == 1 else (n, mh + 1, m4)
                        nxt = (n + 1, 0, m4) if mh == 1 else (n, 1, m4)
                        qO_hist[nxt] = qO.cnt
        E.join()
        E.flush_block("la_proj")

    with ExitStack() as pxo:
      oT = pxo.enter_context(st_("la_oT", [128, 16 * 1024], BF16))
      with ExitStack() as px:
          scol = px.enter_context(st_("la_scol", [128, 16], F32))
          nscol = px.enter_context(st_("la_nscol", [128, 16], F32))
          bdcol = px.enter_context(st_("la_bdcol", [128, 16], F32))
          kdcol = px.enter_context(st_("la_kdcol", [128, 16], F32))
          nsdiag = px.enter_context(st_("la_nsdiag", [16, 1], F32))
          k64c = px.enter_context(st_("la_k64c", [128, 1], F32))
          Dts = px.enter_context(st_("la_Dts", [64, 64], F32))
          Pcs = px.enter_context(st_("la_Pcs", [16, 8], F32))
          W16 = px.enter_context(st_("la_W16", [16, 8], F32))
          Wcols = px.enter_context(st_("la_Wcols", [128, 128], F32))
          decT = px.enter_context(st_("la_decT", [64, 16 * 64], BF16))
          rowq = px.enter_context(st_("la_rowq", [1, TC], F32))
          g["row"] = px.enter_context(st_("la_row", [1, TC], F32))
          g["row2"] = px.enter_context(st_("la_row2", [1, TC], F32))
          qd64 = px.enter_context(st_("la_qd64", [128, TC], F32))
          qgdec = px.enter_context(st_("la_qgdec", [128, TC], F32))
          slrow = px.enter_context(st_("la_slrow", [1, 16], F32))
          q_h = px.enter_context(st_("la_qh", [128, 1024], BF16))
          k_h = px.enter_context(st_("la_kh", [128, 1024], BF16))
          k64 = px.enter_context(st_("la_k64", [64, 2048], BF16))
          v64 = px.enter_context(st_("la_v64", [64, 2048], BF16))
          qT = px.enter_context(st_("la_qT", [128, 1024], BF16))
          kT = px.enter_context(st_("la_kT", [128, 1024], BF16))
          qd = px.enter_context(st_("la_qd", [128, 1024], BF16))
          qgdT = px.enter_context(st_("la_qgdT", [128, 16 * 1024], BF16))
          kd_st = px.enter_context(st_("la_kdst", [64, 128], BF16))
          PT = px.enter_context(st_("la_PT", [64, 64], BF16))
          Ssb = px.enter_context(st_("la_Ssb", [128, 128], F32))
          Sbf = px.enter_context(st_("la_Sbf", [128, 128], BF16))
          psT = px.enter_context(_psum(nc, "la_psT2", [128, 1024], BF16))
          psA = [px.enter_context(_psum(nc, f"la_psA{i}", [64, 64], F32))
                 for i in range(2)]
          psS = px.enter_context(_psum(nc, "la_psS", [128, 128], F32))
          psO = [px.enter_context(_psum(nc, f"la_psO{i}", [128, 64], F32))
                 for i in range(2)]
          psB = px.enter_context(_psum(nc, "la_psB", [128, 512], F32))
          E.opi("y", lambda e: e.dma_start(out=slrow[:], in_=io["slopes"]), qS, 16)
          E.opi("y", lambda e: e.dma_start(out=k64c[:], in_=io["k64m"]), qS, 16)
          E.opi("y", lambda e: e.dma_start(out=Dts[:], in_=io["Dt"]), qS, 16)
          E.opi("y", lambda e: e.dma_start(out=Pcs[:], in_=io["Pc"]), qS, 16)
          E.opi("y", lambda e: e.dma_start(out=g["row"][0:1, :TC],
                                           in_=io["iota_m64"]), qS, 16)
          E.opi("y", lambda e: e.dma_start(out=g["row2"][0:1, :TC],
                                           in_=io["iota_loc"]), qS, 16)
          E.wait("t", qS)
          E.opi("t", lambda e: e.matmul(psB[:, 0:16], g["ones_f"][0:1, :],
                                        slrow[0:1, :], start=True, stop=True), tE, 1)
          E.opi("t", lambda e: e.transpose(psS[0:16, 0:1], slrow[0:1, :],
                                           g["ident_f"][0:1, 0:1]), tE, 1)
          E.wait("v", tE)
          E.opi("v", lambda e: e.tensor_copy(scol[:], psB[:, 0:16]), vE, 1)
          E.dr("v")
          E.op("v", lambda e: e.tensor_scalar(nscol[:], scol[:], -1.0, None,
                                              ALU.mult))
          E.dr("v")
          E.opi("v", lambda e: e.tensor_scalar(nsdiag[:], psS[0:16, 0:1], -1.0,
                                               None, ALU.mult), vE, 1)
          E.dr("v")
          E.wait("s", vE)
          E.opi("s", lambda e: e.activation(bdcol[:], scol[:], AF.Exp,
                                            scale=-64.0), sE, 1)
          for h in range(NH):
              E.opi("s", lambda e, h=h: e.activation(
                  kdcol[:, h:h + 1], k64c[:], AF.Exp,
                  scale=nscol[:, h:h + 1]), sE, 1)
              E.opi("s", lambda e, h=h: e.activation(
                  decT[:, 64 * h:64 * (h + 1)], Dts[:], AF.Exp,
                  scale=nscol[0:64, h:h + 1]), sE, 1)
          E.opi("s", lambda e: e.activation(W16[:], Pcs[:], AF.Exp,
                                            scale=nsdiag[:]), sE, 1)
          E.wait("t", sE)
          for r in range(8):
              E.opi("t", lambda e, r=r: e.transpose(
                  psB[0:1, 16 * r:16 * (r + 1)], W16[:, r:r + 1],
                  g["ident_f"][0:16, 0:16]), tE, 1)
          E.wait("v", tE)
          E.opi("v", lambda e: e.tensor_copy(rowq[0:1, 0:128], psB[0:1, 0:128]),
                vE, 1)
          E.wait("t", vE)
          E.opi("t", lambda e: e.matmul(psB[:, 0:128], g["ones_f"][0:1, :],
                                        rowq[0:1, 0:128], start=True, stop=True),
                tE, 1)
          E.wait("v", tE)
          E.opi("v", lambda e: e.tensor_copy(Wcols[:], psB[:, 0:128]), vE, 1)
          E.join()

          for h in range(NH):
              qX = qX0 if h % 2 == 0 else qX1
              if h >= 1:
                  E.wait("g", tE)
                  E.wait("g", vE)
              E.opi("g", lambda e, h=h: e.dma_start(
                  out=q_h[:].rearrange("p (m c) -> p m c", c=128),
                  in_=dr["qkv0"].ap()[:, 128 * h:128 * (h + 1)]
                  .rearrange("(m p) c -> m p c", p=128).transpose([1, 0, 2])),
                  qX, 16)
              E.opi("g", lambda e, h=h: e.dma_start(
                  out=k_h[:].rearrange("p (m c) -> p m c", c=128),
                  in_=dr["qkv0"].ap()[:, D + 128 * h:D + 128 * (h + 1)]
                  .rearrange("(m p) c -> m p c", p=128).transpose([1, 0, 2])),
                  qX, 16)
              E.opi("g", lambda e, h=h: e.dma_start(
                  out=k64[:].rearrange("p (m c) -> p m c", c=128),
                  in_=dr["qkv0"].ap()[:, D + 128 * h:D + 128 * (h + 1)]
                  .rearrange("(m p) c -> m p c", p=64).transpose([1, 0, 2])),
                  qX, 16)
              E.opi("g", lambda e, h=h: e.dma_start(
                  out=v64[:].rearrange("p (m c) -> p m c", c=128),
                  in_=dr["qkv0"].ap()[:, 2 * D + 128 * h:2 * D + 128 * (h + 1)]
                  .rearrange("(m p) c -> m p c", p=64).transpose([1, 0, 2])),
                  qX, 16)
              E.wait("t", qX)
              _transpose_128(b, lambda k: q_h[:, 128 * k:128 * (k + 1)], 8, psT, qT, 0)
              _transpose_128(b, lambda k: k_h[:, 128 * k:128 * (k + 1)], 8, psT, kT, 0)
              E.wait("s", tE)
              E.opi("s", lambda e, h=h: e.activation(
                  rowq[:], g["row"][0:1, :TC], AF.Exp,
                  scale=nscol[0:1, h:h + 1]), sE, 1)
              E.wait("t", sE)
              _bcast(b, psB, lambda j, ww: rowq[0:1, 512 * j:512 * j + ww], qd64, TC)
              E.wait("s", tE)
              E.opi("s", lambda e, h=h: e.activation(
                  rowq[:], g["row2"][0:1, :TC], AF.Exp,
                  scale=nscol[0:1, h:h + 1]), sE, 1)
              E.wait("t", sE)
              _bcast(b, psB, lambda j, ww: rowq[0:1, 512 * j:512 * j + ww], qgdec, TC)
              E.wait("v", tE)
              E.dr("v")
              E.opi("v", lambda e: e.tensor_tensor(qd[:], qT[:], qd64[:], ALU.mult),
                    vE, 1)
              E.opi("v", lambda e, h=h: e.tensor_tensor(
                  qgdT[:, 1024 * h:1024 * (h + 1)], qT[:], qgdec[:], ALU.mult),
                  vE, 1)
              E.wait("v", E.sem("qO"))
              E.op("v", lambda e: e.memset(Ssb[:], 0.0))
              E.opi("v", lambda e: e.memset(Sbf[:], 0.0), vE, 1)
              E.wait("t", vE)
              sE_evac = {}
              for c_ in range(16):
                  pa = psA[c_ % 2]
                  po = psO[c_ % 2]
                  if c_ >= 2:
                      E.wait("t", sE, sE_evac[c_ - 2])
                  E.opi("t", lambda e, c_=c_, pa=pa: e.matmul(
                      pa[:], kT[:, 64 * c_:64 * (c_ + 1)],
                      qT[:, 64 * c_:64 * (c_ + 1)], start=True, stop=True), tE, 1)
                  E.wait("v", tE)
                  E.opi("v", lambda e, h=h, pa=pa: e.tensor_tensor(
                      PT[:], pa[:], decT[:, 64 * h:64 * (h + 1)], ALU.mult), vE, 1)
                  E.opi("v", lambda e, c_=c_, h=h: e.tensor_scalar(
                      kd_st[:], k64[:, 128 * c_:128 * (c_ + 1)],
                      kdcol[0:64, h:h + 1], None, ALU.mult), vE, 1)
                  E.wait("t", vE)
                  E.opi("t", lambda e, c_=c_, po=po: e.matmul(
                      po[:], v64[:, 128 * c_:128 * (c_ + 1)], PT[:],
                      start=True, stop=False), tE, 1)
                  E.opi("t", lambda e, c_=c_, po=po: e.matmul(
                      po[:], Sbf[:], qd[:, 64 * c_:64 * (c_ + 1)],
                      start=False, stop=True), tE, 1)
                  E.opi("t", lambda e, c_=c_: e.matmul(
                      psS[:], kd_st[:], v64[:, 128 * c_:128 * (c_ + 1)],
                      start=True, stop=True), tE, 1)
                  E.wait("v", tE)
                  E.opi("v", lambda e, h=h: e.scalar_tensor_tensor(
                      Ssb[:], Ssb[:], bdcol[:, h:h + 1], psS[:],
                      ALU.mult, ALU.add), vE, 1)
                  E.dr("v")
                  E.opi("v", lambda e: e.tensor_copy(Sbf[:], Ssb[:]), vE, 1)
                  E.wait("s", tE)
                  E.opi("s", lambda e, c_=c_, h=h, po=po: e.activation(
                      oT[:, 1024 * h + 64 * c_:1024 * h + 64 * (c_ + 1)], po[:],
                      AF.Copy), sE, 1)
                  sE_evac[c_] = sE.cnt
              E.wait("g", vE)
              E.wait("g", sE)
              E.opi("g", lambda e, h=h: e.dma_start(
                  out=dr["Tc_sh"].ap()[h], in_=Ssb[:]), qO, 16)
          _ag(b, "Tc_sh", "Tc")

          with ExitStack() as pf:
              tc0 = pf.enter_context(st_("la_tc0", [128, 128], F32))
              tc1 = pf.enter_context(st_("la_tc1", [128, 128], F32))
              Sin = pf.enter_context(st_("la_Sin", [128, 128], F32))
              Sinb = pf.enter_context(st_("la_Sinb", [128, 128], BF16))
              E.wait("g", cc)
              for h in range(NH):
                  E.op("v", lambda e: e.memset(Sin[:], 0.0))
                  for r in range(8):
                      buf = tc0 if r % 2 == 0 else tc1
                      q = qX0 if r % 2 == 0 else qX1
                      if r >= 2:
                          E.wait("g", vE)
                      E.opi("g", lambda e, r=r, h=h, bu=buf: e.dma_start(
                          out=bu[:], in_=dr["Tc"].ap()[r, h]), q, 16)
                      E.wait("v", q)
                      E.opi("v", lambda e, r=r, h=h, bu=buf: e.scalar_tensor_tensor(
                          Sin[:], bu[:], Wcols[:, 16 * r + h:16 * r + h + 1],
                          Sin[:], ALU.mult, ALU.add), vE, 1)
                  E.opi("v", lambda e: e.tensor_copy(Sinb[:], Sin[:]), vE, 1)
                  E.wait("t", vE)
                  for j in range(2):
                      E.opi("t", lambda e, h=h, j=j: e.matmul(
                          psB[:], Sinb[:],
                          qgdT[:, 1024 * h + 512 * j:1024 * h + 512 * (j + 1)],
                          start=True, stop=True), tE, 1)
                      E.wait("v", tE)
                      E.opi("v", lambda e, h=h, j=j: e.tensor_tensor(
                          oT[:, 1024 * h + 512 * j:1024 * h + 512 * (j + 1)],
                          oT[:, 1024 * h + 512 * j:1024 * h + 512 * (j + 1)],
                          psB[:], ALU.add), vE, 1)
                      E.wait("t", vE)
          E.join()

      with ExitStack() as pf:
          xts = [pf.enter_context(st_(f"la_x{i}", [128, D], F32))
                 for i in range(8)]
          wsl = [pf.enter_context(st_(f"la_wo{i}", [128, 16 * 512], BF16))
                 for i in range(2)]
          brow = pf.enter_context(st_("la_bo", [1, D], BF16))
          psP = [pf.enter_context(_psum(nc, f"la_psQ{i}", [128, 512], F32))
                 for i in range(2)]
          E.opi("g", lambda e: e.dma_start(out=brow[:], in_=io["bout"]), qG, 16)
          for m in range(8):
              E.opi("y", lambda e, m=m: e.dma_start(
                  out=xts[m][:], in_=xsrc[128 * m:128 * (m + 1), :]), qA0, 16)
          tEh = {}
          for n in range(4):
              q = qX0 if n % 2 == 0 else qX1
              if n >= 2:
                  E.wait("g", tE, tEh[n - 2])
              E.opi("g", lambda e, n=n: e.dma_start(
                  out=wsl[n % 2][:, :].rearrange("p (c o) -> p c o", o=512),
                  in_=io["wout"].rearrange("(c p) o -> p c o", p=128)
                  [:, :, 512 * n:512 * (n + 1)]), q, 16)
              E.wait("t", q)
              E.wait("t", qG)
              E.wait("t", qA0)
              for m in range(8):
                  pb = psP[m % 2]
                  if m >= 2:
                      E.wait("t", vE)
                  for k in range(16):
                      E.opi("t", lambda e, m=m, k=k, n=n, pb=pb: e.matmul(
                          pb[:], oT[:, 1024 * k + 128 * m:1024 * k + 128 * (m + 1)],
                          wsl[n % 2][:, 512 * k:512 * (k + 1)],
                          start=(k == 0), stop=False), tE, 0)
                  E.opi("t", lambda e, n=n, pb=pb: e.matmul(
                      pb[:], g["ones_rbf"][0:1, :],
                      brow[0:1, 512 * n:512 * (n + 1)],
                      start=False, stop=True), tE, 1)
                  E.wait("v", tE)
                  E.opi("v", lambda e, m=m, n=n, pb=pb: e.tensor_tensor(
                      xts[m][:, 512 * n:512 * (n + 1)],
                      xts[m][:, 512 * n:512 * (n + 1)], pb[:], ALU.add),
                      vE, 1)
              tEh[n] = tE.cnt
          E.wait("g", vE)
          for m in range(8):
              E.opi("g", lambda e, m=m: e.dma_start(
                  out=xdst[128 * m:128 * (m + 1), :], in_=xts[m][:]), qO, 16)
      E.join()
      E.flush_block("la_main")



def _sattn(b, src, dst, final=False):
    nc, E, io, dr, g, st_ = b.nc, b.E, b.io, b.dr, b.g, b.st
    qW0, qW1 = E.sem("qW0"), E.sem("qW1")
    qX0, qX1 = E.sem("qX0"), E.sem("qX1")
    qA0, qA1 = E.sem("qA0"), E.sem("qA1")
    qS, qO, qG = E.sem("qS"), E.sem("qO"), E.sem("qG")
    tE, vE, sE, cc = E.sem("tE"), E.sem("vE"), E.sem("sE"), E.sem("cc")
    xsrc = dr[src].ap()
    xdst = io["y"] if final else dr[dst].ap()
    S = g["stat"]

    def emit_qslab(dst, mat, n, q):
        for c in range(2):
            E.opi("y", lambda e, mat=mat, n=n, c=c: e.dma_start(
                out=dst[:].rearrange("p (r c o) -> p r c o", c=2, o=512)
                [:, :, c, :],
                in_=dr["qkvob"].ap()[:, mat, c, :, 512 * n:512 * (n + 1)]
                .transpose([1, 0, 2])), q, 16)

    E.wait("y", E.sem("cc"), b.cc_marks["qkvob"])
    with ExitStack() as px:
        xt1 = px.enter_context(st_("sa_xt", [128, D], F32))
        sq = px.enter_context(st_("sa_sq", [128, D], F32))
        xqb = px.enter_context(st_("sa_xqb", [128, D], BF16))
        xlnT = px.enter_context(st_("sa_xlnT", [128, 8 * D], BF16))
        psT = px.enter_context(_psum(nc, "sa_psT", [128, 1024], BF16))
        for m in range(8):
            qA = qA0 if m % 2 == 0 else qA1
            if m >= 1:
                E.wait("y", vE)
            E.opi("y", lambda e, m=m: e.dma_start(
                out=xt1[:], in_=xsrc[128 * m:128 * (m + 1), :]), qA, 16)
            E.wait("v", qA)
            E.wait("s", qA)
            _ln_stats(b, xt1[:], sq, 0, quant=False)
            E.dr("v")
            E.opi("v", lambda e, m=m: e.tensor_scalar(
                xqb[:], xt1[:], S[:, 0:1], S[:, 1:2],
                ALU.subtract, ALU.mult), vE, 1)
            E.wait("t", vE)
            _transpose_128(b, lambda k: xqb[:, 128 * k:128 * (k + 1)],
                           16, psT, xlnT, D * m)
        coss = px.enter_context(st_("sa_cos", [128, 8 * 64], F32))
        sins = px.enter_context(st_("sa_sin", [128, 8 * 64], F32))
        E.opi("y", lambda e: e.dma_start(
            out=coss[:].rearrange("p (m f) -> p m f", f=64),
            in_=io["cos"].rearrange("(m p) f -> m p f", p=128)
            .transpose([1, 0, 2])), qS, 16)
        E.opi("y", lambda e: e.dma_start(
            out=sins[:].rearrange("p (m f) -> p m f", f=64),
            in_=io["sin"].rearrange("(m p) f -> m p f", p=128)
            .transpose([1, 0, 2])), qS, 16)

        with ExitStack() as pf:
            wsl = [pf.enter_context(st_(f"sa_w{i}", [128, 16 * 512], BF16))
                   for i in range(2)]
            stg = [pf.enter_context(st_(f"sa_stg{i}", [128, 512], BF16))
                   for i in range(4)]
            qk_sb = pf.enter_context(st_("sa_qk", [128, 8 * D], BF16))
            rt = [pf.enter_context(st_(f"sa_rt{i}", [128, 1024], F32))
                  for i in range(4)]
            kstg = [pf.enter_context(st_(f"sa_kstg{i}", [128, 16 * 128], BF16))
                    for i in range(2)]
            psP = [pf.enter_context(_psum(nc, f"sa_psP{i}", [128, 512], F32))
                   for i in range(4)]

            vq_hist = {}
            vreg = (dr["qkv1_sh"].ap()[2].rearrange("h a t -> h (a t)")
                    .rearrange("h (t c) -> h t c", c=128))
            for n in range(4):
                q = qW0 if n % 2 == 0 else qW1
                if n >= 2:
                    E.wait("y", tE)
                emit_qslab(wsl[n % 2], 2, n, q)
                E.wait("t", q)
                for mh in range(2):
                    E.wait("t", vE)
                    for m4 in range(4):
                        m = 4 * mh + m4
                        for k in range(16):
                            E.opi("t", lambda e, m=m, k=k, n=n, m4=m4: e.matmul(
                                psP[m4][:],
                                xlnT[:, D * m + 128 * k:D * m + 128 * (k + 1)],
                                wsl[n % 2][:, 512 * k:512 * (k + 1)],
                                start=(k == 0), stop=(k == 15)),
                                tE, 1 if k == 15 else 0)
                    E.wait("v", tE)
                    for m4 in range(4):
                        m = 4 * mh + m4
                        vi = 2 * n + mh
                        if vi >= 1:
                            E.wait("v", qO, vq_hist.get(vi - 1, 0))
                        E.opi("v", lambda e, m4=m4: e.tensor_copy(
                            stg[m4][:], psP[m4][:]), vE, 1)
                        E.wait("g", vE)
                        E.opi("g", lambda e, m=m, n=n, m4=m4: e.dma_start(
                            out=vreg[4 * n:4 * (n + 1), 128 * m:128 * (m + 1), :]
                            .transpose([1, 0, 2]),
                            in_=stg[m4][:].rearrange("p (h c) -> p h c", c=128)),
                            qO, 16)
                        vq_hist[vi] = qO.cnt
            kq_hist = {}
            for mat in (1, 0):
                for n in range(4):
                    q = qW0 if n % 2 == 0 else qW1
                    E.wait("y", tE)
                    emit_qslab(wsl[n % 2], mat, n, q)
                    E.wait("t", q)
                    for mh in range(2):
                        E.wait("t", vE)
                        for m4 in range(4):
                            m = 4 * mh + m4
                            for k in range(16):
                                E.opi("t", lambda e, m=m, k=k, n=n, m4=m4:
                                      e.matmul(
                                          psP[m4][:],
                                          xlnT[:, D * m + 128 * k:
                                               D * m + 128 * (k + 1)],
                                          wsl[n % 2][:, 512 * k:512 * (k + 1)],
                                          start=(k == 0), stop=(k == 15)),
                                      tE, 1 if k == 15 else 0)
                        E.wait("v", tE)
                        for m4 in range(4):
                            m = 4 * mh + m4
                            if mat == 0:
                                E.opi("v", lambda e, m=m, n=n, m4=m4:
                                      e.tensor_scalar(
                                          qk_sb[:, D * m + 512 * n:
                                                D * m + 512 * (n + 1)],
                                          psP[m4][:], QSCL, None, ALU.mult),
                                      vE, 1)
                            else:
                                E.opi("v", lambda e, m=m, n=n, m4=m4:
                                      e.tensor_copy(
                                          qk_sb[:, D * m + 512 * n:
                                                D * m + 512 * (n + 1)],
                                          psP[m4][:]), vE, 1)
                dst_mat = dr["qkv1_sh"].ap()[mat]
                for m in range(8):
                    base = D * m
                    k1 = qk_sb[:, base:base + D].rearrange(
                        "p (h c) -> p h c", c=128)[:, :, 0:64]
                    k2 = qk_sb[:, base:base + D].rearrange(
                        "p (h c) -> p h c", c=128)[:, :, 64:128]
                    cosb = (coss[:, 64 * m:64 * (m + 1)]
                            .rearrange("p (a f) -> p a f", a=1)
                            .broadcast_to([128, 16, 64]))
                    sinb = (sins[:, 64 * m:64 * (m + 1)]
                            .rearrange("p (a f) -> p a f", a=1)
                            .broadcast_to([128, 16, 64]))
                    rt3 = [r[:].rearrange("p (h f) -> p h f", f=64) for r in rt]
                    E.op("v", lambda e, k1=k1, cosb=cosb, r=rt3[0]:
                         e.tensor_tensor(r, k1, cosb, ALU.mult))
                    E.op("v", lambda e, k1=k1, sinb=sinb, r=rt3[1]:
                         e.tensor_tensor(r, k1, sinb, ALU.mult))
                    E.op("v", lambda e, k2=k2, sinb=sinb, r=rt3[2]:
                         e.tensor_tensor(r, k2, sinb, ALU.mult))
                    E.op("v", lambda e, k2=k2, cosb=cosb, r=rt3[3]:
                         e.tensor_tensor(r, k2, cosb, ALU.mult))
                    E.op("v", lambda e, k1=k1, a=rt3[0], c=rt3[2]:
                         e.tensor_tensor(k1, a, c, ALU.subtract))
                    E.opi("v", lambda e, k2=k2, a=rt3[1], c=rt3[3]:
                          e.tensor_tensor(k2, a, c, ALU.add), vE, 1)
                    E.wait("t", vE)
                    if m >= 1:
                        E.wait("v", qO, kq_hist.get((mat, m - 1), 0))
                    _transpose_128(b, lambda k, base=base:
                                   qk_sb[:, base + 128 * k:base + 128 * (k + 1)],
                                   16, psT, kstg[m % 2], 0)
                    E.wait("g", vE)
                    E.opi("g", lambda e, m=m, dm=dst_mat: e.dma_start(
                        out=dm[:, :, 128 * m:128 * (m + 1)].transpose([1, 0, 2]),
                        in_=kstg[m % 2][:].rearrange("p (h t) -> p h t", t=128)),
                        qO, 16)
                    kq_hist[(mat, m)] = qO.cnt
        _ag(b, "qkv1_sh", "qkv1")
        E.join()
        E.flush_block("sa_proj")

    from concourse.bass import ds
    with ExitStack() as px:
        kT = px.enter_context(st_("sa_kT", [128, 8192], BF16))
        vsb = px.enter_context(st_("sa_v", [128, 8192], BF16))
        qTf = px.enter_context(st_("sa_qTf", [128, 8192], BF16))
        PT = [px.enter_context(st_(f"sa_PT{i}", [128, 512], BF16))
              for i in range(3)]
        obuf = [px.enter_context(st_(f"sa_ob{i}", [128, 512], BF16))
                for i in range(2)]
        rlb = [px.enter_context(st_(f"sa_rlb{i}", [128, 512], F32))
               for i in range(2)]
        lrow = [px.enter_context(st_(f"sa_lrow{i}", [1, 512], F32))
                for i in range(2)]
        msk = px.enter_context(st_("sa_msk", [128, 4 * 512], BF16))
        psA = [px.enter_context(_psum(nc, f"sa_psA{i}", [128, 512], F32))
               for i in range(3)]
        psO = [px.enter_context(_psum(nc, f"sa_psO{i}", [128, 512], F32))
               for i in range(2)]
        psL = [px.enter_context(_psum(nc, f"sa_psL{i}", [1, 512], F32))
               for i in range(2)]
        psB = [px.enter_context(_psum(nc, "sa_psB", [128, 512], F32))
               for i in range(1)]
        E.wait("y", cc)
        E.opi("y", lambda e: e.dma_start(
            out=msk[:].rearrange("p (j q) -> p j q", q=512),
            in_=io["fmask"].transpose([1, 0, 2])), qS, 16)
        E.opi("v", lambda e: e.memset(S[:, 30:31], -SOFF), vE, 1)
        E.wait("s", vE)
        hpbox = {}

        def snap_heads(e):
            pid = nc.partition_id(engines=[e.engine])
            for j in range(2):
                hpbox[("hp", j)] = e.snap((pid * 2 + j) * 128,
                                          min_val=0, max_val=15 * 128)
                hpbox[("hf", j)] = e.snap((pid * 2 + j) * (128 * TC),
                                          min_val=0, max_val=15 * 128 * TC)
            return None
        E.op("y", snap_heads)
        for j in range(2):
            def load_kv(e, j=j):
                hp = hpbox[("hp", j)]
                hf = hpbox[("hf", j)]
                i1 = (dr["qkv1"].ap()[:, 1].rearrange("r h p t -> r (h p) t")
                      [:, ds(hp, 128), :].transpose([1, 0, 2]))
                e.dma_start(out=kT[:].rearrange("p (r t) -> p r t", t=TC),
                            in_=i1).then_inc(E.sems["qD0"].h, 16)
                i1q = (dr["qkv1"].ap()[:, 0].rearrange("r h p t -> r (h p) t")
                       [:, ds(hp, 128), :].transpose([1, 0, 2]))
                e.dma_start(out=qTf[:].rearrange("p (r t) -> p r t", t=TC),
                            in_=i1q).then_inc(E.sems["qD0"].h, 16)
                for r in range(8):
                    i2 = (dr["qkv1"].ap()[r, 2]
                          .rearrange("h a t -> (h a t)")[ds(hf, 128 * TC)]
                          .rearrange("(kt p c) -> kt p c", p=128, c=128)
                          .transpose([1, 0, 2]))
                    e.dma_start(
                        out=vsb[:, 1024 * r:1024 * (r + 1)]
                        .rearrange("p (kt c) -> p kt c", c=128),
                        in_=i2).then_inc(E.sems["qD0"].h, 16)
                return None
            if j >= 1:
                E.wait("y", tE)
            E.sems["qD0"].cnt += 16 * 10
            E.op("y", load_kv)
            E.wait("t", E.sems["qD0"])
            sE_qt_last = 0
            tE_qt_last = 0
            obuf_hist = {}
            for qt in range(16):
                nkt = 4 * qt + 4
                qb = qTf[:, 512 * qt:512 * (qt + 1)]
                po = psO[qt % 2]
                pl = psL[qt % 2]
                pb2 = psB[0]
                ob2 = obuf[qt % 2]
                rl2 = rlb[qt % 2]
                lr2 = lrow[qt % 2]
                sEh, tEh, aEh = {}, {}, {}

                def emitA(kt, qb=qb):
                    if kt >= 3:
                        E.wait("t", sE, sEh[kt - 3])
                    elif sE_qt_last:
                        E.wait("t", sE, sE_qt_last)
                    if kt == 0 and qt >= 2:
                        E.wait("t", vE, obuf_hist[qt - 2])
                    E.opi("t", lambda e, kt=kt, pa=psA[kt % 3], qb=qb: e.matmul(
                        pa[:], kT[:, 128 * kt:128 * (kt + 1)], qb,
                        start=True, stop=True), tE, 1)
                    aEh[kt] = tE.cnt

                def emitPV(kt, nkt=nkt, po=po, pl=pl):
                    pa = psA[kt % 3]
                    pt = PT[kt % 3]
                    if kt >= 3:
                        E.wait("s", tE, tEh[kt - 3])
                    elif tE_qt_last:
                        E.wait("s", tE, tE_qt_last)
                    E.wait("s", tE, aEh[kt])
                    E.opi("s", lambda e, pa=pa, pt=pt: e.activation(
                        pt[:], pa[:], AF.Exp, bias=S[:, 30:31]), sE, 1)
                    diag = kt - 4 * qt
                    if diag >= 0:
                        E.wait("v", sE)
                        E.opi("v", lambda e, pt=pt, diag=diag: e.tensor_tensor(
                            pt[:], pt[:], msk[:, 512 * diag:512 * (diag + 1)],
                            ALU.mult), vE, 1)
                        E.wait("t", vE)
                    else:
                        E.wait("t", sE)
                    sEh[kt] = sE.cnt
                    E.opi("t", lambda e, kt=kt, pt=pt, nkt=nkt, po=po: e.matmul(
                        po[:], vsb[:, 128 * kt:128 * (kt + 1)], pt[:],
                        start=(kt == 0), stop=(kt == nkt - 1)), tE, 1)
                    E.opi("t", lambda e, kt=kt, pt=pt, nkt=nkt, pl=pl: e.matmul(
                        pl[:], g["ones_bf"][:], pt[:],
                        start=(kt == 0), stop=(kt == nkt - 1)), tE, 1)
                    tEh[kt] = tE.cnt

                emitA(0)
                if nkt > 1:
                    emitA(1)
                for kt in range(nkt):
                    emitPV(kt)
                    if kt + 2 < nkt:
                        emitA(kt + 2)
                sE_qt_last = sE.cnt
                tE_qt_last = tE.cnt
                E.wait("v", tE)
                E.opi("v", lambda e, pl=pl, lr2=lr2: e.tensor_copy(
                    lr2[:], pl[:]), vE, 1)
                E.dr("v")
                E.opi("v", lambda e, lr2=lr2: e.reciprocal(lr2[:], lr2[:]),
                      vE, 1)
                E.dr("v")
                E.wait("t", vE)
                E.opi("t", lambda e, lr2=lr2, pb2=pb2: e.matmul(
                    pb2[:], g["ones_f"][0:1, :], lr2[:],
                    start=True, stop=True), tE, 1)
                E.wait("v", tE)
                E.opi("v", lambda e, pb2=pb2, rl2=rl2: e.tensor_copy(
                    rl2[:], pb2[:]), vE, 1)
                E.dr("v")
                if qt >= 2:
                    E.wait("v", qO)
                E.opi("v", lambda e, po=po, rl2=rl2, ob2=ob2: e.tensor_tensor(
                    ob2[:], po[:], rl2[:], ALU.mult), vE, 1)
                obuf_hist[qt] = vE.cnt
                E.wait("g", vE)
                E.opi("g", lambda e, j=j, qt=qt, ob2=ob2: e.dma_start(
                    out=dr["o1_sh"].ap()[j][:, 512 * qt:512 * (qt + 1)],
                    in_=ob2[:]), qO, 16)
        _ag(b, "o1_sh", "o1")
        E.join()
        E.flush_block("sa_flash")

    with ExitStack() as px:
        xts = [px.enter_context(st_(f"sa_x{i}", [128, D], F32)) for i in range(8)]
        oTa = px.enter_context(st_("sa_oTa", [128, 16 * 1024], BF16))
        wsl = [px.enter_context(st_(f"sa_wo{i}", [128, 16 * 512], BF16))
               for i in range(2)]
        psP = [px.enter_context(_psum(nc, f"sa_psQ{i}", [128, 512], F32))
               for i in range(2)]
        E.wait("g", cc)
        import os as _os
        if _os.environ.get("MK_DBG5"):
            E.join()
            E.opi("g", lambda e: e.dma_start(out=io["dbg0"],
                                             in_=dr["o1"].ap()[0, 0]),
                  E.sem("qO"), 16)
            E.opi("g", lambda e: e.dma_start(out=io["dbg2"],
                                             in_=dr["o1"].ap()[0, 1][:, 0:D]),
                  E.sem("qO"), 16)
            E.join()

        E.wait("s", cc)
        def load_oT(e):
            pid = nc.partition_id(engines=[e.engine])
            qoff = e.snap(pid * 1024, min_val=0, max_val=7 * 1024)
            i4 = (dr["o1"].ap().rearrange("r j p q -> (r j) p q")
                  [:, :, ds(qoff, 1024)].transpose([1, 0, 2]))
            e.dma_start(out=oTa[:].rearrange("p (h t) -> p h t", t=1024),
                        in_=i4).then_inc(E.sems["qD1"].h, 16)
            return None
        E.sems["qD1"].cnt += 16
        E.op("s", load_oT)
        for m in range(8):
            E.opi("y", lambda e, m=m: e.dma_start(
                out=xts[m][:], in_=xsrc[128 * m:128 * (m + 1), :]), qA0, 16)
        E.wait("t", E.sems["qD1"])
        E.wait("t", qA0)
        tEh = {}
        for n in range(4):
            q = qW0 if n % 2 == 0 else qW1
            if n >= 2:
                E.wait("y", tE, tEh[n - 2])
            emit_qslab(wsl[n % 2], 3, n, q)
            E.wait("t", q)
            for m in range(8):
                pb = psP[m % 2]
                if (n, m) >= (0, 2):
                    E.wait("t", vE)
                for k in range(16):
                    E.opi("t", lambda e, m=m, k=k, n=n, pb=pb: e.matmul(
                        pb[:], oTa[:, 1024 * k + 128 * m:1024 * k + 128 * (m + 1)],
                        wsl[n % 2][:, 512 * k:512 * (k + 1)],
                        start=(k == 0), stop=(k == 15)), tE,
                        1 if k == 15 else 0)
                E.wait("v", tE)
                E.opi("v", lambda e, m=m, n=n, pb=pb: e.tensor_tensor(
                    xts[m][:, 512 * n:512 * (n + 1)],
                    xts[m][:, 512 * n:512 * (n + 1)], pb[:], ALU.add), vE, 1)
            tEh[n] = tE.cnt
        E.wait("g", vE)
        for m in range(8):
            E.opi("g", lambda e, m=m: e.dma_start(
                out=xdst[128 * m:128 * (m + 1), :], in_=xts[m][:]), qO, 16)
        E.join()
        E.flush_block("sa_out")


def kernel(**inputs):
    maps = _shard_inputs(inputs)
    import os
    upto = int(os.environ.get("MK_UPTO", "6"))
    nc = build(upto=upto)
    res = run_bass_kernel_spmd(nc, maps, core_ids=list(range(NCORES)))
    out = np.concatenate([np.asarray(r["y"]) for r in res.results], axis=0)
    return out.reshape(1, 8192, D).astype(np.float32)


# version 09
# speedup vs baseline: 1.1204x; 1.0093x over previous
import numpy as np
import ml_dtypes
from contextlib import ExitStack
import concourse.bass as bass
import concourse.mybir as mybir
from concourse.bass_utils import run_bass_kernel_spmd

F32 = mybir.dt.float32
BF16 = mybir.dt.bfloat16
I32 = mybir.dt.int32
AF = mybir.ActivationFunctionType
ALU = mybir.AluOpType
AX = mybir.AxisListType
BF16NP = ml_dtypes.bfloat16

NCORES = 8
TC = 1024
D = 2048
H0 = 8192
H1 = 5632
NH = 16
HD = 128
MAGIC = 12582912.0
BIGNEG = -30000.0
SOFF = 32.0
LNEPS = 1e-5
QSCL = 0.08838834764831845

ENGS = ("t", "v", "s", "g", "y")
QUEUE_ENG = {"qS": "y", "qW0": "y", "qW1": "y", "qA0": "y", "qA1": "y",
             "qD0": "y", "qD1": "s",
             "qG": "g", "qX0": "g", "qX1": "g", "qO": "g"}


class Sem:
    __slots__ = ("h", "cnt", "name", "joined")

    def __init__(self, h, name):
        self.h, self.cnt, self.name, self.joined = h, 0, name, 0


class Emit:
    def __init__(self, nc, ctx):
        self.nc, self.ctx = nc, ctx
        self.bufs = {e: [] for e in ENGS}
        self.sems = {}

    def sem(self, name):
        if name not in self.sems:
            self.sems[name] = Sem(self.ctx.enter_context(self.nc.semaphore(name)),
                                  name)
        return self.sems[name]

    def op(self, eng, fn):
        self.bufs[eng].append(fn)

    def wait(self, eng, sem, cnt=None):
        n = sem.cnt if cnt is None else cnt
        if n > 0:
            self.bufs[eng].append(lambda e, h=sem.h, n=n: e.wait_ge(h, n))

    def dr(self, eng):
        self.bufs[eng].append(lambda e: e.drain())

    def opi(self, eng, fn, sem, k=1):
        if k == 0:
            self.bufs[eng].append(fn)
            return
        sem.cnt += k
        self.bufs[eng].append(lambda e, f=fn, h=sem.h, k=k: f(e).then_inc(h, k))

    def join(self):
        for qn, eng in QUEUE_ENG.items():
            if qn in self.sems:
                s = self.sems[qn]
                if s.cnt > s.joined:
                    self.wait(eng, s)
                    s.joined = s.cnt
        ph = self.sem("ph")
        ph.cnt += 5
        tgt = ph.cnt
        for e in ENGS:
            if e in ("t", "v", "s"):
                self.bufs[e].append(lambda eng: eng.drain())
            self.bufs[e].append(lambda eng, h=ph.h: eng.sem_inc(h, 1))
            self.bufs[e].append(lambda eng, h=ph.h, n=tgt: eng.wait_ge(h, n))

    _blk_n = 0

    def flush_block(self, name=None):
        Emit._blk_n += 1
        name = f"{name or 'blk'}_{Emit._blk_n}"
        bufs = self.bufs
        self.bufs = {e: [] for e in ENGS}
        with self.nc.Block(name) as blk:
            for key, reg in (("t", blk.tensor), ("v", blk.vector),
                             ("s", blk.scalar), ("g", blk.gpsimd),
                             ("y", blk.sync)):
                def body(e, ops=bufs[key]):
                    for f in ops:
                        f(e)
                reg(body)



def _consts_per_core(c):
    i_ = np.arange(64)
    Dt = (i_[None, :] - i_[:, None]).astype(np.float32)
    Dt[i_[None, :] < i_[:, None]] = 1e9
    Pc = np.full((16, 8), 1e9, np.float32)
    for r in range(c):
        Pc[:, r] = 1024.0 * (c - 1 - r)
    pos = (np.arange(TC, dtype=np.float64) + TC * c)
    half = HD // 2
    inv_freq = 1.0 / (10000.0 ** (np.arange(half, dtype=np.float64) / half))
    freqs = np.outer(pos, inv_freq)
    masks = np.zeros((4, 128, 512), np.float32)
    kk = np.arange(128)[:, None]
    qq = np.arange(512)[None, :]
    for j in range(4):
        masks[j] = np.where(128 * j + kk <= qq, 1.0, 0.0)
    masks = masks.astype(BF16NP)
    return dict(
        ident=np.eye(128, dtype=BF16NP),
        ident_f=np.eye(128, dtype=np.float32),
        ones_f=np.ones((1, 128), np.float32),
        ones_rbf=np.ones((1, 128), BF16NP),
        ones_bf=np.ones((128, 1), BF16NP),
        iota_loc=np.arange(TC, dtype=np.float32).reshape(1, TC),
        iota_m64=(np.arange(TC) % 64).astype(np.float32).reshape(1, TC),
        iota64=np.arange(64, dtype=np.float32).reshape(1, 64),
        k64m=(64.0 - (np.arange(128) % 64)).astype(np.float32).reshape(128, 1),
        Dt=Dt, Pc=Pc,
        cos=np.cos(freqs).astype(np.float32),
        sin=np.sin(freqs).astype(np.float32),
        fmask=masks,
    )


def _shard_inputs(inp):
    x = np.ascontiguousarray(np.asarray(inp["L0_attn_x"], np.float32).reshape(8192, D))
    w1 = np.asarray(inp["L0_ffn_w1"], np.float32)
    w2 = np.asarray(inp["L0_ffn_w2"], np.float32)
    w1T = np.ascontiguousarray(w1.T)
    w2T = np.ascontiguousarray(w2.T)
    qkvoT = np.stack([np.ascontiguousarray(np.asarray(inp[k], np.float32).T)
                      for k in ("L1_attn_w_q", "L1_attn_w_k",
                                "L1_attn_w_v", "L1_attn_w_o")])
    gT = np.ascontiguousarray(np.asarray(inp["L1_ffn_wq_gate"], np.int32).T)
    uT = np.ascontiguousarray(np.asarray(inp["L1_ffn_wq_up"], np.int32).T)
    dT_raw = np.ascontiguousarray(np.asarray(inp["L1_ffn_wq_down"], np.int32).T)
    dT = np.zeros((48 * 128, D), np.int32)
    dT[:H1] = dT_raw
    base = dict(
        wqkv=np.ascontiguousarray(np.asarray(inp["L0_attn_Wqkv"], np.float32)),
        bqkv=np.asarray(inp["L0_attn_bqkv"], np.float32).reshape(1, 3 * D),
        wout=np.ascontiguousarray(np.asarray(inp["L0_attn_Wout"], np.float32)),
        bout=np.asarray(inp["L0_attn_bout"], np.float32).reshape(1, D),
        slopes=np.asarray(inp["L0_attn_slopes"], np.float32).reshape(1, 16),
        sg=np.asarray(inp["L1_ffn_s_gate"], np.float32).reshape(1, H1),
        su=np.asarray(inp["L1_ffn_s_up"], np.float32).reshape(1, H1),
        sd=np.asarray(inp["L1_ffn_s_down"], np.float32).reshape(1, D),
    )
    maps = []
    for c in range(NCORES):
        m = dict(base)
        m.update(
            x0=np.ascontiguousarray(x[TC * c:TC * (c + 1)]),
            w1_rows=np.ascontiguousarray(w1[1024 * c:1024 * (c + 1)]),
            w1T_sh=np.ascontiguousarray(w1T[256 * c:256 * (c + 1)]),
            w2_rows=np.ascontiguousarray(w2[256 * c:256 * (c + 1)]),
            w2T_sh=np.ascontiguousarray(w2T[1024 * c:1024 * (c + 1)]),
            qkvoT_sh=np.ascontiguousarray(qkvoT[:, 256 * c:256 * (c + 1), :]),
            gT_sh=np.ascontiguousarray(gT[256 * c:256 * (c + 1)]),
            uT_sh=np.ascontiguousarray(uT[256 * c:256 * (c + 1)]),
            dT_sh=np.ascontiguousarray(dT[768 * c:768 * (c + 1)]),
        )
        m.update(_consts_per_core(c))
        maps.append(m)
    return maps



class B:
    pass


def _psum(nc, *a, **k):
    return nc.psum_tensor(*a, allow_name_mangling=True, **k)


def build(upto=6):
    nc = bass.Bass("TRN2", target_bir_lowering=False, debug=False,
                   detect_race_conditions=False)
    b = B()
    b.nc = nc
    dt = nc.dram_tensor

    def st_(*a, **k):
        return nc.sbuf_tensor(*a, allow_name_mangling=True, **k)
    b.st = st_
    io = {}
    io["x0"] = dt("x0", [TC, D], F32, kind="ExternalInput").ap()
    io["y"] = dt("y", [TC, D], F32, kind="ExternalOutput").ap()
    io["dbg0"] = dt("dbg0", [128, H0], F32, kind="ExternalOutput").ap()
    io["dbg1"] = dt("dbg1", [128, 64], F32, kind="ExternalOutput").ap()
    io["dbg2"] = dt("dbg2", [128, D], F32, kind="ExternalOutput").ap()
    for name, shape, dty in [
        ("w1_rows", [1024, D], F32), ("w1T_sh", [256, H0], F32),
        ("w2_rows", [256, H0], F32), ("w2T_sh", [1024, D], F32),
        ("wqkv", [D, 3 * D], F32), ("bqkv", [1, 3 * D], F32),
        ("wout", [D, D], F32), ("bout", [1, D], F32), ("slopes", [1, 16], F32),
        ("qkvoT_sh", [4, 256, D], F32),
        ("gT_sh", [256, H1], I32), ("uT_sh", [256, H1], I32),
        ("dT_sh", [768, D], I32),
        ("sg", [1, H1], F32), ("su", [1, H1], F32), ("sd", [1, D], F32),
        ("ident", [128, 128], BF16), ("ident_f", [128, 128], F32),
        ("ones_f", [1, 128], F32),
        ("ones_rbf", [1, 128], BF16), ("ones_bf", [128, 1], BF16),
        ("iota_loc", [1, TC], F32), ("iota_m64", [1, TC], F32),
        ("iota64", [1, 64], F32), ("k64m", [128, 1], F32),
        ("Dt", [64, 64], F32), ("Pc", [16, 8], F32),
        ("cos", [TC, 64], F32), ("sin", [TC, 64], F32),
        ("fmask", [4, 128, 512], BF16),
    ]:
        io[name] = dt(name, shape, dty, kind="ExternalInput").ap()
    b.io = io

    dr = {}
    dr["xa"] = dt("xa", [TC, D], F32)
    dr["xb"] = dt("xb", [TC, D], F32)
    dr["st_sh"] = dt("st_sh", [1, 1280], F32)
    dr["st_all"] = dt("st_all", [8, 1280], F32)
    dr["w1b_sh"] = dt("w1b_sh", [2, 128, H0], BF16)
    dr["w1b"] = dt("w1b", [16, 128, H0], BF16)
    dr["w2b_sh"] = dt("w2b_sh", [8, 128, D], BF16)
    dr["w2b"] = dt("w2b", [64, 128, D], BF16)
    dr["qkvob_sh"] = dt("qkvob_sh", [4, 2, 128, D], BF16)
    dr["qkvob"] = dt("qkvob", [8, 4, 2, 128, D], BF16)
    dr["gb_sh"] = dt("gb_sh", [2, 128, H1], BF16)
    dr["ub_sh"] = dt("ub_sh", [2, 128, H1], BF16)
    dr["db_sh"] = dt("db_sh", [6, 128, D], BF16)
    dr["gb"] = dt("gb", [16, 128, H1], BF16)
    dr["ub"] = dt("ub", [16, 128, H1], BF16)
    dr["db"] = dt("db", [48, 128, D], BF16)
    dr["qkv0"] = dt("qkv0", [TC, 3 * D], BF16)
    dr["Tc_sh"] = dt("Tc_sh", [16, 128, 128], F32)
    dr["Tc"] = dt("Tc", [8, 16, 128, 128], F32)
    dr["qkv1_sh"] = dt("qkv1_sh", [3, NH, 128, TC], BF16)
    dr["qkv1"] = dt("qkv1", [8, 3, NH, 128, TC], BF16)
    dr["o1_sh"] = dt("o1_sh", [2, 128, 8192], BF16)
    dr["o1"] = dt("o1", [8, 2, 128, 8192], BF16)
    b.dr = dr

    ctx = ExitStack()
    b.ctx = ctx
    E = Emit(nc, ctx)
    b.E = E
    g = {}
    g["ident"] = ctx.enter_context(st_("c_ident", [128, 128], BF16))
    g["ident_f"] = ctx.enter_context(st_("c_identf", [128, 128], F32))
    g["ones_f"] = ctx.enter_context(st_("c_ones_f", [1, 128], F32))
    g["ones_rbf"] = ctx.enter_context(st_("c_ones_rbf", [1, 128], BF16))
    g["ones_bf"] = ctx.enter_context(st_("c_ones_bf", [128, 1], BF16))
    g["stat"] = ctx.enter_context(st_("g_stat", [128, 64], F32))
    b.g = g

    for s in ("qS", "qW0", "qW1", "qA0", "qA1", "qD0", "qD1",
              "qG", "qX0", "qX1", "qO", "tE", "vE", "sE", "cc"):
        E.sem(s)

    qS = E.sem("qS")
    for name, dst in (("ident", g["ident"]), ("ident_f", g["ident_f"]),
                      ("ones_f", g["ones_f"]),
                      ("ones_rbf", g["ones_rbf"]), ("ones_bf", g["ones_bf"])):
        E.opi("y", lambda e, s=dst, i=io[name]: e.dma_start(out=s[:], in_=i), qS, 16)
    E.opi("y", lambda e: e.dma_start(out=dr["xa"].ap(), in_=io["x0"]), qS, 16)
    E.join()
    E.flush_block("p0")

    _prep(b)

    xc = "xa"
    seq = [("f0", "_ffn0"), ("la", "_lattn"), ("f0", "_ffn0"),
           ("f1", "_ffn1"), ("sa", "_sattn"), ("f1", "_ffn1")]
    import os as _os
    only = _os.environ.get("MK_ONLY")
    run_seq = [seq[int(only) - 1]] if only else seq[:upto]
    for i, (tag, fn) in enumerate(run_seq):
        phase_i = i + 1
        last = (phase_i == 6) and not only
        dst = None if last else ("xb" if xc == "xa" else "xa")
        globals()[fn](b, xc, dst, final=last)
        if dst is not None:
            xc = dst
    upto = 0 if only else upto
    if upto < 6:
        E.opi("y", lambda e, x=xc: e.dma_start(out=io["y"], in_=dr[x].ap()), qS, 16)
        E.join()
        E.flush_block("fincopy")
    ctx.close()
    return nc



def _bcast(b, psb, src_ap_fn, dst, w, f32=True):
    E, g = b.E, b.g
    tE, vE = E.sem("tE"), E.sem("vE")
    E.wait("t", vE)
    for j in range((w + 511) // 512):
        ww = min(512, w - 512 * j)
        E.opi("t", lambda e, j=j, ww=ww: e.matmul(
            psb[:, :ww], g["ones_f"][0:1, :], src_ap_fn(j, ww),
            start=True, stop=True), tE, 1)
        E.wait("v", tE)
        E.opi("v", lambda e, j=j, ww=ww: e.tensor_copy(
            dst[:, 512 * j:512 * j + ww], psb[:, :ww]), vE, 1)
        E.wait("t", vE)


def _ag(b, name_in, name_out):
    E, dr = b.E, b.dr
    cc, qG = E.sem("cc"), E.sem("qG")
    E.wait("g", qG)
    E.wait("g", E.sem("qO"))
    E.opi("g", lambda e: e.collective_compute(
        "AllGather", ALU.bypass, replica_groups=[list(range(NCORES))],
        ins=[dr[name_in].ap()], outs=[dr[name_out].ap()]), cc, 1)
    if not hasattr(b, "cc_marks"):
        b.cc_marks = {}
    b.cc_marks[name_out] = cc.cnt


def _ternary_quant_ops(b, wt_ap, rc, sc, tq, out_ap, w):
    E = b.E
    vE = E.sem("vE")
    E.wait("v", E.sem("qG"))
    E.op("v", lambda e: e.tensor_tensor(tq[:, :w], wt_ap, rc[:, :w], ALU.mult))
    E.op("v", lambda e: e.tensor_scalar(tq[:, :w], tq[:, :w], MAGIC, MAGIC,
                                        ALU.add, ALU.subtract))
    E.op("v", lambda e: e.tensor_scalar(tq[:, :w], tq[:, :w], 1.0, -1.0,
                                        ALU.min, ALU.max))
    E.opi("v", lambda e: e.tensor_tensor(out_ap, tq[:, :w], sc[:, :w], ALU.mult),
          vE, 1)



def _prep(b):
    import os
    pstop = int(os.environ.get("MK_PSTOP", "9"))
    nc, E, io, dr, g, st_ = b.nc, b.E, b.io, b.dr, b.g, b.st
    if pstop <= 0:
        return
    qS, qG = E.sem("qS"), E.sem("qG")
    qW0, qW1 = E.sem("qW0"), E.sem("qW1")
    qX0, qX1 = E.sem("qX0"), E.sem("qX1")
    tE, vE, sE, cc = E.sem("tE"), E.sem("vE"), E.sem("sE"), E.sem("cc")

    with ExitStack() as px:
        wrow = px.enter_context(st_("p_wrow", [128, H0], F32))
        wrow2 = px.enter_context(st_("p_wrow2", [128, H0], F32))
        stt = px.enter_context(st_("p_stats", [128, 10], F32))
        srow = px.enter_context(st_("p_srow", [1, 1280], F32))
        pT = px.enter_context(_psum(nc, "pp_T", [1, 512], F32))
        vh = []
        for j in range(10):
            buf = wrow if j % 2 == 0 else wrow2
            q = qW0 if j % 2 == 0 else qW1
            if j >= 2:
                E.wait("y", vE, vh[j - 2])
            if j < 8:
                E.opi("y", lambda e, bu=buf, j=j: e.dma_start(
                    out=bu[:, :D], in_=io["w1_rows"][128 * j:128 * (j + 1), :]),
                    q, 16)
                E.wait("v", q)
                E.opi("v", lambda e, bu=buf, j=j: e.tensor_reduce(
                    stt[:, j:j + 1], bu[:, :D], axis=AX.X, op=ALU.add,
                    apply_absolute_value=True), vE, 1)
            else:
                jj = j - 8
                E.opi("y", lambda e, bu=buf, jj=jj: e.dma_start(
                    out=bu[:, :], in_=io["w2_rows"][128 * jj:128 * (jj + 1), :]),
                    q, 16)
                E.wait("v", q)
                E.opi("v", lambda e, bu=buf, j=j: e.tensor_reduce(
                    stt[:, j:j + 1], bu[:, :], axis=AX.X, op=ALU.add,
                    apply_absolute_value=True), vE, 1)
            vh.append(vE.cnt)
        pamode = int(os.environ.get("MK_PAMODE", "4"))
        if pamode >= 2:
            E.wait("t", vE)
            for j0 in range(0, 10, 4):
                nb = min(4, 10 - j0)
                for k in range(nb):
                    E.opi("t", lambda e, j=j0 + k, s=k: e.transpose(
                        pT[0:1, 128 * s:128 * s + 128], stt[:, j:j + 1],
                        g["ident_f"][:]), tE, 1)
                E.wait("v", tE)
                E.opi("v", lambda e, j0=j0, nb=nb: e.tensor_copy(
                    srow[0:1, 128 * j0:128 * (j0 + nb)],
                    pT[0:1, 0:128 * nb]), vE, 1)
                E.wait("t", vE)
        if pamode >= 3:
            E.wait("g", vE)
            E.opi("g", lambda e: e.dma_start(out=dr["st_sh"].ap(), in_=srow[:]),
                  qG, 16)
        if pamode >= 4:
            _ag(b, "st_sh", "st_all")
        E.join()
        E.flush_block("prepA")
    if pstop <= 1:
        return

    with ExitStack() as px:
        wt = px.enter_context(st_("pb_wt", [128, H0], F32))
        wt2 = px.enter_context(st_("pb_wt2", [128, H0], F32))
        tq = px.enter_context(st_("pb_tq", [128, 2048], F32))
        ob = px.enter_context(st_("pb_ob", [128, H0], BF16))
        rc = px.enter_context(st_("pb_rc", [128, 2048], F32))
        sc = px.enter_context(st_("pb_sc", [128, 2048], F32))
        psb = px.enter_context(_psum(nc, "pp_b", [128, 512], F32))
        g["row"] = px.enter_context(st_("pb_row", [1, H0], F32))
        g["row2"] = px.enter_context(st_("pb_row2", [1, H0], F32))
        E.wait("y", cc)
        E.opi("y", lambda e: e.dma_start(
            out=g["row"][0:1, :H0].rearrange("a (r n) -> a r n", r=8),
            in_=dr["st_all"].ap()[:, 0:1024].rearrange("r n -> () r n")), qS, 16)
        E.wait("v", qS)
        E.op("v", lambda e: e.tensor_scalar(
            g["row"][0:1, :H0], g["row"][0:1, :H0], 1.0 / D, 1e-5,
            ALU.mult, ALU.max))
        E.dr("v")
        E.opi("v", lambda e: e.reciprocal(g["row2"][0:1, :H0], g["row"][0:1, :H0]),
              vE, 1)
        E.dr("v")
        for i in range(2):
            buf = wt if i % 2 == 0 else wt2
            q = qW0 if i % 2 == 0 else qW1
            E.opi("y", lambda e, bu=buf, i=i: e.dma_start(
                out=bu[:], in_=io["w1T_sh"][128 * i:128 * (i + 1), :]), q, 16)
            E.wait("v", q)
            for sl in range(4):
                lo = 2048 * sl
                _bcast(b, psb, lambda j, ww, lo=lo: g["row2"][0:1, lo + 512 * j:lo + 512 * j + ww],
                       rc, 2048)
                _bcast(b, psb, lambda j, ww, lo=lo: g["row"][0:1, lo + 512 * j:lo + 512 * j + ww],
                       sc, 2048)
                _ternary_quant_ops(b, buf[:, lo:lo + 2048], rc, sc, tq,
                                   ob[:, lo:lo + 2048], 2048)
            E.wait("g", vE)
            E.opi("g", lambda e, i=i: e.dma_start(
                out=dr["w1b_sh"].ap()[i], in_=ob[:]), qG, 16)
        _ag(b, "w1b_sh", "w1b")
        E.join()
        E.flush_block("prepB")
    if pstop <= 2:
        return

    with ExitStack() as px:
        wt = px.enter_context(st_("pc_wt", [128, D], F32))
        wt2 = px.enter_context(st_("pc_wt2", [128, D], F32))
        tq = px.enter_context(st_("pc_tq", [128, D], F32))
        ob = px.enter_context(st_("pc_ob", [128, D], BF16))
        rc = px.enter_context(st_("pc_rc", [128, D], F32))
        sc = px.enter_context(st_("pc_sc", [128, D], F32))
        psb = px.enter_context(_psum(nc, "pp_c", [128, 512], F32))
        g["row"] = px.enter_context(st_("pc_row", [1, D], F32))
        g["row2"] = px.enter_context(st_("pc_row2", [1, D], F32))
        E.opi("y", lambda e: e.dma_start(
            out=g["row"][0:1, :D].rearrange("a (r n) -> a r n", r=8),
            in_=dr["st_all"].ap()[:, 1024:1280].rearrange("r n -> () r n")), qS, 16)
        E.wait("v", qS)
        E.op("v", lambda e: e.tensor_scalar(
            g["row"][0:1, :D], g["row"][0:1, :D], 1.0 / H0, 1e-5,
            ALU.mult, ALU.max))
        E.dr("v")
        E.opi("v", lambda e: e.reciprocal(g["row2"][0:1, :D], g["row"][0:1, :D]),
              vE, 1)
        E.dr("v")
        _bcast(b, psb, lambda j, ww: g["row2"][0:1, 512 * j:512 * j + ww], rc, D)
        _bcast(b, psb, lambda j, ww: g["row"][0:1, 512 * j:512 * j + ww], sc, D)
        obh = []
        for i in range(8):
            buf = wt if i % 2 == 0 else wt2
            q = qW0 if i % 2 == 0 else qW1
            if i >= 1:
                E.wait("y", vE)
                if i >= 1:
                    E.wait("y", qG)
            E.opi("y", lambda e, bu=buf, i=i: e.dma_start(
                out=bu[:], in_=io["w2T_sh"][128 * i:128 * (i + 1), :]), q, 16)
            E.wait("v", q)
            _ternary_quant_ops(b, buf[:], rc, sc, tq, ob[:], D)
            E.wait("g", vE)
            E.opi("g", lambda e, i=i: e.dma_start(
                out=dr["w2b_sh"].ap()[i], in_=ob[:]), qG, 16)
        _ag(b, "w2b_sh", "w2b")
        E.join()
        E.flush_block("prepC")
    if pstop <= 3:
        return

    with ExitStack() as px:
        wt = px.enter_context(st_("pd_wt", [128, D], F32))
        awt = px.enter_context(st_("pd_awt", [128, D], F32))
        tq = px.enter_context(st_("pd_tq", [128, D], F32))
        ob = px.enter_context(st_("pd_ob", [128, D], BF16))
        rc = px.enter_context(st_("pd_rc", [128, D], F32))
        sc = px.enter_context(st_("pd_sc", [128, D], F32))
        psb = px.enter_context(_psum(nc, "pp_d", [128, 512], F32))
        pss = px.enter_context(_psum(nc, "pp_ds", [1, 2048], F32))
        onesc = px.enter_context(st_("pd_ones", [128, 1], F32))
        g["row"] = px.enter_context(st_("pd_row", [1, D], F32))
        g["row2"] = px.enter_context(st_("pd_row2", [1, D], F32))
        E.op("v", lambda e: e.memset(onesc[:], 1.0))
        for mat in range(4):
            for i in range(2):
                t_i = mat * 2 + i
                q = qW0 if t_i % 2 == 0 else qW1
                if t_i >= 1:
                    E.wait("y", vE)
                    E.wait("y", qG)
                E.opi("y", lambda e, mat=mat, i=i: e.dma_start(
                    out=wt[:], in_=io["qkvoT_sh"][mat, 128 * i:128 * (i + 1), :]),
                    q, 16)
                E.wait("s", q)
                E.opi("s", lambda e: e.activation(awt[:], wt[:], AF.Abs), sE, 1)
                E.wait("t", sE)
                E.wait("t", vE)
                for j in range(4):
                    E.opi("t", lambda e, j=j: e.matmul(
                        pss[0:1, 512 * j:512 * (j + 1)], onesc[:],
                        awt[:, 512 * j:512 * (j + 1)], start=True, stop=True),
                        tE, 1)
                E.wait("v", tE)
                E.op("v", lambda e: e.tensor_scalar(
                    g["row"][0:1, :D], pss[0:1, :], 1.0 / 128.0, 1e-8,
                    ALU.mult, ALU.max))
                E.dr("v")
                E.opi("v", lambda e: e.reciprocal(
                    g["row2"][0:1, :D], g["row"][0:1, :D]), vE, 1)
                E.dr("v")
                _bcast(b, psb, lambda j, ww: g["row2"][0:1, 512 * j:512 * j + ww],
                       rc, D)
                _bcast(b, psb, lambda j, ww: g["row"][0:1, 512 * j:512 * j + ww],
                       sc, D)
                _ternary_quant_ops(b, wt[:], rc, sc, tq, ob[:], D)
                E.wait("g", vE)
                E.opi("g", lambda e, mat=mat, i=i: e.dma_start(
                    out=dr["qkvob_sh"].ap()[mat, i], in_=ob[:]), qG, 16)
        _ag(b, "qkvob_sh", "qkvob")
        E.join()
        E.flush_block("prepD")
    if pstop <= 4:
        return

    with ExitStack() as px:
        wt = px.enter_context(st_("pe_wt", [128, H1], F32))
        tq = px.enter_context(st_("pe_tq", [128, H1], F32))
        ob = px.enter_context(st_("pe_ob", [128, H1], BF16))
        scl = px.enter_context(st_("pe_scl", [128, H1], F32))
        psb = px.enter_context(_psum(nc, "pp_e", [128, 512], F32))
        g["row"] = px.enter_context(st_("pe_row", [1, H1], F32))
        for srow_name, wsh, osh, width, nt in (
                ("sg", "gT_sh", "gb_sh", H1, 2), ("su", "uT_sh", "ub_sh", H1, 2),
                ("sd", "dT_sh", "db_sh", D, 6)):
            E.wait("y", tE)
            E.opi("y", lambda e, n=srow_name, w=width: e.dma_start(
                out=g["row"][0:1, :w], in_=io[n]), qS, 16)
            E.wait("t", qS)
            _bcast(b, psb, lambda j, ww: g["row"][0:1, 512 * j:512 * j + ww],
                   scl, width)
            for i in range(nt):
                q = qX0 if i % 2 == 0 else qX1
                if i >= 1:
                    E.wait("g", vE)
                    E.wait("g", qG)
                E.opi("g", lambda e, wsh=wsh, i=i, w=width: e.dma_start(
                    out=wt[:, :w], in_=io[wsh][128 * i:128 * (i + 1), :]), q, 16)
                E.wait("v", q)
                E.wait("v", qG)
                E.op("v", lambda e, w=width: e.tensor_scalar(
                    tq[:, :w], wt[:, :w], -128.0, 0.0, ALU.add, ALU.add))
                E.opi("v", lambda e, w=width: e.tensor_tensor(
                    ob[:, :w], tq[:, :w], scl[:, :w], ALU.mult), vE, 1)
                E.wait("g", vE)
                E.opi("g", lambda e, osh=osh, i=i: e.dma_start(
                    out=dr[osh].ap()[i], in_=ob[:, :dr[osh].shape[2]]), qG, 16)
        _ag(b, "gb_sh", "gb")
        _ag(b, "ub_sh", "ub")
        _ag(b, "db_sh", "db")
        E.join()
        E.flush_block("prepE")



def _ln_stats(b, xt_ap, sq, mi, quant):
    E, g = b.E, b.g
    vE, sE = E.sem("vE"), E.sem("sE")
    S = g["stat"]
    c0 = 16 * mi

    def col(j):
        return S[:, c0 + j:c0 + j + 1]
    E.wait("s", vE)
    E.opi("s", lambda e: e.activation(sq[:], xt_ap, AF.Square), sE, 1)
    E.op("v", lambda e: e.tensor_reduce(col(5), xt_ap, axis=AX.X, op=ALU.add))
    E.wait("v", sE)
    E.op("v", lambda e: e.tensor_reduce(col(6), sq[:], axis=AX.X, op=ALU.add))
    E.dr("v")
    E.op("v", lambda e: e.tensor_scalar(col(0), col(5), 1.0 / D, None, ALU.mult))
    E.op("v", lambda e: e.tensor_scalar(col(6), col(6), 1.0 / D, None, ALU.mult))
    E.dr("v")
    E.op("v", lambda e: e.tensor_tensor(col(7), col(0), col(0), ALU.mult))
    E.dr("v")
    E.op("v", lambda e: e.tensor_tensor(col(6), col(6), col(7), ALU.subtract))
    E.dr("v")
    E.opi("v", lambda e: e.tensor_scalar(col(6), col(6), 1.0, LNEPS,
                                         ALU.mult, ALU.add), vE, 1)
    E.dr("v")
    E.wait("s", vE)
    E.opi("s", lambda e: e.activation(col(7), col(6), AF.Sqrt), sE, 1)
    E.dr("s")
    E.wait("v", sE)
    E.opi("v", lambda e: e.reciprocal(col(1), col(7)), vE, 1)
    if quant:
        E.op("v", lambda e: e.tensor_reduce(col(5), xt_ap, axis=AX.X, op=ALU.max))
        E.op("v", lambda e: e.tensor_reduce(col(6), xt_ap, axis=AX.X, op=ALU.min))
        E.dr("v")
        E.op("v", lambda e: e.tensor_tensor(col(5), col(5), col(0), ALU.subtract))
        E.op("v", lambda e: e.tensor_tensor(col(6), col(0), col(6), ALU.subtract))
        E.dr("v")
        E.op("v", lambda e: e.tensor_tensor(col(5), col(5), col(6), ALU.max))
        E.dr("v")
        E.op("v", lambda e: e.tensor_tensor(col(5), col(5), col(1), ALU.mult))
        E.dr("v")
        E.op("v", lambda e: e.tensor_scalar(col(5), col(5), 1.0, 1e-5,
                                            ALU.mult, ALU.max))
        E.dr("v")
        E.op("v", lambda e: e.tensor_scalar(col(2), col(5), 1.0 / 127.0, None,
                                            ALU.mult))
        E.op("v", lambda e: e.reciprocal(col(6), col(5)))
        E.dr("v")
        E.op("v", lambda e: e.tensor_scalar(col(3), col(6), 127.0, None, ALU.mult))
        E.dr("v")
        E.opi("v", lambda e: e.tensor_tensor(col(4), col(1), col(3), ALU.mult),
              vE, 1)
        E.dr("v")


def _transpose_128(b, src_bf_ap_fn, nchunk, psT, dst, dst_off):
    E, g = b.E, b.g
    tE, vE = E.sem("tE"), E.sem("vE")
    for b0 in range(0, nchunk, 8):
        nb = min(8, nchunk - b0)
        for k in range(nb):
            E.opi("t", lambda e, k=k, b0=b0: e.transpose(
                psT[:, 128 * k:128 * (k + 1)], src_bf_ap_fn(b0 + k),
                g["ident"][:]), tE, 1)
        E.wait("v", tE)
        E.opi("v", lambda e, b0=b0, nb=nb: e.tensor_copy(
            dst[:, dst_off + 128 * b0:dst_off + 128 * (b0 + nb)],
            psT[:, :128 * nb]), vE, 1)
        E.wait("t", vE)



def _ffn0(b, src, dst, final=False):
    nc, E, io, dr, g, st_ = b.nc, b.E, b.io, b.dr, b.g, b.st
    qW0, qW1 = E.sem("qW0"), E.sem("qW1")
    qX0, qX1 = E.sem("qX0"), E.sem("qX1")
    qA0, qA1 = E.sem("qA0"), E.sem("qA1")
    qO = E.sem("qO")
    tE, vE, sE = E.sem("tE"), E.sem("vE"), E.sem("sE")
    xsrc = dr[src].ap()
    xdst = io["y"] if final else dr[dst].ap()
    S = g["stat"]

    E.wait("y", E.sem("cc"), b.cc_marks["w2b"])
    with ExitStack() as px:
        xt = [px.enter_context(st_(f"f0_xt{i}", [128, D], F32)) for i in range(2)]
        sq = px.enter_context(st_("f0_sq", [128, H0 // 4], F32))
        xqb = px.enter_context(st_("f0_xqb", [128, D], BF16))
        xqT = [px.enter_context(st_(f"f0_xqT{i}", [128, D], BF16)) for i in range(2)]
        hb = [px.enter_context(st_(f"f0_hb{i}", [128, H0], F32)) for i in range(2)]
        hqT = [px.enter_context(st_(f"f0_hqT{i}", [128, H0], BF16)) for i in range(2)]
        w1s = [px.enter_context(st_(f"f0_w1s{i}", [128, 16 * 512], BF16))
               for i in range(2)]
        w2s = [px.enter_context(st_(f"f0_w2s{i}", [128, 16 * 512], BF16))
               for i in range(2)]
        psT = px.enter_context(_psum(nc, "f0_psT", [128, 1024], BF16))
        qO_hist = []
        for p in range(4):
            qA = qA0 if p % 2 == 0 else qA1
            if p >= 1:
                E.wait("y", qO)
            for mi in range(2):
                m = 2 * p + mi
                E.opi("y", lambda e, mi=mi, m=m: e.dma_start(
                    out=xt[mi][:], in_=xsrc[128 * m:128 * (m + 1), :]), qA, 16)
            for mi in range(2):
                E.wait("v", qA)
                E.wait("s", qA)
                _ln_stats(b, xt[mi][:], sq, mi, quant=True)
                c0 = 16 * mi
                E.dr("v")
                E.op("v", lambda e, mi=mi, c0=c0: e.tensor_scalar(
                    sq[:, :D], xt[mi][:], S[:, c0:c0 + 1], S[:, c0 + 4:c0 + 5],
                    ALU.subtract, ALU.mult))
                E.opi("v", lambda e: e.tensor_scalar(
                    xqb[:], sq[:, :D], MAGIC, MAGIC, ALU.add, ALU.subtract),
                    vE, 1)
                E.wait("t", vE)
                _transpose_128(b, lambda k, mi=mi: xqb[:, 128 * k:128 * (k + 1)],
                               16, psT, xqT[mi], 0)
            with ExitStack() as pf:
                psF = [pf.enter_context(_psum(nc, f"f0_psF{i}", [128, 512], F32))
                       for i in range(4)]
                tEh, sEh = {}, {}
                tEh[-2] = tEh[-1] = b.f0_prev_ffn1_last if hasattr(b, "f0_prev_ffn1_last") else 0
                for mi in range(2):
                    E.op("v", lambda e, mi=mi: e.memset(
                        S[:, 16 * mi + 11:16 * mi + 12], 0.0))
                E.wait("s", vE)
                for n in range(16):
                    q = qW0 if n % 2 == 0 else qW1
                    if tEh[n - 2]:
                        E.wait("y", tE, tEh[n - 2])
                    for half in range(2):
                        E.opi("y", lambda e, n=n, half=half: e.dma_start(
                            out=w1s[n % 2][:, 4096 * half:4096 * (half + 1)]
                            .rearrange("p (c h) -> p c h", h=512),
                            in_=dr["w1b"].ap()[8 * half:8 * (half + 1), :,
                                               512 * n:512 * (n + 1)]
                            .transpose([1, 0, 2])), q, 16)
                    E.wait("t", q)
                    if n >= 2:
                        E.wait("t", sE, sEh[n - 2])
                    for mi in range(2):
                        bank = psF[2 * mi + n % 2]
                        for k in range(16):
                            E.opi("t", lambda e, mi=mi, k=k, n=n, bank=bank:
                                  e.matmul(bank[:],
                                           xqT[mi][:, 128 * k:128 * (k + 1)],
                                           w1s[n % 2][:, 512 * k:512 * (k + 1)],
                                           start=(k == 0), stop=(k == 15)),
                                  tE, 1 if k == 15 else 0)
                    tEh[n] = tE.cnt
                    E.wait("s", tE)
                    for mi in range(2):
                        c0 = 16 * mi
                        bank = psF[2 * mi + n % 2]
                        E.opi("s", lambda e, mi=mi, n=n, c0=c0, bank=bank:
                              e.activation(hb[mi][:, 512 * n:512 * (n + 1)],
                                           bank[:], AF.Gelu_apprx_tanh,
                                           scale=S[:, c0 + 2:c0 + 3]), sE, 1)
                    sEh[n] = sE.cnt
                    E.wait("v", sE)
                    for mi in range(2):
                        c0 = 16 * mi
                        E.op("v", lambda e, mi=mi, n=n, c0=c0: e.tensor_reduce(
                            S[:, c0 + 8:c0 + 9], hb[mi][:, 512 * n:512 * (n + 1)],
                            axis=AX.X, op=ALU.max, apply_absolute_value=True))
                        E.dr("v")
                        E.op("v", lambda e, c0=c0: e.tensor_tensor(
                            S[:, c0 + 11:c0 + 12], S[:, c0 + 11:c0 + 12],
                            S[:, c0 + 8:c0 + 9], ALU.max))
            b.f0_prev_ffn1_last = E.sem("tE").cnt
            import os as _os
            if p == 0 and _os.environ.get("MK_DBG"):
                E.join()
                E.opi("g", lambda e: e.dma_start(out=io["dbg0"], in_=hb[0][:]),
                      E.sem("qO"), 16)
                E.opi("g", lambda e: e.dma_start(out=io["dbg1"], in_=S[:]),
                      E.sem("qO"), 16)
                E.opi("g", lambda e: e.dma_start(out=io["dbg2"], in_=xqT[0][:]),
                      E.sem("qO"), 16)
                E.join()
            for mi in range(2):
                c0 = 16 * mi
                E.dr("v")
                E.op("v", lambda e, c0=c0: e.tensor_scalar(
                    S[:, c0 + 8:c0 + 9], S[:, c0 + 11:c0 + 12], 1.0, 1e-5,
                    ALU.mult, ALU.max))
                E.dr("v")
                E.op("v", lambda e, c0=c0: e.reciprocal(
                    S[:, c0 + 7:c0 + 8], S[:, c0 + 8:c0 + 9]))
                E.dr("v")
                E.op("v", lambda e, c0=c0: e.tensor_scalar(
                    S[:, c0 + 9:c0 + 10], S[:, c0 + 7:c0 + 8], 127.0, None,
                    ALU.mult))
                E.opi("v", lambda e, c0=c0: e.tensor_scalar(
                    S[:, c0 + 10:c0 + 11], S[:, c0 + 8:c0 + 9], 0.5 / 127.0,
                    None, ALU.mult), vE, 1)
                E.dr("v")
            for mi in range(2):
                c0 = 16 * mi
                for kq in range(4):
                    E.op("v", lambda e, mi=mi, kq=kq, c0=c0: e.tensor_scalar(
                        sq[:], hb[mi][:, 2048 * kq:2048 * (kq + 1)],
                        S[:, c0 + 9:c0 + 10], MAGIC, ALU.mult, ALU.add))
                    E.opi("v", lambda e: e.tensor_scalar(
                        xqb[:], sq[:], MAGIC, None, ALU.subtract), vE, 1)
                    E.wait("t", vE)
                    _transpose_128(b, lambda k: xqb[:, 128 * k:128 * (k + 1)],
                                   16, psT, hqT[mi], 2048 * kq)
            with ExitStack() as pf:
                psD = [pf.enter_context(_psum(nc, f"f0_psD{i}", [128, 512], F32))
                       for i in range(4)]
                visit = 0
                tEh2 = {}
                tEh2[-2] = tEh2[-1] = getattr(b, "f0_prev_ffn2_last", 0)
                for half in range(2):
                    for kq in range(4):
                        for nd2 in range(2):
                            nd = 2 * half + nd2
                            q = qW0 if visit % 2 == 0 else qW1
                            if tEh2[visit - 2]:
                                E.wait("y", tE, tEh2[visit - 2])
                            for hh in range(2):
                                E.opi("y", lambda e, kq=kq, nd=nd, hh=hh, v2=visit:
                                      e.dma_start(
                                          out=w2s[v2 % 2][:, 4096 * hh:4096 * (hh + 1)]
                                          .rearrange("p (c d) -> p c d", d=512),
                                          in_=dr["w2b"].ap()[16 * kq + 8 * hh:
                                                             16 * kq + 8 * (hh + 1),
                                                             :, 512 * nd:512 * (nd + 1)]
                                          .transpose([1, 0, 2])), q, 16)
                            E.wait("t", q)
                            if kq == 0:
                                E.wait("t", vE)
                            for mi in range(2):
                                bank = psD[2 * mi + nd2]
                                for k in range(16):
                                    E.opi("t", lambda e, mi=mi, k=k, kq=kq, v2=visit,
                                          bank=bank: e.matmul(
                                              bank[:],
                                              hqT[mi][:, 2048 * kq + 128 * k:
                                                      2048 * kq + 128 * (k + 1)],
                                              w2s[v2 % 2][:, 512 * k:512 * (k + 1)],
                                              start=(kq == 0 and k == 0),
                                              stop=(kq == 3 and k == 15)),
                                          tE, 1 if k == 15 else 0)
                            tEh2[visit] = tE.cnt
                            visit += 1
                    E.wait("v", tE)
                    for mi in range(2):
                        c0 = 16 * mi
                        for nd2 in range(2):
                            nd = 2 * half + nd2
                            bank = psD[2 * mi + nd2]
                            E.opi("v", lambda e, mi=mi, nd=nd, c0=c0, bank=bank:
                                  e.scalar_tensor_tensor(
                                      xt[mi][:, 512 * nd:512 * (nd + 1)],
                                      bank[:], S[:, c0 + 10:c0 + 11],
                                      xt[mi][:, 512 * nd:512 * (nd + 1)],
                                      ALU.mult, ALU.add), vE, 1)
            b.f0_prev_ffn2_last = E.sem("tE").cnt
            E.wait("g", vE)
            for mi in range(2):
                m = 2 * p + mi
                E.opi("g", lambda e, mi=mi, m=m: e.dma_start(
                    out=xdst[128 * m:128 * (m + 1), :], in_=xt[mi][:]), qO, 16)
            qO_hist.append(qO.cnt)
        E.join()
        E.flush_block("ffn0")



def _ffn1(b, src, dst, final=False):
    nc, E, io, dr, g, st_ = b.nc, b.E, b.io, b.dr, b.g, b.st
    qW0, qW1 = E.sem("qW0"), E.sem("qW1")
    qX0, qX1 = E.sem("qX0"), E.sem("qX1")
    qA0, qA1 = E.sem("qA0"), E.sem("qA1")
    qO = E.sem("qO")
    tE, vE, sE = E.sem("tE"), E.sem("vE"), E.sem("sE")
    xsrc = dr[src].ap()
    xdst = io["y"] if final else dr[dst].ap()
    S = g["stat"]

    E.wait("y", E.sem("cc"), b.cc_marks["db"])
    with ExitStack() as px:
        xt = [px.enter_context(st_(f"f1_xt{i}", [128, D], F32)) for i in range(2)]
        sq = px.enter_context(st_("f1_sq", [128, D], F32))
        xqb = px.enter_context(st_("f1_xqb", [128, D], BF16))
        xqT = [px.enter_context(st_(f"f1_xqT{i}", [128, D], BF16)) for i in range(2)]
        gst = [px.enter_context(st_(f"f1_gst{i}", [128, 512], F32)) for i in range(2)]
        mid = px.enter_context(st_("f1_mid", [128, 512], BF16))
        midT = [px.enter_context(st_(f"f1_midT{i}", [128, 44 * 128], BF16))
                for i in range(2)]
        wgs = [px.enter_context(st_(f"f1_wgs{i}", [128, 16 * 512], BF16))
               for i in range(2)]
        wus = [px.enter_context(st_(f"f1_wus{i}", [128, 16 * 512], BF16))
               for i in range(2)]
        wds = [px.enter_context(st_(f"f1_wds{i}", [128, 11 * 512], BF16))
               for i in range(2)]
        psT = px.enter_context(_psum(nc, "f1_psT", [128, 1024], BF16))
        for p in range(4):
            qA = qA0 if p % 2 == 0 else qA1
            if p >= 1:
                E.wait("y", qO)
            for mi in range(2):
                m = 2 * p + mi
                E.opi("y", lambda e, mi=mi, m=m: e.dma_start(
                    out=xt[mi][:], in_=xsrc[128 * m:128 * (m + 1), :]), qA, 16)
            for mi in range(2):
                E.wait("v", qA)
                E.wait("s", qA)
                _ln_stats(b, xt[mi][:], sq, mi, quant=False)
                c0 = 16 * mi
                E.dr("v")
                E.opi("v", lambda e, mi=mi, c0=c0: e.tensor_scalar(
                    xqb[:], xt[mi][:], S[:, c0:c0 + 1], S[:, c0 + 1:c0 + 2],
                    ALU.subtract, ALU.mult), vE, 1)
                E.wait("t", vE)
                _transpose_128(b, lambda k: xqb[:, 128 * k:128 * (k + 1)],
                               16, psT, xqT[mi], 0)
            with ExitStack() as pf:
                psG = [pf.enter_context(_psum(nc, f"f1_psG{i}", [128, 512], F32))
                       for i in range(2)]
                psU = [pf.enter_context(_psum(nc, f"f1_psU{i}", [128, 512], F32))
                       for i in range(2)]
                tEh, vEh, sEh = {}, {}, {}
                tEh[-2] = tEh[-1] = getattr(b, "f1_prev_gu_last", 0)
                for n in range(11):
                    q = qW0 if n % 2 == 0 else qW1
                    qg = qX0 if n % 2 == 0 else qX1
                    if tEh[n - 2]:
                        E.wait("y", tE, tEh[n - 2])
                    for hh in range(2):
                        E.opi("y", lambda e, n=n, hh=hh: e.dma_start(
                            out=wgs[n % 2][:, 4096 * hh:4096 * (hh + 1)]
                            .rearrange("p (c h) -> p c h", h=512),
                            in_=dr["gb"].ap()[8 * hh:8 * (hh + 1), :,
                                              512 * n:512 * (n + 1)]
                            .transpose([1, 0, 2])), q, 16)
                        E.opi("y", lambda e, n=n, hh=hh: e.dma_start(
                            out=wus[n % 2][:, 4096 * hh:4096 * (hh + 1)]
                            .rearrange("p (c h) -> p c h", h=512),
                            in_=dr["ub"].ap()[8 * hh:8 * (hh + 1), :,
                                              512 * n:512 * (n + 1)]
                            .transpose([1, 0, 2])), q, 16)
                    E.wait("t", q)
                    if n >= 1:
                        E.wait("t", sE, sEh[n - 1])
                        E.wait("t", vE, vEh[n - 1])
                    for mi in range(2):
                        for k in range(16):
                            E.opi("t", lambda e, mi=mi, k=k, n=n: e.matmul(
                                psG[mi][:], xqT[mi][:, 128 * k:128 * (k + 1)],
                                wgs[n % 2][:, 512 * k:512 * (k + 1)],
                                start=(k == 0), stop=(k == 15)),
                                tE, 1 if k == 15 else 0)
                        for k in range(16):
                            E.opi("t", lambda e, mi=mi, k=k, n=n: e.matmul(
                                psU[mi][:], xqT[mi][:, 128 * k:128 * (k + 1)],
                                wus[n % 2][:, 512 * k:512 * (k + 1)],
                                start=(k == 0), stop=(k == 15)),
                                tE, 1 if k == 15 else 0)
                    tEh[n] = tE.cnt
                    E.wait("s", tE)
                    for mi in range(2):
                        E.opi("s", lambda e, mi=mi: e.activation(
                            gst[mi][:], psG[mi][:], AF.Silu), sE, 1)
                    sEh[n] = sE.cnt
                    E.wait("v", sE)
                    for mi in range(2):
                        E.opi("v", lambda e, mi=mi: e.tensor_tensor(
                            mid[:], gst[mi][:], psU[mi][:], ALU.mult), vE, 1)
                        E.wait("t", vE)
                        nch = 4 if n < 10 else 4
                        _transpose_128(b, lambda k: mid[:, 128 * k:128 * (k + 1)],
                                       4, psT, midT[mi], 512 * n)
                    vEh[n] = vE.cnt
            b.f1_prev_gu_last = E.sem("tE").cnt
            with ExitStack() as pf:
                psD = [pf.enter_context(_psum(nc, f"f1_psD{i}", [128, 512], F32))
                       for i in range(4)]
                visit = 0
                tEh2 = {}
                tEh2[-2] = tEh2[-1] = getattr(b, "f1_prev_dn_last", 0)
                for half in range(2):
                    for kdq in range(4):
                        nk = 11
                        for nd2 in range(2):
                            nd = 2 * half + nd2
                            q = qW0 if visit % 2 == 0 else qW1
                            if tEh2[visit - 2]:
                                E.wait("y", tE, tEh2[visit - 2])
                            E.opi("y", lambda e, kdq=kdq, nd=nd, v2=visit:
                                  e.dma_start(
                                      out=wds[v2 % 2][:, :11 * 512]
                                      .rearrange("p (c d) -> p c d", d=512),
                                      in_=dr["db"].ap()[11 * kdq:11 * (kdq + 1),
                                                        :, 512 * nd:512 * (nd + 1)]
                                      .transpose([1, 0, 2])), q, 16)
                            E.wait("t", q)
                            if kdq == 0:
                                E.wait("t", vE)
                            for mi in range(2):
                                bank = psD[2 * mi + nd2]
                                for k in range(11):
                                    kk = 11 * kdq + k
                                    E.opi("t", lambda e, mi=mi, k=k, kk=kk, v2=visit,
                                          bank=bank: e.matmul(
                                              bank[:],
                                              midT[mi][:, 128 * kk:128 * (kk + 1)],
                                              wds[v2 % 2][:, 512 * k:512 * (k + 1)],
                                              start=(kk == 0), stop=(kk == 43)),
                                          tE, 1 if k == 10 else 0)
                            tEh2[visit] = tE.cnt
                            visit += 1
                    E.wait("v", tE)
                    for mi in range(2):
                        for nd2 in range(2):
                            nd = 2 * half + nd2
                            bank = psD[2 * mi + nd2]
                            E.opi("v", lambda e, mi=mi, nd=nd, bank=bank:
                                  e.scalar_tensor_tensor(
                                      xt[mi][:, 512 * nd:512 * (nd + 1)],
                                      bank[:], 0.5,
                                      xt[mi][:, 512 * nd:512 * (nd + 1)],
                                      ALU.mult, ALU.add), vE, 1)
            b.f1_prev_dn_last = E.sem("tE").cnt
            E.wait("g", vE)
            for mi in range(2):
                m = 2 * p + mi
                E.opi("g", lambda e, mi=mi, m=m: e.dma_start(
                    out=xdst[128 * m:128 * (m + 1), :], in_=xt[mi][:]), qO, 16)
        E.join()
        E.flush_block("ffn1")



def _lattn(b, src, dst, final=False):
    nc, E, io, dr, g, st_ = b.nc, b.E, b.io, b.dr, b.g, b.st
    qW0, qW1 = E.sem("qW0"), E.sem("qW1")
    qX0, qX1 = E.sem("qX0"), E.sem("qX1")
    qA0, qA1 = E.sem("qA0"), E.sem("qA1")
    qS, qO, qG = E.sem("qS"), E.sem("qO"), E.sem("qG")
    tE, vE, sE, cc = E.sem("tE"), E.sem("vE"), E.sem("sE"), E.sem("cc")
    xsrc = dr[src].ap()
    xdst = io["y"] if final else dr[dst].ap()
    S = g["stat"]

    with ExitStack() as px:
        xt1 = px.enter_context(st_("la_xt", [128, D], F32))
        sq = px.enter_context(st_("la_sq", [128, D], F32))
        xqb = px.enter_context(st_("la_xqb", [128, D], BF16))
        xlnT = px.enter_context(st_("la_xlnT", [128, 8 * D], BF16))
        mr = px.enter_context(st_("la_mr", [128, 16], F32))
        psT = px.enter_context(_psum(nc, "la_psT", [128, 1024], BF16))
        for m in range(8):
            qA = qA0 if m % 2 == 0 else qA1
            if m >= 1:
                E.wait("y", vE)
            E.opi("y", lambda e, m=m: e.dma_start(
                out=xt1[:], in_=xsrc[128 * m:128 * (m + 1), :]), qA, 16)
            E.wait("v", qA)
            E.wait("s", qA)
            _ln_stats(b, xt1[:], sq, 0, quant=False)
            E.dr("v")
            E.opi("v", lambda e, m=m: e.tensor_scalar(
                xqb[:], xt1[:], S[:, 0:1], S[:, 1:2],
                ALU.subtract, ALU.mult), vE, 1)
            E.wait("t", vE)
            _transpose_128(b, lambda k: xqb[:, 128 * k:128 * (k + 1)],
                           16, psT, xlnT, D * m)
        with ExitStack() as pf:
            wsl = [pf.enter_context(st_(f"la_w{i}", [128, 16 * 512], BF16))
                   for i in range(2)]
            brow = pf.enter_context(st_("la_brow", [1, 3 * D], BF16))
            stg = [pf.enter_context(st_(f"la_stg{i}", [128, 512], BF16))
                   for i in range(4)]
            psP = [pf.enter_context(_psum(nc, f"la_psP{i}", [128, 512], F32))
                   for i in range(4)]
            E.opi("g", lambda e: e.dma_start(out=brow[:], in_=io["bqkv"]), qG, 16)
            tEh = {}
            vEh = {}
            qO_hist = {}
            for n in range(12):
                q = qX0 if n % 2 == 0 else qX1
                if n >= 2:
                    E.wait("g", tE, tEh[n - 2])
                for hh in range(2):
                    E.opi("g", lambda e, n=n, hh=hh: e.dma_start(
                        out=wsl[n % 2][:, 4096 * hh:4096 * (hh + 1)]
                        .rearrange("p (c o) -> p c o", o=512),
                        in_=io["wqkv"].rearrange("(c p) o -> p c o", p=128)
                        [:, 8 * hh:8 * (hh + 1), 512 * n:512 * (n + 1)]), q, 16)
                E.wait("t", q)
                E.wait("t", qG)
                for mh in range(2):
                    E.wait("t", vE)
                    for m4 in range(4):
                        m = 4 * mh + m4
                        for k in range(16):
                            E.opi("t", lambda e, m=m, k=k, n=n, m4=m4: e.matmul(
                                psP[m4][:],
                                xlnT[:, D * m + 128 * k:D * m + 128 * (k + 1)],
                                wsl[n % 2][:, 512 * k:512 * (k + 1)],
                                start=(k == 0), stop=False), tE, 0)
                        E.opi("t", lambda e, n=n, m4=m4: e.matmul(
                            psP[m4][:], g["ones_rbf"][0:1, :],
                            brow[0:1, 512 * n:512 * (n + 1)],
                            start=False, stop=True), tE, 1)
                    tEh[n] = tE.cnt
                    E.wait("v", tE)
                    for m4 in range(4):
                        m = 4 * mh + m4
                        si = m4
                        if (n, mh) > (0, 0):
                            E.wait("v", qO, qO_hist.get((n, mh, m4), 0))
                        E.opi("v", lambda e, m4=m4, si=si: e.tensor_copy(
                            stg[si][:], psP[m4][:]), vE, 1)
                        E.wait("g", vE)
                        E.opi("g", lambda e, m=m, n=n, si=si: e.dma_start(
                            out=dr["qkv0"].ap()[128 * m:128 * (m + 1),
                                                512 * n:512 * (n + 1)],
                            in_=stg[si][:]), qO, 16)
                        nxt = (n, mh, m4) if mh == 1 else (n, mh + 1, m4)
                        nxt = (n + 1, 0, m4) if mh == 1 else (n, 1, m4)
                        qO_hist[nxt] = qO.cnt
        E.join()
        E.flush_block("la_proj")

    with ExitStack() as pxo:
      oT = pxo.enter_context(st_("la_oT", [128, 16 * 1024], BF16))
      with ExitStack() as px:
          scol = px.enter_context(st_("la_scol", [128, 16], F32))
          nscol = px.enter_context(st_("la_nscol", [128, 16], F32))
          bdcol = px.enter_context(st_("la_bdcol", [128, 16], F32))
          kdcol = px.enter_context(st_("la_kdcol", [128, 16], F32))
          nsdiag = px.enter_context(st_("la_nsdiag", [16, 1], F32))
          k64c = px.enter_context(st_("la_k64c", [128, 1], F32))
          Dts = px.enter_context(st_("la_Dts", [64, 64], F32))
          Pcs = px.enter_context(st_("la_Pcs", [16, 8], F32))
          W16 = px.enter_context(st_("la_W16", [16, 8], F32))
          Wcols = px.enter_context(st_("la_Wcols", [128, 128], F32))
          decT = px.enter_context(st_("la_decT", [64, 16 * 64], BF16))
          rowq = px.enter_context(st_("la_rowq", [1, TC], F32))
          g["row"] = px.enter_context(st_("la_row", [1, TC], F32))
          g["row2"] = px.enter_context(st_("la_row2", [1, TC], F32))
          qd64 = px.enter_context(st_("la_qd64", [128, TC], F32))
          qgdec = px.enter_context(st_("la_qgdec", [128, TC], F32))
          slrow = px.enter_context(st_("la_slrow", [1, 16], F32))
          q_h = px.enter_context(st_("la_qh", [128, 1024], BF16))
          k_h = px.enter_context(st_("la_kh", [128, 1024], BF16))
          k64 = px.enter_context(st_("la_k64", [64, 2048], BF16))
          v64 = px.enter_context(st_("la_v64", [64, 2048], BF16))
          qT = px.enter_context(st_("la_qT", [128, 1024], BF16))
          kT = px.enter_context(st_("la_kT", [128, 1024], BF16))
          qd = px.enter_context(st_("la_qd", [128, 1024], BF16))
          qgdT = px.enter_context(st_("la_qgdT", [128, 16 * 1024], BF16))
          kd_st = px.enter_context(st_("la_kdst", [64, 128], BF16))
          PT = px.enter_context(st_("la_PT", [64, 64], BF16))
          Ssb = px.enter_context(st_("la_Ssb", [128, 128], F32))
          Sbf = px.enter_context(st_("la_Sbf", [128, 128], BF16))
          psT = px.enter_context(_psum(nc, "la_psT2", [128, 1024], BF16))
          psA = [px.enter_context(_psum(nc, f"la_psA{i}", [64, 64], F32))
                 for i in range(2)]
          psS = px.enter_context(_psum(nc, "la_psS", [128, 128], F32))
          psO = [px.enter_context(_psum(nc, f"la_psO{i}", [128, 64], F32))
                 for i in range(2)]
          psB = px.enter_context(_psum(nc, "la_psB", [128, 512], F32))
          E.opi("y", lambda e: e.dma_start(out=slrow[:], in_=io["slopes"]), qS, 16)
          E.opi("y", lambda e: e.dma_start(out=k64c[:], in_=io["k64m"]), qS, 16)
          E.opi("y", lambda e: e.dma_start(out=Dts[:], in_=io["Dt"]), qS, 16)
          E.opi("y", lambda e: e.dma_start(out=Pcs[:], in_=io["Pc"]), qS, 16)
          E.opi("y", lambda e: e.dma_start(out=g["row"][0:1, :TC],
                                           in_=io["iota_m64"]), qS, 16)
          E.opi("y", lambda e: e.dma_start(out=g["row2"][0:1, :TC],
                                           in_=io["iota_loc"]), qS, 16)
          E.wait("t", qS)
          E.opi("t", lambda e: e.matmul(psB[:, 0:16], g["ones_f"][0:1, :],
                                        slrow[0:1, :], start=True, stop=True), tE, 1)
          E.opi("t", lambda e: e.transpose(psS[0:16, 0:1], slrow[0:1, :],
                                           g["ident_f"][0:1, 0:1]), tE, 1)
          E.wait("v", tE)
          E.opi("v", lambda e: e.tensor_copy(scol[:], psB[:, 0:16]), vE, 1)
          E.dr("v")
          E.op("v", lambda e: e.tensor_scalar(nscol[:], scol[:], -1.0, None,
                                              ALU.mult))
          E.dr("v")
          E.opi("v", lambda e: e.tensor_scalar(nsdiag[:], psS[0:16, 0:1], -1.0,
                                               None, ALU.mult), vE, 1)
          E.dr("v")
          E.wait("s", vE)
          E.opi("s", lambda e: e.activation(bdcol[:], scol[:], AF.Exp,
                                            scale=-64.0), sE, 1)
          for h in range(NH):
              E.opi("s", lambda e, h=h: e.activation(
                  kdcol[:, h:h + 1], k64c[:], AF.Exp,
                  scale=nscol[:, h:h + 1]), sE, 1)
              E.opi("s", lambda e, h=h: e.activation(
                  decT[:, 64 * h:64 * (h + 1)], Dts[:], AF.Exp,
                  scale=nscol[0:64, h:h + 1]), sE, 1)
          E.opi("s", lambda e: e.activation(W16[:], Pcs[:], AF.Exp,
                                            scale=nsdiag[:]), sE, 1)
          E.wait("t", sE)
          for r in range(8):
              E.opi("t", lambda e, r=r: e.transpose(
                  psB[0:1, 16 * r:16 * (r + 1)], W16[:, r:r + 1],
                  g["ident_f"][0:16, 0:16]), tE, 1)
          E.wait("v", tE)
          E.opi("v", lambda e: e.tensor_copy(rowq[0:1, 0:128], psB[0:1, 0:128]),
                vE, 1)
          E.wait("t", vE)
          E.opi("t", lambda e: e.matmul(psB[:, 0:128], g["ones_f"][0:1, :],
                                        rowq[0:1, 0:128], start=True, stop=True),
                tE, 1)
          E.wait("v", tE)
          E.opi("v", lambda e: e.tensor_copy(Wcols[:], psB[:, 0:128]), vE, 1)
          E.join()

          ev_seed = {-2: 0, -1: 0}
          for h in range(NH):
              qX = qX0 if h % 2 == 0 else qX1
              if h >= 1:
                  E.wait("g", tE)
                  E.wait("g", vE)
              E.opi("g", lambda e, h=h: e.dma_start(
                  out=q_h[:].rearrange("p (m c) -> p m c", c=128),
                  in_=dr["qkv0"].ap()[:, 128 * h:128 * (h + 1)]
                  .rearrange("(m p) c -> m p c", p=128).transpose([1, 0, 2])),
                  qX, 16)
              E.opi("g", lambda e, h=h: e.dma_start(
                  out=k_h[:].rearrange("p (m c) -> p m c", c=128),
                  in_=dr["qkv0"].ap()[:, D + 128 * h:D + 128 * (h + 1)]
                  .rearrange("(m p) c -> m p c", p=128).transpose([1, 0, 2])),
                  qX, 16)
              E.opi("g", lambda e, h=h: e.dma_start(
                  out=k64[:].rearrange("p (m c) -> p m c", c=128),
                  in_=dr["qkv0"].ap()[:, D + 128 * h:D + 128 * (h + 1)]
                  .rearrange("(m p) c -> m p c", p=64).transpose([1, 0, 2])),
                  qX, 16)
              E.opi("g", lambda e, h=h: e.dma_start(
                  out=v64[:].rearrange("p (m c) -> p m c", c=128),
                  in_=dr["qkv0"].ap()[:, 2 * D + 128 * h:2 * D + 128 * (h + 1)]
                  .rearrange("(m p) c -> m p c", p=64).transpose([1, 0, 2])),
                  qX, 16)
              E.wait("t", qX)
              _transpose_128(b, lambda k: q_h[:, 128 * k:128 * (k + 1)], 8, psT, qT, 0)
              _transpose_128(b, lambda k: k_h[:, 128 * k:128 * (k + 1)], 8, psT, kT, 0)
              E.wait("s", tE)
              E.opi("s", lambda e, h=h: e.activation(
                  rowq[:], g["row"][0:1, :TC], AF.Exp,
                  scale=nscol[0:1, h:h + 1]), sE, 1)
              E.wait("t", sE)
              _bcast(b, psB, lambda j, ww: rowq[0:1, 512 * j:512 * j + ww], qd64, TC)
              E.wait("s", tE)
              E.opi("s", lambda e, h=h: e.activation(
                  rowq[:], g["row2"][0:1, :TC], AF.Exp,
                  scale=nscol[0:1, h:h + 1]), sE, 1)
              E.wait("t", sE)
              _bcast(b, psB, lambda j, ww: rowq[0:1, 512 * j:512 * j + ww], qgdec, TC)
              E.wait("v", tE)
              E.dr("v")
              E.opi("v", lambda e: e.tensor_tensor(qd[:], qT[:], qd64[:], ALU.mult),
                    vE, 1)
              E.opi("v", lambda e, h=h: e.tensor_tensor(
                  qgdT[:, 1024 * h:1024 * (h + 1)], qT[:], qgdec[:], ALU.mult),
                  vE, 1)
              E.wait("v", E.sem("qO"))
              E.op("v", lambda e: e.memset(Ssb[:], 0.0))
              E.opi("v", lambda e: e.memset(Sbf[:], 0.0), vE, 1)
              E.wait("t", vE)
              tA, tMMb, tPsS = {}, {}, {}
              vPT, vKD, vStt = {}, {}, {}
              sCopy, sE_evac = {}, dict(ev_seed)

              def emitA(c_):
                  E.opi("t", lambda e, c_=c_, pa=psA[c_ % 2]: e.matmul(
                      pa[:], kT[:, 64 * c_:64 * (c_ + 1)],
                      qT[:, 64 * c_:64 * (c_ + 1)], start=True, stop=True), tE, 1)
                  tA[c_] = tE.cnt

              emitA(0)
              emitA(1)
              for c_ in range(16):
                  pa = psA[c_ % 2]
                  po = psO[c_ % 2]
                  E.wait("v", tE, tPsS.get(c_ - 1, tA[c_]))
                  E.opi("v", lambda e, h=h, pa=pa: e.tensor_tensor(
                      PT[:], pa[:], decT[:, 64 * h:64 * (h + 1)], ALU.mult), vE, 1)
                  vPT[c_] = vE.cnt
                  E.opi("v", lambda e, c_=c_, h=h: e.tensor_scalar(
                      kd_st[:], k64[:, 128 * c_:128 * (c_ + 1)],
                      kdcol[0:64, h:h + 1], None, ALU.mult), vE, 1)
                  vKD[c_] = vE.cnt
                  E.wait("t", sE, sE_evac[c_ - 2])
                  if c_ >= 1:
                      E.wait("t", sE, sCopy[c_ - 1])
                  E.opi("t", lambda e, c_=c_, po=po: e.matmul(
                      po[:], Sbf[:], qd[:, 64 * c_:64 * (c_ + 1)],
                      start=True, stop=False), tE, 1)
                  E.wait("t", vE, vPT[c_])
                  E.opi("t", lambda e, c_=c_, po=po: e.matmul(
                      po[:], v64[:, 128 * c_:128 * (c_ + 1)], PT[:],
                      start=False, stop=True), tE, 1)
                  tMMb[c_] = tE.cnt
                  if c_ >= 1:
                      E.wait("t", vE, vStt[c_ - 1])
                  E.opi("t", lambda e, c_=c_: e.matmul(
                      psS[:], kd_st[:], v64[:, 128 * c_:128 * (c_ + 1)],
                      start=True, stop=True), tE, 1)
                  tPsS[c_] = tE.cnt
                  if c_ + 2 < 16:
                      emitA(c_ + 2)
                  E.wait("v", tE, tPsS[c_])
                  if c_ >= 1:
                      E.wait("v", sE, sCopy[c_ - 1])
                  E.opi("v", lambda e, h=h: e.scalar_tensor_tensor(
                      Ssb[:], Ssb[:], bdcol[:, h:h + 1], psS[:],
                      ALU.mult, ALU.add), vE, 1)
                  vStt[c_] = vE.cnt
                  E.dr("v")
                  E.wait("s", tE, tMMb[c_])
                  E.opi("s", lambda e, c_=c_, h=h, po=po: e.activation(
                      oT[:, 1024 * h + 64 * c_:1024 * h + 64 * (c_ + 1)], po[:],
                      AF.Copy), sE, 1)
                  sE_evac[c_] = sE.cnt
                  E.wait("s", vE, vStt[c_])
                  E.opi("s", lambda e: e.activation(Sbf[:], Ssb[:], AF.Copy),
                        sE, 1)
                  sCopy[c_] = sE.cnt
              ev_seed = {-2: sE_evac[14], -1: sE_evac[15]}
              E.wait("g", vE)
              E.wait("g", sE)
              E.opi("g", lambda e, h=h: e.dma_start(
                  out=dr["Tc_sh"].ap()[h], in_=Ssb[:]), qO, 16)
          _ag(b, "Tc_sh", "Tc")

          with ExitStack() as pf:
              tc0 = pf.enter_context(st_("la_tc0", [128, 128], F32))
              tc1 = pf.enter_context(st_("la_tc1", [128, 128], F32))
              Sin = pf.enter_context(st_("la_Sin", [128, 128], F32))
              Sinb = pf.enter_context(st_("la_Sinb", [128, 128], BF16))
              E.wait("g", cc)
              for h in range(NH):
                  E.op("v", lambda e: e.memset(Sin[:], 0.0))
                  for r in range(8):
                      buf = tc0 if r % 2 == 0 else tc1
                      q = qX0 if r % 2 == 0 else qX1
                      if r >= 2:
                          E.wait("g", vE)
                      E.opi("g", lambda e, r=r, h=h, bu=buf: e.dma_start(
                          out=bu[:], in_=dr["Tc"].ap()[r, h]), q, 16)
                      E.wait("v", q)
                      E.opi("v", lambda e, r=r, h=h, bu=buf: e.scalar_tensor_tensor(
                          Sin[:], bu[:], Wcols[:, 16 * r + h:16 * r + h + 1],
                          Sin[:], ALU.mult, ALU.add), vE, 1)
                  E.opi("v", lambda e: e.tensor_copy(Sinb[:], Sin[:]), vE, 1)
                  E.wait("t", vE)
                  for j in range(2):
                      E.opi("t", lambda e, h=h, j=j: e.matmul(
                          psB[:], Sinb[:],
                          qgdT[:, 1024 * h + 512 * j:1024 * h + 512 * (j + 1)],
                          start=True, stop=True), tE, 1)
                      E.wait("v", tE)
                      E.opi("v", lambda e, h=h, j=j: e.tensor_tensor(
                          oT[:, 1024 * h + 512 * j:1024 * h + 512 * (j + 1)],
                          oT[:, 1024 * h + 512 * j:1024 * h + 512 * (j + 1)],
                          psB[:], ALU.add), vE, 1)
                      E.wait("t", vE)
          E.join()

      with ExitStack() as pf:
          xts = [pf.enter_context(st_(f"la_x{i}", [128, D], F32))
                 for i in range(8)]
          wsl = [pf.enter_context(st_(f"la_wo{i}", [128, 16 * 512], BF16))
                 for i in range(2)]
          brow = pf.enter_context(st_("la_bo", [1, D], BF16))
          psP = [pf.enter_context(_psum(nc, f"la_psQ{i}", [128, 512], F32))
                 for i in range(2)]
          E.opi("g", lambda e: e.dma_start(out=brow[:], in_=io["bout"]), qG, 16)
          for m in range(8):
              E.opi("y", lambda e, m=m: e.dma_start(
                  out=xts[m][:], in_=xsrc[128 * m:128 * (m + 1), :]), qA0, 16)
          tEh = {}
          for n in range(4):
              q = qX0 if n % 2 == 0 else qX1
              if n >= 2:
                  E.wait("g", tE, tEh[n - 2])
              E.opi("g", lambda e, n=n: e.dma_start(
                  out=wsl[n % 2][:, :].rearrange("p (c o) -> p c o", o=512),
                  in_=io["wout"].rearrange("(c p) o -> p c o", p=128)
                  [:, :, 512 * n:512 * (n + 1)]), q, 16)
              E.wait("t", q)
              E.wait("t", qG)
              E.wait("t", qA0)
              for m in range(8):
                  pb = psP[m % 2]
                  if m >= 2:
                      E.wait("t", vE)
                  for k in range(16):
                      E.opi("t", lambda e, m=m, k=k, n=n, pb=pb: e.matmul(
                          pb[:], oT[:, 1024 * k + 128 * m:1024 * k + 128 * (m + 1)],
                          wsl[n % 2][:, 512 * k:512 * (k + 1)],
                          start=(k == 0), stop=False), tE, 0)
                  E.opi("t", lambda e, n=n, pb=pb: e.matmul(
                      pb[:], g["ones_rbf"][0:1, :],
                      brow[0:1, 512 * n:512 * (n + 1)],
                      start=False, stop=True), tE, 1)
                  E.wait("v", tE)
                  E.opi("v", lambda e, m=m, n=n, pb=pb: e.tensor_tensor(
                      xts[m][:, 512 * n:512 * (n + 1)],
                      xts[m][:, 512 * n:512 * (n + 1)], pb[:], ALU.add),
                      vE, 1)
              tEh[n] = tE.cnt
          E.wait("g", vE)
          for m in range(8):
              E.opi("g", lambda e, m=m: e.dma_start(
                  out=xdst[128 * m:128 * (m + 1), :], in_=xts[m][:]), qO, 16)
      E.join()
      E.flush_block("la_main")



def _sattn(b, src, dst, final=False):
    nc, E, io, dr, g, st_ = b.nc, b.E, b.io, b.dr, b.g, b.st
    qW0, qW1 = E.sem("qW0"), E.sem("qW1")
    qX0, qX1 = E.sem("qX0"), E.sem("qX1")
    qA0, qA1 = E.sem("qA0"), E.sem("qA1")
    qS, qO, qG = E.sem("qS"), E.sem("qO"), E.sem("qG")
    tE, vE, sE, cc = E.sem("tE"), E.sem("vE"), E.sem("sE"), E.sem("cc")
    xsrc = dr[src].ap()
    xdst = io["y"] if final else dr[dst].ap()
    S = g["stat"]

    def emit_qslab(dst, mat, n, q):
        for c in range(2):
            E.opi("y", lambda e, mat=mat, n=n, c=c: e.dma_start(
                out=dst[:].rearrange("p (r c o) -> p r c o", c=2, o=512)
                [:, :, c, :],
                in_=dr["qkvob"].ap()[:, mat, c, :, 512 * n:512 * (n + 1)]
                .transpose([1, 0, 2])), q, 16)

    E.wait("y", E.sem("cc"), b.cc_marks["qkvob"])
    with ExitStack() as px:
        xt1 = px.enter_context(st_("sa_xt", [128, D], F32))
        sq = px.enter_context(st_("sa_sq", [128, D], F32))
        xqb = px.enter_context(st_("sa_xqb", [128, D], BF16))
        xlnT = px.enter_context(st_("sa_xlnT", [128, 8 * D], BF16))
        psT = px.enter_context(_psum(nc, "sa_psT", [128, 1024], BF16))
        for m in range(8):
            qA = qA0 if m % 2 == 0 else qA1
            if m >= 1:
                E.wait("y", vE)
            E.opi("y", lambda e, m=m: e.dma_start(
                out=xt1[:], in_=xsrc[128 * m:128 * (m + 1), :]), qA, 16)
            E.wait("v", qA)
            E.wait("s", qA)
            _ln_stats(b, xt1[:], sq, 0, quant=False)
            E.dr("v")
            E.opi("v", lambda e, m=m: e.tensor_scalar(
                xqb[:], xt1[:], S[:, 0:1], S[:, 1:2],
                ALU.subtract, ALU.mult), vE, 1)
            E.wait("t", vE)
            _transpose_128(b, lambda k: xqb[:, 128 * k:128 * (k + 1)],
                           16, psT, xlnT, D * m)
        coss = px.enter_context(st_("sa_cos", [128, 8 * 64], F32))
        sins = px.enter_context(st_("sa_sin", [128, 8 * 64], F32))
        E.opi("y", lambda e: e.dma_start(
            out=coss[:].rearrange("p (m f) -> p m f", f=64),
            in_=io["cos"].rearrange("(m p) f -> m p f", p=128)
            .transpose([1, 0, 2])), qS, 16)
        E.opi("y", lambda e: e.dma_start(
            out=sins[:].rearrange("p (m f) -> p m f", f=64),
            in_=io["sin"].rearrange("(m p) f -> m p f", p=128)
            .transpose([1, 0, 2])), qS, 16)

        with ExitStack() as pf:
            wsl = [pf.enter_context(st_(f"sa_w{i}", [128, 16 * 512], BF16))
                   for i in range(2)]
            stg = [pf.enter_context(st_(f"sa_stg{i}", [128, 512], BF16))
                   for i in range(4)]
            qk_sb = pf.enter_context(st_("sa_qk", [128, 8 * D], BF16))
            rt = [pf.enter_context(st_(f"sa_rt{i}", [128, 1024], F32))
                  for i in range(4)]
            kstg = [pf.enter_context(st_(f"sa_kstg{i}", [128, 16 * 128], BF16))
                    for i in range(2)]
            psP = [pf.enter_context(_psum(nc, f"sa_psP{i}", [128, 512], F32))
                   for i in range(4)]

            vq_hist = {}
            vreg = (dr["qkv1_sh"].ap()[2].rearrange("h a t -> h (a t)")
                    .rearrange("h (t c) -> h t c", c=128))
            for n in range(4):
                q = qW0 if n % 2 == 0 else qW1
                if n >= 2:
                    E.wait("y", tE)
                emit_qslab(wsl[n % 2], 2, n, q)
                E.wait("t", q)
                for mh in range(2):
                    E.wait("t", vE)
                    for m4 in range(4):
                        m = 4 * mh + m4
                        for k in range(16):
                            E.opi("t", lambda e, m=m, k=k, n=n, m4=m4: e.matmul(
                                psP[m4][:],
                                xlnT[:, D * m + 128 * k:D * m + 128 * (k + 1)],
                                wsl[n % 2][:, 512 * k:512 * (k + 1)],
                                start=(k == 0), stop=(k == 15)),
                                tE, 1 if k == 15 else 0)
                    E.wait("v", tE)
                    for m4 in range(4):
                        m = 4 * mh + m4
                        vi = 2 * n + mh
                        if vi >= 1:
                            E.wait("v", qO, vq_hist.get(vi - 1, 0))
                        E.opi("v", lambda e, m4=m4: e.tensor_copy(
                            stg[m4][:], psP[m4][:]), vE, 1)
                        E.wait("g", vE)
                        E.opi("g", lambda e, m=m, n=n, m4=m4: e.dma_start(
                            out=vreg[4 * n:4 * (n + 1), 128 * m:128 * (m + 1), :]
                            .transpose([1, 0, 2]),
                            in_=stg[m4][:].rearrange("p (h c) -> p h c", c=128)),
                            qO, 16)
                        vq_hist[vi] = qO.cnt
            kq_hist = {}
            for mat in (1, 0):
                for n in range(4):
                    q = qW0 if n % 2 == 0 else qW1
                    E.wait("y", tE)
                    emit_qslab(wsl[n % 2], mat, n, q)
                    E.wait("t", q)
                    for mh in range(2):
                        E.wait("t", vE)
                        for m4 in range(4):
                            m = 4 * mh + m4
                            for k in range(16):
                                E.opi("t", lambda e, m=m, k=k, n=n, m4=m4:
                                      e.matmul(
                                          psP[m4][:],
                                          xlnT[:, D * m + 128 * k:
                                               D * m + 128 * (k + 1)],
                                          wsl[n % 2][:, 512 * k:512 * (k + 1)],
                                          start=(k == 0), stop=(k == 15)),
                                      tE, 1 if k == 15 else 0)
                        E.wait("v", tE)
                        for m4 in range(4):
                            m = 4 * mh + m4
                            if mat == 0:
                                E.opi("v", lambda e, m=m, n=n, m4=m4:
                                      e.tensor_scalar(
                                          qk_sb[:, D * m + 512 * n:
                                                D * m + 512 * (n + 1)],
                                          psP[m4][:], QSCL, None, ALU.mult),
                                      vE, 1)
                            else:
                                E.opi("v", lambda e, m=m, n=n, m4=m4:
                                      e.tensor_copy(
                                          qk_sb[:, D * m + 512 * n:
                                                D * m + 512 * (n + 1)],
                                          psP[m4][:]), vE, 1)
                dst_mat = dr["qkv1_sh"].ap()[mat]
                for m in range(8):
                    base = D * m
                    k1 = qk_sb[:, base:base + D].rearrange(
                        "p (h c) -> p h c", c=128)[:, :, 0:64]
                    k2 = qk_sb[:, base:base + D].rearrange(
                        "p (h c) -> p h c", c=128)[:, :, 64:128]
                    cosb = (coss[:, 64 * m:64 * (m + 1)]
                            .rearrange("p (a f) -> p a f", a=1)
                            .broadcast_to([128, 16, 64]))
                    sinb = (sins[:, 64 * m:64 * (m + 1)]
                            .rearrange("p (a f) -> p a f", a=1)
                            .broadcast_to([128, 16, 64]))
                    rt3 = [r[:].rearrange("p (h f) -> p h f", f=64) for r in rt]
                    E.op("v", lambda e, k1=k1, cosb=cosb, r=rt3[0]:
                         e.tensor_tensor(r, k1, cosb, ALU.mult))
                    E.op("v", lambda e, k1=k1, sinb=sinb, r=rt3[1]:
                         e.tensor_tensor(r, k1, sinb, ALU.mult))
                    E.op("v", lambda e, k2=k2, sinb=sinb, r=rt3[2]:
                         e.tensor_tensor(r, k2, sinb, ALU.mult))
                    E.op("v", lambda e, k2=k2, cosb=cosb, r=rt3[3]:
                         e.tensor_tensor(r, k2, cosb, ALU.mult))
                    E.op("v", lambda e, k1=k1, a=rt3[0], c=rt3[2]:
                         e.tensor_tensor(k1, a, c, ALU.subtract))
                    E.opi("v", lambda e, k2=k2, a=rt3[1], c=rt3[3]:
                          e.tensor_tensor(k2, a, c, ALU.add), vE, 1)
                    E.wait("t", vE)
                    if m >= 1:
                        E.wait("v", qO, kq_hist.get((mat, m - 1), 0))
                    _transpose_128(b, lambda k, base=base:
                                   qk_sb[:, base + 128 * k:base + 128 * (k + 1)],
                                   16, psT, kstg[m % 2], 0)
                    E.wait("g", vE)
                    E.opi("g", lambda e, m=m, dm=dst_mat: e.dma_start(
                        out=dm[:, :, 128 * m:128 * (m + 1)].transpose([1, 0, 2]),
                        in_=kstg[m % 2][:].rearrange("p (h t) -> p h t", t=128)),
                        qO, 16)
                    kq_hist[(mat, m)] = qO.cnt
        _ag(b, "qkv1_sh", "qkv1")
        E.join()
        E.flush_block("sa_proj")

    from concourse.bass import ds
    with ExitStack() as px:
        kT = px.enter_context(st_("sa_kT", [128, 8192], BF16))
        vsb = px.enter_context(st_("sa_v", [128, 8192], BF16))
        qTf = px.enter_context(st_("sa_qTf", [128, 8192], BF16))
        PT = [px.enter_context(st_(f"sa_PT{i}", [128, 512], BF16))
              for i in range(3)]
        obuf = [px.enter_context(st_(f"sa_ob{i}", [128, 512], BF16))
                for i in range(2)]
        rlb = [px.enter_context(st_(f"sa_rlb{i}", [128, 512], F32))
               for i in range(2)]
        lrow = [px.enter_context(st_(f"sa_lrow{i}", [1, 512], F32))
                for i in range(2)]
        msk = px.enter_context(st_("sa_msk", [128, 4 * 512], BF16))
        psA = [px.enter_context(_psum(nc, f"sa_psA{i}", [128, 512], F32))
               for i in range(3)]
        psO = [px.enter_context(_psum(nc, f"sa_psO{i}", [128, 512], F32))
               for i in range(2)]
        psL = [px.enter_context(_psum(nc, f"sa_psL{i}", [1, 512], F32))
               for i in range(2)]
        psB = [px.enter_context(_psum(nc, "sa_psB", [128, 512], F32))
               for i in range(1)]
        E.wait("y", cc)
        E.opi("y", lambda e: e.dma_start(
            out=msk[:].rearrange("p (j q) -> p j q", q=512),
            in_=io["fmask"].transpose([1, 0, 2])), qS, 16)
        E.opi("v", lambda e: e.memset(S[:, 30:31], -SOFF), vE, 1)
        E.wait("s", vE)
        hpbox = {}

        def snap_heads(e):
            pid = nc.partition_id(engines=[e.engine])
            for j in range(2):
                hpbox[("hp", j)] = e.snap((pid * 2 + j) * 128,
                                          min_val=0, max_val=15 * 128)
                hpbox[("hf", j)] = e.snap((pid * 2 + j) * (128 * TC),
                                          min_val=0, max_val=15 * 128 * TC)
            return None
        E.op("y", snap_heads)
        for j in range(2):
            def load_kv(e, j=j):
                hp = hpbox[("hp", j)]
                hf = hpbox[("hf", j)]
                i1 = (dr["qkv1"].ap()[:, 1].rearrange("r h p t -> r (h p) t")
                      [:, ds(hp, 128), :].transpose([1, 0, 2]))
                e.dma_start(out=kT[:].rearrange("p (r t) -> p r t", t=TC),
                            in_=i1).then_inc(E.sems["qD0"].h, 16)
                i1q = (dr["qkv1"].ap()[:, 0].rearrange("r h p t -> r (h p) t")
                       [:, ds(hp, 128), :].transpose([1, 0, 2]))
                e.dma_start(out=qTf[:].rearrange("p (r t) -> p r t", t=TC),
                            in_=i1q).then_inc(E.sems["qD0"].h, 16)
                for r in range(8):
                    i2 = (dr["qkv1"].ap()[r, 2]
                          .rearrange("h a t -> (h a t)")[ds(hf, 128 * TC)]
                          .rearrange("(kt p c) -> kt p c", p=128, c=128)
                          .transpose([1, 0, 2]))
                    e.dma_start(
                        out=vsb[:, 1024 * r:1024 * (r + 1)]
                        .rearrange("p (kt c) -> p kt c", c=128),
                        in_=i2).then_inc(E.sems["qD0"].h, 16)
                return None
            if j >= 1:
                E.wait("y", tE)
            E.sems["qD0"].cnt += 16 * 10
            E.op("y", load_kv)
            E.wait("t", E.sems["qD0"])
            sE_qt_last = 0
            tE_qt_last = 0
            obuf_hist = {}
            for qt in range(16):
                nkt = 4 * qt + 4
                qb = qTf[:, 512 * qt:512 * (qt + 1)]
                po = psO[qt % 2]
                pl = psL[qt % 2]
                pb2 = psB[0]
                ob2 = obuf[qt % 2]
                rl2 = rlb[qt % 2]
                lr2 = lrow[qt % 2]
                sEh, tEh, aEh = {}, {}, {}

                def emitA(kt, qb=qb):
                    if kt >= 3:
                        E.wait("t", sE, sEh[kt - 3])
                    elif sE_qt_last:
                        E.wait("t", sE, sE_qt_last)
                    if kt == 0 and qt >= 2:
                        E.wait("t", vE, obuf_hist[qt - 2])
                    E.opi("t", lambda e, kt=kt, pa=psA[kt % 3], qb=qb: e.matmul(
                        pa[:], kT[:, 128 * kt:128 * (kt + 1)], qb,
                        start=True, stop=True), tE, 1)
                    aEh[kt] = tE.cnt

                def emitPV(kt, nkt=nkt, po=po, pl=pl):
                    pa = psA[kt % 3]
                    pt = PT[kt % 3]
                    if kt >= 3:
                        E.wait("s", tE, tEh[kt - 3])
                    elif tE_qt_last:
                        E.wait("s", tE, tE_qt_last)
                    E.wait("s", tE, aEh[kt])
                    E.opi("s", lambda e, pa=pa, pt=pt: e.activation(
                        pt[:], pa[:], AF.Exp, bias=S[:, 30:31]), sE, 1)
                    diag = kt - 4 * qt
                    if diag >= 0:
                        E.wait("v", sE)
                        E.opi("v", lambda e, pt=pt, diag=diag: e.tensor_tensor(
                            pt[:], pt[:], msk[:, 512 * diag:512 * (diag + 1)],
                            ALU.mult), vE, 1)
                        E.wait("t", vE)
                    else:
                        E.wait("t", sE)
                    sEh[kt] = sE.cnt
                    E.opi("t", lambda e, kt=kt, pt=pt, nkt=nkt, po=po: e.matmul(
                        po[:], vsb[:, 128 * kt:128 * (kt + 1)], pt[:],
                        start=(kt == 0), stop=(kt == nkt - 1)), tE, 1)
                    E.opi("t", lambda e, kt=kt, pt=pt, nkt=nkt, pl=pl: e.matmul(
                        pl[:], g["ones_bf"][:], pt[:],
                        start=(kt == 0), stop=(kt == nkt - 1)), tE, 1)
                    tEh[kt] = tE.cnt

                emitA(0)
                if nkt > 1:
                    emitA(1)
                for kt in range(nkt):
                    emitPV(kt)
                    if kt + 2 < nkt:
                        emitA(kt + 2)
                sE_qt_last = sE.cnt
                tE_qt_last = tE.cnt
                E.wait("v", tE)
                E.opi("v", lambda e, pl=pl, lr2=lr2: e.tensor_copy(
                    lr2[:], pl[:]), vE, 1)
                E.dr("v")
                E.opi("v", lambda e, lr2=lr2: e.reciprocal(lr2[:], lr2[:]),
                      vE, 1)
                E.dr("v")
                E.wait("t", vE)
                E.opi("t", lambda e, lr2=lr2, pb2=pb2: e.matmul(
                    pb2[:], g["ones_f"][0:1, :], lr2[:],
                    start=True, stop=True), tE, 1)
                E.wait("v", tE)
                E.opi("v", lambda e, pb2=pb2, rl2=rl2: e.tensor_copy(
                    rl2[:], pb2[:]), vE, 1)
                E.dr("v")
                if qt >= 2:
                    E.wait("v", qO)
                E.opi("v", lambda e, po=po, rl2=rl2, ob2=ob2: e.tensor_tensor(
                    ob2[:], po[:], rl2[:], ALU.mult), vE, 1)
                obuf_hist[qt] = vE.cnt
                E.wait("g", vE)
                E.opi("g", lambda e, j=j, qt=qt, ob2=ob2: e.dma_start(
                    out=dr["o1_sh"].ap()[j][:, 512 * qt:512 * (qt + 1)],
                    in_=ob2[:]), qO, 16)
        _ag(b, "o1_sh", "o1")
        E.join()
        E.flush_block("sa_flash")

    with ExitStack() as px:
        xts = [px.enter_context(st_(f"sa_x{i}", [128, D], F32)) for i in range(8)]
        oTa = px.enter_context(st_("sa_oTa", [128, 16 * 1024], BF16))
        wsl = [px.enter_context(st_(f"sa_wo{i}", [128, 16 * 512], BF16))
               for i in range(2)]
        psP = [px.enter_context(_psum(nc, f"sa_psQ{i}", [128, 512], F32))
               for i in range(2)]
        E.wait("g", cc)
        import os as _os
        if _os.environ.get("MK_DBG5"):
            E.join()
            E.opi("g", lambda e: e.dma_start(out=io["dbg0"],
                                             in_=dr["o1"].ap()[0, 0]),
                  E.sem("qO"), 16)
            E.opi("g", lambda e: e.dma_start(out=io["dbg2"],
                                             in_=dr["o1"].ap()[0, 1][:, 0:D]),
                  E.sem("qO"), 16)
            E.join()

        E.wait("s", cc)
        def load_oT(e):
            pid = nc.partition_id(engines=[e.engine])
            qoff = e.snap(pid * 1024, min_val=0, max_val=7 * 1024)
            i4 = (dr["o1"].ap().rearrange("r j p q -> (r j) p q")
                  [:, :, ds(qoff, 1024)].transpose([1, 0, 2]))
            e.dma_start(out=oTa[:].rearrange("p (h t) -> p h t", t=1024),
                        in_=i4).then_inc(E.sems["qD1"].h, 16)
            return None
        E.sems["qD1"].cnt += 16
        E.op("s", load_oT)
        for m in range(8):
            E.opi("y", lambda e, m=m: e.dma_start(
                out=xts[m][:], in_=xsrc[128 * m:128 * (m + 1), :]), qA0, 16)
        E.wait("t", E.sems["qD1"])
        E.wait("t", qA0)
        tEh = {}
        for n in range(4):
            q = qW0 if n % 2 == 0 else qW1
            if n >= 2:
                E.wait("y", tE, tEh[n - 2])
            emit_qslab(wsl[n % 2], 3, n, q)
            E.wait("t", q)
            for m in range(8):
                pb = psP[m % 2]
                if (n, m) >= (0, 2):
                    E.wait("t", vE)
                for k in range(16):
                    E.opi("t", lambda e, m=m, k=k, n=n, pb=pb: e.matmul(
                        pb[:], oTa[:, 1024 * k + 128 * m:1024 * k + 128 * (m + 1)],
                        wsl[n % 2][:, 512 * k:512 * (k + 1)],
                        start=(k == 0), stop=(k == 15)), tE,
                        1 if k == 15 else 0)
                E.wait("v", tE)
                E.opi("v", lambda e, m=m, n=n, pb=pb: e.tensor_tensor(
                    xts[m][:, 512 * n:512 * (n + 1)],
                    xts[m][:, 512 * n:512 * (n + 1)], pb[:], ALU.add), vE, 1)
            tEh[n] = tE.cnt
        E.wait("g", vE)
        for m in range(8):
            E.opi("g", lambda e, m=m: e.dma_start(
                out=xdst[128 * m:128 * (m + 1), :], in_=xts[m][:]), qO, 16)
        E.join()
        E.flush_block("sa_out")


def kernel(**inputs):
    maps = _shard_inputs(inputs)
    import os
    upto = int(os.environ.get("MK_UPTO", "6"))
    nc = build(upto=upto)
    res = run_bass_kernel_spmd(nc, maps, core_ids=list(range(NCORES)))
    out = np.concatenate([np.asarray(r["y"]) for r in res.results], axis=0)
    return out.reshape(1, 8192, D).astype(np.float32)


# version 10
# speedup vs baseline: 1.1217x; 1.0012x over previous
import numpy as np
import ml_dtypes
from contextlib import ExitStack
import concourse.bass as bass
import concourse.mybir as mybir
from concourse.bass_utils import run_bass_kernel_spmd

F32 = mybir.dt.float32
BF16 = mybir.dt.bfloat16
I32 = mybir.dt.int32
AF = mybir.ActivationFunctionType
ALU = mybir.AluOpType
AX = mybir.AxisListType
BF16NP = ml_dtypes.bfloat16

NCORES = 8
TC = 1024
D = 2048
H0 = 8192
H1 = 5632
NH = 16
HD = 128
MAGIC = 12582912.0
BIGNEG = -30000.0
SOFF = 32.0
LNEPS = 1e-5
QSCL = 0.08838834764831845

ENGS = ("t", "v", "s", "g", "y")
QUEUE_ENG = {"qS": "y", "qW0": "y", "qW1": "y", "qA0": "y", "qA1": "y",
             "qD0": "y", "qD1": "s",
             "qG": "g", "qX0": "g", "qX1": "g", "qO": "g"}


class Sem:
    __slots__ = ("h", "cnt", "name", "joined")

    def __init__(self, h, name):
        self.h, self.cnt, self.name, self.joined = h, 0, name, 0


class Emit:
    def __init__(self, nc, ctx):
        self.nc, self.ctx = nc, ctx
        self.bufs = {e: [] for e in ENGS}
        self.sems = {}

    def sem(self, name):
        if name not in self.sems:
            self.sems[name] = Sem(self.ctx.enter_context(self.nc.semaphore(name)),
                                  name)
        return self.sems[name]

    def op(self, eng, fn):
        self.bufs[eng].append(fn)

    def wait(self, eng, sem, cnt=None):
        n = sem.cnt if cnt is None else cnt
        if n > 0:
            self.bufs[eng].append(lambda e, h=sem.h, n=n: e.wait_ge(h, n))

    def dr(self, eng):
        self.bufs[eng].append(lambda e: e.drain())

    def opi(self, eng, fn, sem, k=1):
        if k == 0:
            self.bufs[eng].append(fn)
            return
        sem.cnt += k
        self.bufs[eng].append(lambda e, f=fn, h=sem.h, k=k: f(e).then_inc(h, k))

    def join(self):
        for qn, eng in QUEUE_ENG.items():
            if qn in self.sems:
                s = self.sems[qn]
                if s.cnt > s.joined:
                    self.wait(eng, s)
                    s.joined = s.cnt
        ph = self.sem("ph")
        ph.cnt += 5
        tgt = ph.cnt
        for e in ENGS:
            if e in ("t", "v", "s"):
                self.bufs[e].append(lambda eng: eng.drain())
            self.bufs[e].append(lambda eng, h=ph.h: eng.sem_inc(h, 1))
            self.bufs[e].append(lambda eng, h=ph.h, n=tgt: eng.wait_ge(h, n))

    _blk_n = 0

    def flush_block(self, name=None):
        Emit._blk_n += 1
        name = f"{name or 'blk'}_{Emit._blk_n}"
        bufs = self.bufs
        self.bufs = {e: [] for e in ENGS}
        with self.nc.Block(name) as blk:
            for key, reg in (("t", blk.tensor), ("v", blk.vector),
                             ("s", blk.scalar), ("g", blk.gpsimd),
                             ("y", blk.sync)):
                def body(e, ops=bufs[key]):
                    for f in ops:
                        f(e)
                reg(body)



def _consts_per_core(c):
    i_ = np.arange(64)
    Dt = (i_[None, :] - i_[:, None]).astype(np.float32)
    Dt[i_[None, :] < i_[:, None]] = 1e9
    Pc = np.full((16, 8), 1e9, np.float32)
    for r in range(c):
        Pc[:, r] = 1024.0 * (c - 1 - r)
    pos = (np.arange(TC, dtype=np.float64) + TC * c)
    half = HD // 2
    inv_freq = 1.0 / (10000.0 ** (np.arange(half, dtype=np.float64) / half))
    freqs = np.outer(pos, inv_freq)
    masks = np.zeros((4, 128, 512), np.float32)
    kk = np.arange(128)[:, None]
    qq = np.arange(512)[None, :]
    for j in range(4):
        masks[j] = np.where(128 * j + kk <= qq, 1.0, 0.0)
    masks = masks.astype(BF16NP)
    return dict(
        ident=np.eye(128, dtype=BF16NP),
        ident_f=np.eye(128, dtype=np.float32),
        ones_f=np.ones((1, 128), np.float32),
        ones_rbf=np.ones((1, 128), BF16NP),
        ones_bf=np.ones((128, 1), BF16NP),
        iota_loc=np.arange(TC, dtype=np.float32).reshape(1, TC),
        iota_m64=(np.arange(TC) % 64).astype(np.float32).reshape(1, TC),
        iota64=np.arange(64, dtype=np.float32).reshape(1, 64),
        k64m=(64.0 - (np.arange(128) % 64)).astype(np.float32).reshape(128, 1),
        Dt=Dt, Pc=Pc,
        cos=np.cos(freqs).astype(np.float32),
        sin=np.sin(freqs).astype(np.float32),
        fmask=masks,
    )


def _shard_inputs(inp):
    x = np.ascontiguousarray(np.asarray(inp["L0_attn_x"], np.float32).reshape(8192, D))
    w1 = np.asarray(inp["L0_ffn_w1"], np.float32)
    w2 = np.asarray(inp["L0_ffn_w2"], np.float32)
    w1T = np.ascontiguousarray(w1.T)
    w2T = np.ascontiguousarray(w2.T)
    qkvoT = np.stack([np.ascontiguousarray(np.asarray(inp[k], np.float32).T)
                      for k in ("L1_attn_w_q", "L1_attn_w_k",
                                "L1_attn_w_v", "L1_attn_w_o")])
    gT = np.ascontiguousarray(np.asarray(inp["L1_ffn_wq_gate"], np.int32).T)
    uT = np.ascontiguousarray(np.asarray(inp["L1_ffn_wq_up"], np.int32).T)
    dT_raw = np.ascontiguousarray(np.asarray(inp["L1_ffn_wq_down"], np.int32).T)
    dT = np.zeros((48 * 128, D), np.int32)
    dT[:H1] = dT_raw
    base = dict(
        wqkv=np.ascontiguousarray(np.asarray(inp["L0_attn_Wqkv"], np.float32)),
        bqkv=np.asarray(inp["L0_attn_bqkv"], np.float32).reshape(1, 3 * D),
        wout=np.ascontiguousarray(np.asarray(inp["L0_attn_Wout"], np.float32)),
        bout=np.asarray(inp["L0_attn_bout"], np.float32).reshape(1, D),
        slopes=np.asarray(inp["L0_attn_slopes"], np.float32).reshape(1, 16),
        sg=np.asarray(inp["L1_ffn_s_gate"], np.float32).reshape(1, H1),
        su=np.asarray(inp["L1_ffn_s_up"], np.float32).reshape(1, H1),
        sd=np.asarray(inp["L1_ffn_s_down"], np.float32).reshape(1, D),
    )
    maps = []
    for c in range(NCORES):
        m = dict(base)
        m.update(
            x0=np.ascontiguousarray(x[TC * c:TC * (c + 1)]),
            w1_rows=np.ascontiguousarray(w1[1024 * c:1024 * (c + 1)]),
            w1T_sh=np.ascontiguousarray(w1T[256 * c:256 * (c + 1)]),
            w2_rows=np.ascontiguousarray(w2[256 * c:256 * (c + 1)]),
            w2T_sh=np.ascontiguousarray(w2T[1024 * c:1024 * (c + 1)]),
            qkvoT_sh=np.ascontiguousarray(qkvoT[:, 256 * c:256 * (c + 1), :]),
            gT_sh=np.ascontiguousarray(gT[256 * c:256 * (c + 1)]),
            uT_sh=np.ascontiguousarray(uT[256 * c:256 * (c + 1)]),
            dT_sh=np.ascontiguousarray(dT[768 * c:768 * (c + 1)]),
        )
        m.update(_consts_per_core(c))
        maps.append(m)
    return maps



class B:
    pass


def _psum(nc, *a, **k):
    return nc.psum_tensor(*a, allow_name_mangling=True, **k)


def build(upto=6):
    nc = bass.Bass("TRN2", target_bir_lowering=False, debug=False,
                   detect_race_conditions=False)
    b = B()
    b.nc = nc
    dt = nc.dram_tensor

    def st_(*a, **k):
        return nc.sbuf_tensor(*a, allow_name_mangling=True, **k)
    b.st = st_
    io = {}
    io["x0"] = dt("x0", [TC, D], F32, kind="ExternalInput").ap()
    io["y"] = dt("y", [TC, D], F32, kind="ExternalOutput").ap()
    io["dbg0"] = dt("dbg0", [128, H0], F32, kind="ExternalOutput").ap()
    io["dbg1"] = dt("dbg1", [128, 64], F32, kind="ExternalOutput").ap()
    io["dbg2"] = dt("dbg2", [128, D], F32, kind="ExternalOutput").ap()
    for name, shape, dty in [
        ("w1_rows", [1024, D], F32), ("w1T_sh", [256, H0], F32),
        ("w2_rows", [256, H0], F32), ("w2T_sh", [1024, D], F32),
        ("wqkv", [D, 3 * D], F32), ("bqkv", [1, 3 * D], F32),
        ("wout", [D, D], F32), ("bout", [1, D], F32), ("slopes", [1, 16], F32),
        ("qkvoT_sh", [4, 256, D], F32),
        ("gT_sh", [256, H1], I32), ("uT_sh", [256, H1], I32),
        ("dT_sh", [768, D], I32),
        ("sg", [1, H1], F32), ("su", [1, H1], F32), ("sd", [1, D], F32),
        ("ident", [128, 128], BF16), ("ident_f", [128, 128], F32),
        ("ones_f", [1, 128], F32),
        ("ones_rbf", [1, 128], BF16), ("ones_bf", [128, 1], BF16),
        ("iota_loc", [1, TC], F32), ("iota_m64", [1, TC], F32),
        ("iota64", [1, 64], F32), ("k64m", [128, 1], F32),
        ("Dt", [64, 64], F32), ("Pc", [16, 8], F32),
        ("cos", [TC, 64], F32), ("sin", [TC, 64], F32),
        ("fmask", [4, 128, 512], BF16),
    ]:
        io[name] = dt(name, shape, dty, kind="ExternalInput").ap()
    b.io = io

    dr = {}
    dr["xa"] = dt("xa", [TC, D], F32)
    dr["xb"] = dt("xb", [TC, D], F32)
    dr["st_sh"] = dt("st_sh", [1, 1280], F32)
    dr["st_all"] = dt("st_all", [8, 1280], F32)
    dr["w1b_sh"] = dt("w1b_sh", [2, 128, H0], BF16)
    dr["w1b"] = dt("w1b", [16, 128, H0], BF16)
    dr["w2b_sh"] = dt("w2b_sh", [8, 128, D], BF16)
    dr["w2b"] = dt("w2b", [64, 128, D], BF16)
    dr["qkvob_sh"] = dt("qkvob_sh", [4, 2, 128, D], BF16)
    dr["qkvob"] = dt("qkvob", [8, 4, 2, 128, D], BF16)
    dr["gb_sh"] = dt("gb_sh", [2, 128, H1], BF16)
    dr["ub_sh"] = dt("ub_sh", [2, 128, H1], BF16)
    dr["db_sh"] = dt("db_sh", [6, 128, D], BF16)
    dr["gb"] = dt("gb", [16, 128, H1], BF16)
    dr["ub"] = dt("ub", [16, 128, H1], BF16)
    dr["db"] = dt("db", [48, 128, D], BF16)
    dr["qkv0"] = dt("qkv0", [TC, 3 * D], BF16)
    dr["Tc_sh"] = dt("Tc_sh", [16, 128, 128], F32)
    dr["Tc"] = dt("Tc", [8, 16, 128, 128], F32)
    dr["qkv1_sh"] = dt("qkv1_sh", [3, NH, 128, TC], BF16)
    dr["qkv1"] = dt("qkv1", [8, 3, NH, 128, TC], BF16)
    dr["o1_sh"] = dt("o1_sh", [2, 128, 8192], BF16)
    dr["o1"] = dt("o1", [8, 2, 128, 8192], BF16)
    b.dr = dr

    ctx = ExitStack()
    b.ctx = ctx
    E = Emit(nc, ctx)
    b.E = E
    g = {}
    g["ident"] = ctx.enter_context(st_("c_ident", [128, 128], BF16))
    g["ident_f"] = ctx.enter_context(st_("c_identf", [128, 128], F32))
    g["ones_f"] = ctx.enter_context(st_("c_ones_f", [1, 128], F32))
    g["ones_rbf"] = ctx.enter_context(st_("c_ones_rbf", [1, 128], BF16))
    g["ones_bf"] = ctx.enter_context(st_("c_ones_bf", [128, 1], BF16))
    g["stat"] = ctx.enter_context(st_("g_stat", [128, 64], F32))
    b.g = g

    for s in ("qS", "qW0", "qW1", "qA0", "qA1", "qD0", "qD1",
              "qG", "qX0", "qX1", "qO", "tE", "vE", "sE", "cc"):
        E.sem(s)

    qS = E.sem("qS")
    for name, dst in (("ident", g["ident"]), ("ident_f", g["ident_f"]),
                      ("ones_f", g["ones_f"]),
                      ("ones_rbf", g["ones_rbf"]), ("ones_bf", g["ones_bf"])):
        E.opi("y", lambda e, s=dst, i=io[name]: e.dma_start(out=s[:], in_=i), qS, 16)
    E.opi("y", lambda e: e.dma_start(out=dr["xa"].ap(), in_=io["x0"]), qS, 16)
    E.join()
    E.flush_block("p0")

    _prep(b)

    xc = "xa"
    seq = [("f0", "_ffn0"), ("la", "_lattn"), ("f0", "_ffn0"),
           ("f1", "_ffn1"), ("sa", "_sattn"), ("f1", "_ffn1")]
    import os as _os
    only = _os.environ.get("MK_ONLY")
    run_seq = [seq[int(only) - 1]] if only else seq[:upto]
    for i, (tag, fn) in enumerate(run_seq):
        phase_i = i + 1
        last = (phase_i == 6) and not only
        dst = None if last else ("xb" if xc == "xa" else "xa")
        globals()[fn](b, xc, dst, final=last)
        if dst is not None:
            xc = dst
    upto = 0 if only else upto
    if upto < 6:
        E.opi("y", lambda e, x=xc: e.dma_start(out=io["y"], in_=dr[x].ap()), qS, 16)
        E.join()
        E.flush_block("fincopy")
    ctx.close()
    return nc



def _bcast(b, psb, src_ap_fn, dst, w, f32=True):
    E, g = b.E, b.g
    tE, vE = E.sem("tE"), E.sem("vE")
    E.wait("t", vE)
    for j in range((w + 511) // 512):
        ww = min(512, w - 512 * j)
        E.opi("t", lambda e, j=j, ww=ww: e.matmul(
            psb[:, :ww], g["ones_f"][0:1, :], src_ap_fn(j, ww),
            start=True, stop=True), tE, 1)
        E.wait("v", tE)
        E.opi("v", lambda e, j=j, ww=ww: e.tensor_copy(
            dst[:, 512 * j:512 * j + ww], psb[:, :ww]), vE, 1)
        E.wait("t", vE)


def _ag(b, name_in, name_out):
    E, dr = b.E, b.dr
    cc, qG = E.sem("cc"), E.sem("qG")
    E.wait("g", qG)
    E.wait("g", E.sem("qO"))
    E.opi("g", lambda e: e.collective_compute(
        "AllGather", ALU.bypass, replica_groups=[list(range(NCORES))],
        ins=[dr[name_in].ap()], outs=[dr[name_out].ap()]), cc, 1)
    if not hasattr(b, "cc_marks"):
        b.cc_marks = {}
    b.cc_marks[name_out] = cc.cnt


def _ternary_quant_ops(b, wt_ap, rc, sc, tq, out_ap, w):
    E = b.E
    vE = E.sem("vE")
    E.wait("v", E.sem("qG"))
    E.op("v", lambda e: e.tensor_tensor(tq[:, :w], wt_ap, rc[:, :w], ALU.mult))
    E.op("v", lambda e: e.tensor_scalar(tq[:, :w], tq[:, :w], MAGIC, MAGIC,
                                        ALU.add, ALU.subtract))
    E.op("v", lambda e: e.tensor_scalar(tq[:, :w], tq[:, :w], 1.0, -1.0,
                                        ALU.min, ALU.max))
    E.opi("v", lambda e: e.tensor_tensor(out_ap, tq[:, :w], sc[:, :w], ALU.mult),
          vE, 1)



def _prep(b):
    import os
    pstop = int(os.environ.get("MK_PSTOP", "9"))
    nc, E, io, dr, g, st_ = b.nc, b.E, b.io, b.dr, b.g, b.st
    if pstop <= 0:
        return
    qS, qG = E.sem("qS"), E.sem("qG")
    qW0, qW1 = E.sem("qW0"), E.sem("qW1")
    qX0, qX1 = E.sem("qX0"), E.sem("qX1")
    tE, vE, sE, cc = E.sem("tE"), E.sem("vE"), E.sem("sE"), E.sem("cc")

    with ExitStack() as px:
        wrow = px.enter_context(st_("p_wrow", [128, H0], F32))
        wrow2 = px.enter_context(st_("p_wrow2", [128, H0], F32))
        stt = px.enter_context(st_("p_stats", [128, 10], F32))
        srow = px.enter_context(st_("p_srow", [1, 1280], F32))
        pT = px.enter_context(_psum(nc, "pp_T", [1, 512], F32))
        vh = []
        for j in range(10):
            buf = wrow if j % 2 == 0 else wrow2
            q = qW0 if j % 2 == 0 else qW1
            if j >= 2:
                E.wait("y", vE, vh[j - 2])
            if j < 8:
                E.opi("y", lambda e, bu=buf, j=j: e.dma_start(
                    out=bu[:, :D], in_=io["w1_rows"][128 * j:128 * (j + 1), :]),
                    q, 16)
                E.wait("v", q)
                E.opi("v", lambda e, bu=buf, j=j: e.tensor_reduce(
                    stt[:, j:j + 1], bu[:, :D], axis=AX.X, op=ALU.add,
                    apply_absolute_value=True), vE, 1)
            else:
                jj = j - 8
                E.opi("y", lambda e, bu=buf, jj=jj: e.dma_start(
                    out=bu[:, :], in_=io["w2_rows"][128 * jj:128 * (jj + 1), :]),
                    q, 16)
                E.wait("v", q)
                E.opi("v", lambda e, bu=buf, j=j: e.tensor_reduce(
                    stt[:, j:j + 1], bu[:, :], axis=AX.X, op=ALU.add,
                    apply_absolute_value=True), vE, 1)
            vh.append(vE.cnt)
        pamode = int(os.environ.get("MK_PAMODE", "4"))
        if pamode >= 2:
            E.wait("t", vE)
            for j0 in range(0, 10, 4):
                nb = min(4, 10 - j0)
                for k in range(nb):
                    E.opi("t", lambda e, j=j0 + k, s=k: e.transpose(
                        pT[0:1, 128 * s:128 * s + 128], stt[:, j:j + 1],
                        g["ident_f"][:]), tE, 1)
                E.wait("v", tE)
                E.opi("v", lambda e, j0=j0, nb=nb: e.tensor_copy(
                    srow[0:1, 128 * j0:128 * (j0 + nb)],
                    pT[0:1, 0:128 * nb]), vE, 1)
                E.wait("t", vE)
        if pamode >= 3:
            E.wait("g", vE)
            E.opi("g", lambda e: e.dma_start(out=dr["st_sh"].ap(), in_=srow[:]),
                  qG, 16)
        if pamode >= 4:
            _ag(b, "st_sh", "st_all")
        E.join()
        E.flush_block("prepA")
    if pstop <= 1:
        return

    with ExitStack() as px:
        wt = px.enter_context(st_("pb_wt", [128, H0], F32))
        wt2 = px.enter_context(st_("pb_wt2", [128, H0], F32))
        tq = px.enter_context(st_("pb_tq", [128, 2048], F32))
        ob = px.enter_context(st_("pb_ob", [128, H0], BF16))
        rc = px.enter_context(st_("pb_rc", [128, 2048], F32))
        sc = px.enter_context(st_("pb_sc", [128, 2048], F32))
        psb = px.enter_context(_psum(nc, "pp_b", [128, 512], F32))
        g["row"] = px.enter_context(st_("pb_row", [1, H0], F32))
        g["row2"] = px.enter_context(st_("pb_row2", [1, H0], F32))
        E.wait("y", cc)
        E.opi("y", lambda e: e.dma_start(
            out=g["row"][0:1, :H0].rearrange("a (r n) -> a r n", r=8),
            in_=dr["st_all"].ap()[:, 0:1024].rearrange("r n -> () r n")), qS, 16)
        E.wait("v", qS)
        E.op("v", lambda e: e.tensor_scalar(
            g["row"][0:1, :H0], g["row"][0:1, :H0], 1.0 / D, 1e-5,
            ALU.mult, ALU.max))
        E.dr("v")
        E.opi("v", lambda e: e.reciprocal(g["row2"][0:1, :H0], g["row"][0:1, :H0]),
              vE, 1)
        E.dr("v")
        for i in range(2):
            buf = wt if i % 2 == 0 else wt2
            q = qW0 if i % 2 == 0 else qW1
            E.opi("y", lambda e, bu=buf, i=i: e.dma_start(
                out=bu[:], in_=io["w1T_sh"][128 * i:128 * (i + 1), :]), q, 16)
            E.wait("v", q)
            for sl in range(4):
                lo = 2048 * sl
                _bcast(b, psb, lambda j, ww, lo=lo: g["row2"][0:1, lo + 512 * j:lo + 512 * j + ww],
                       rc, 2048)
                _bcast(b, psb, lambda j, ww, lo=lo: g["row"][0:1, lo + 512 * j:lo + 512 * j + ww],
                       sc, 2048)
                _ternary_quant_ops(b, buf[:, lo:lo + 2048], rc, sc, tq,
                                   ob[:, lo:lo + 2048], 2048)
            E.wait("g", vE)
            E.opi("g", lambda e, i=i: e.dma_start(
                out=dr["w1b_sh"].ap()[i], in_=ob[:]), qG, 16)
        _ag(b, "w1b_sh", "w1b")
        E.join()
        E.flush_block("prepB")
    if pstop <= 2:
        return

    with ExitStack() as px:
        wt = px.enter_context(st_("pc_wt", [128, D], F32))
        wt2 = px.enter_context(st_("pc_wt2", [128, D], F32))
        tq = px.enter_context(st_("pc_tq", [128, D], F32))
        ob = px.enter_context(st_("pc_ob", [128, D], BF16))
        rc = px.enter_context(st_("pc_rc", [128, D], F32))
        sc = px.enter_context(st_("pc_sc", [128, D], F32))
        psb = px.enter_context(_psum(nc, "pp_c", [128, 512], F32))
        g["row"] = px.enter_context(st_("pc_row", [1, D], F32))
        g["row2"] = px.enter_context(st_("pc_row2", [1, D], F32))
        E.opi("y", lambda e: e.dma_start(
            out=g["row"][0:1, :D].rearrange("a (r n) -> a r n", r=8),
            in_=dr["st_all"].ap()[:, 1024:1280].rearrange("r n -> () r n")), qS, 16)
        E.wait("v", qS)
        E.op("v", lambda e: e.tensor_scalar(
            g["row"][0:1, :D], g["row"][0:1, :D], 1.0 / H0, 1e-5,
            ALU.mult, ALU.max))
        E.dr("v")
        E.opi("v", lambda e: e.reciprocal(g["row2"][0:1, :D], g["row"][0:1, :D]),
              vE, 1)
        E.dr("v")
        _bcast(b, psb, lambda j, ww: g["row2"][0:1, 512 * j:512 * j + ww], rc, D)
        _bcast(b, psb, lambda j, ww: g["row"][0:1, 512 * j:512 * j + ww], sc, D)
        obh = []
        for i in range(8):
            buf = wt if i % 2 == 0 else wt2
            q = qW0 if i % 2 == 0 else qW1
            if i >= 1:
                E.wait("y", vE)
                if i >= 1:
                    E.wait("y", qG)
            E.opi("y", lambda e, bu=buf, i=i: e.dma_start(
                out=bu[:], in_=io["w2T_sh"][128 * i:128 * (i + 1), :]), q, 16)
            E.wait("v", q)
            _ternary_quant_ops(b, buf[:], rc, sc, tq, ob[:], D)
            E.wait("g", vE)
            E.opi("g", lambda e, i=i: e.dma_start(
                out=dr["w2b_sh"].ap()[i], in_=ob[:]), qG, 16)
        _ag(b, "w2b_sh", "w2b")
        E.join()
        E.flush_block("prepC")
    if pstop <= 3:
        return

    with ExitStack() as px:
        wt = px.enter_context(st_("pd_wt", [128, D], F32))
        awt = px.enter_context(st_("pd_awt", [128, D], F32))
        tq = px.enter_context(st_("pd_tq", [128, D], F32))
        ob = px.enter_context(st_("pd_ob", [128, D], BF16))
        rc = px.enter_context(st_("pd_rc", [128, D], F32))
        sc = px.enter_context(st_("pd_sc", [128, D], F32))
        psb = px.enter_context(_psum(nc, "pp_d", [128, 512], F32))
        pss = px.enter_context(_psum(nc, "pp_ds", [1, 2048], F32))
        onesc = px.enter_context(st_("pd_ones", [128, 1], F32))
        g["row"] = px.enter_context(st_("pd_row", [1, D], F32))
        g["row2"] = px.enter_context(st_("pd_row2", [1, D], F32))
        E.op("v", lambda e: e.memset(onesc[:], 1.0))
        for mat in range(4):
            for i in range(2):
                t_i = mat * 2 + i
                q = qW0 if t_i % 2 == 0 else qW1
                if t_i >= 1:
                    E.wait("y", vE)
                    E.wait("y", qG)
                E.opi("y", lambda e, mat=mat, i=i: e.dma_start(
                    out=wt[:], in_=io["qkvoT_sh"][mat, 128 * i:128 * (i + 1), :]),
                    q, 16)
                E.wait("s", q)
                E.opi("s", lambda e: e.activation(awt[:], wt[:], AF.Abs), sE, 1)
                E.wait("t", sE)
                E.wait("t", vE)
                for j in range(4):
                    E.opi("t", lambda e, j=j: e.matmul(
                        pss[0:1, 512 * j:512 * (j + 1)], onesc[:],
                        awt[:, 512 * j:512 * (j + 1)], start=True, stop=True),
                        tE, 1)
                E.wait("v", tE)
                E.op("v", lambda e: e.tensor_scalar(
                    g["row"][0:1, :D], pss[0:1, :], 1.0 / 128.0, 1e-8,
                    ALU.mult, ALU.max))
                E.dr("v")
                E.opi("v", lambda e: e.reciprocal(
                    g["row2"][0:1, :D], g["row"][0:1, :D]), vE, 1)
                E.dr("v")
                _bcast(b, psb, lambda j, ww: g["row2"][0:1, 512 * j:512 * j + ww],
                       rc, D)
                _bcast(b, psb, lambda j, ww: g["row"][0:1, 512 * j:512 * j + ww],
                       sc, D)
                _ternary_quant_ops(b, wt[:], rc, sc, tq, ob[:], D)
                E.wait("g", vE)
                E.opi("g", lambda e, mat=mat, i=i: e.dma_start(
                    out=dr["qkvob_sh"].ap()[mat, i], in_=ob[:]), qG, 16)
        _ag(b, "qkvob_sh", "qkvob")
        E.join()
        E.flush_block("prepD")
    if pstop <= 4:
        return

    with ExitStack() as px:
        wt = px.enter_context(st_("pe_wt", [128, H1], F32))
        tq = px.enter_context(st_("pe_tq", [128, H1], F32))
        ob = px.enter_context(st_("pe_ob", [128, H1], BF16))
        scl = px.enter_context(st_("pe_scl", [128, H1], F32))
        psb = px.enter_context(_psum(nc, "pp_e", [128, 512], F32))
        g["row"] = px.enter_context(st_("pe_row", [1, H1], F32))
        for srow_name, wsh, osh, width, nt in (
                ("sg", "gT_sh", "gb_sh", H1, 2), ("su", "uT_sh", "ub_sh", H1, 2),
                ("sd", "dT_sh", "db_sh", D, 6)):
            E.wait("y", tE)
            E.opi("y", lambda e, n=srow_name, w=width: e.dma_start(
                out=g["row"][0:1, :w], in_=io[n]), qS, 16)
            E.wait("t", qS)
            _bcast(b, psb, lambda j, ww: g["row"][0:1, 512 * j:512 * j + ww],
                   scl, width)
            for i in range(nt):
                q = qX0 if i % 2 == 0 else qX1
                if i >= 1:
                    E.wait("g", vE)
                    E.wait("g", qG)
                E.opi("g", lambda e, wsh=wsh, i=i, w=width: e.dma_start(
                    out=wt[:, :w], in_=io[wsh][128 * i:128 * (i + 1), :]), q, 16)
                E.wait("v", q)
                E.wait("v", qG)
                E.op("v", lambda e, w=width: e.tensor_scalar(
                    tq[:, :w], wt[:, :w], -128.0, 0.0, ALU.add, ALU.add))
                E.opi("v", lambda e, w=width: e.tensor_tensor(
                    ob[:, :w], tq[:, :w], scl[:, :w], ALU.mult), vE, 1)
                E.wait("g", vE)
                E.opi("g", lambda e, osh=osh, i=i: e.dma_start(
                    out=dr[osh].ap()[i], in_=ob[:, :dr[osh].shape[2]]), qG, 16)
        _ag(b, "gb_sh", "gb")
        _ag(b, "ub_sh", "ub")
        _ag(b, "db_sh", "db")
        E.join()
        E.flush_block("prepE")



def _ln_stats(b, xt_ap, sq, mi, quant):
    E, g = b.E, b.g
    vE, sE = E.sem("vE"), E.sem("sE")
    S = g["stat"]
    c0 = 16 * mi

    def col(j):
        return S[:, c0 + j:c0 + j + 1]
    E.wait("s", vE)
    E.opi("s", lambda e: e.activation(sq[:], xt_ap, AF.Square), sE, 1)
    E.op("v", lambda e: e.tensor_reduce(col(5), xt_ap, axis=AX.X, op=ALU.add))
    E.wait("v", sE)
    E.op("v", lambda e: e.tensor_reduce(col(6), sq[:], axis=AX.X, op=ALU.add))
    E.dr("v")
    E.op("v", lambda e: e.tensor_scalar(col(0), col(5), 1.0 / D, None, ALU.mult))
    E.op("v", lambda e: e.tensor_scalar(col(6), col(6), 1.0 / D, None, ALU.mult))
    E.dr("v")
    E.op("v", lambda e: e.tensor_tensor(col(7), col(0), col(0), ALU.mult))
    E.dr("v")
    E.op("v", lambda e: e.tensor_tensor(col(6), col(6), col(7), ALU.subtract))
    E.dr("v")
    E.opi("v", lambda e: e.tensor_scalar(col(6), col(6), 1.0, LNEPS,
                                         ALU.mult, ALU.add), vE, 1)
    E.dr("v")
    E.wait("s", vE)
    E.opi("s", lambda e: e.activation(col(7), col(6), AF.Sqrt), sE, 1)
    E.dr("s")
    E.wait("v", sE)
    E.opi("v", lambda e: e.reciprocal(col(1), col(7)), vE, 1)
    if quant:
        E.op("v", lambda e: e.tensor_reduce(col(5), xt_ap, axis=AX.X, op=ALU.max))
        E.op("v", lambda e: e.tensor_reduce(col(6), xt_ap, axis=AX.X, op=ALU.min))
        E.dr("v")
        E.op("v", lambda e: e.tensor_tensor(col(5), col(5), col(0), ALU.subtract))
        E.op("v", lambda e: e.tensor_tensor(col(6), col(0), col(6), ALU.subtract))
        E.dr("v")
        E.op("v", lambda e: e.tensor_tensor(col(5), col(5), col(6), ALU.max))
        E.dr("v")
        E.op("v", lambda e: e.tensor_tensor(col(5), col(5), col(1), ALU.mult))
        E.dr("v")
        E.op("v", lambda e: e.tensor_scalar(col(5), col(5), 1.0, 1e-5,
                                            ALU.mult, ALU.max))
        E.dr("v")
        E.op("v", lambda e: e.tensor_scalar(col(2), col(5), 1.0 / 127.0, None,
                                            ALU.mult))
        E.op("v", lambda e: e.reciprocal(col(6), col(5)))
        E.dr("v")
        E.op("v", lambda e: e.tensor_scalar(col(3), col(6), 127.0, None, ALU.mult))
        E.dr("v")
        E.opi("v", lambda e: e.tensor_tensor(col(4), col(1), col(3), ALU.mult),
              vE, 1)
        E.dr("v")


def _transpose_128(b, src_bf_ap_fn, nchunk, psT, dst, dst_off):
    E, g = b.E, b.g
    tE, vE = E.sem("tE"), E.sem("vE")
    for b0 in range(0, nchunk, 8):
        nb = min(8, nchunk - b0)
        for k in range(nb):
            E.opi("t", lambda e, k=k, b0=b0: e.transpose(
                psT[:, 128 * k:128 * (k + 1)], src_bf_ap_fn(b0 + k),
                g["ident"][:]), tE, 1)
        E.wait("v", tE)
        E.opi("v", lambda e, b0=b0, nb=nb: e.tensor_copy(
            dst[:, dst_off + 128 * b0:dst_off + 128 * (b0 + nb)],
            psT[:, :128 * nb]), vE, 1)
        E.wait("t", vE)



def _ffn0(b, src, dst, final=False):
    nc, E, io, dr, g, st_ = b.nc, b.E, b.io, b.dr, b.g, b.st
    qW0, qW1 = E.sem("qW0"), E.sem("qW1")
    qX0, qX1 = E.sem("qX0"), E.sem("qX1")
    qA0, qA1 = E.sem("qA0"), E.sem("qA1")
    qO = E.sem("qO")
    tE, vE, sE = E.sem("tE"), E.sem("vE"), E.sem("sE")
    xsrc = dr[src].ap()
    xdst = io["y"] if final else dr[dst].ap()
    S = g["stat"]

    E.wait("y", E.sem("cc"), b.cc_marks["w2b"])
    with ExitStack() as px:
        xt = [px.enter_context(st_(f"f0_xt{i}", [128, D], F32)) for i in range(2)]
        sq = px.enter_context(st_("f0_sq", [128, H0 // 4], F32))
        xqb = px.enter_context(st_("f0_xqb", [128, D], BF16))
        xqT = [px.enter_context(st_(f"f0_xqT{i}", [128, D], BF16)) for i in range(2)]
        hb = [px.enter_context(st_(f"f0_hb{i}", [128, H0], F32)) for i in range(2)]
        hqT = [px.enter_context(st_(f"f0_hqT{i}", [128, H0], BF16)) for i in range(2)]
        w1s = [px.enter_context(st_(f"f0_w1s{i}", [128, 16 * 512], BF16))
               for i in range(2)]
        w2s = [px.enter_context(st_(f"f0_w2s{i}", [128, 16 * 512], BF16))
               for i in range(2)]
        psT = px.enter_context(_psum(nc, "f0_psT", [128, 1024], BF16))
        qO_hist = []
        for p in range(4):
            qA = qA0 if p % 2 == 0 else qA1
            if p >= 1:
                E.wait("y", qO)
            for mi in range(2):
                m = 2 * p + mi
                E.opi("y", lambda e, mi=mi, m=m: e.dma_start(
                    out=xt[mi][:], in_=xsrc[128 * m:128 * (m + 1), :]), qA, 16)
            for mi in range(2):
                E.wait("v", qA)
                E.wait("s", qA)
                _ln_stats(b, xt[mi][:], sq, mi, quant=True)
                c0 = 16 * mi
                E.dr("v")
                E.op("v", lambda e, mi=mi, c0=c0: e.tensor_scalar(
                    sq[:, :D], xt[mi][:], S[:, c0:c0 + 1], S[:, c0 + 4:c0 + 5],
                    ALU.subtract, ALU.mult))
                E.opi("v", lambda e: e.tensor_scalar(
                    xqb[:], sq[:, :D], MAGIC, MAGIC, ALU.add, ALU.subtract),
                    vE, 1)
                E.wait("t", vE)
                _transpose_128(b, lambda k, mi=mi: xqb[:, 128 * k:128 * (k + 1)],
                               16, psT, xqT[mi], 0)
            with ExitStack() as pf:
                psF = [pf.enter_context(_psum(nc, f"f0_psF{i}", [128, 512], F32))
                       for i in range(4)]
                tEh, sEh = {}, {}
                tEh[-2] = tEh[-1] = b.f0_prev_ffn1_last if hasattr(b, "f0_prev_ffn1_last") else 0
                for mi in range(2):
                    E.op("v", lambda e, mi=mi: e.memset(
                        S[:, 16 * mi + 11:16 * mi + 12], 0.0))
                E.wait("s", vE)
                for n in range(16):
                    q = qW0 if n % 2 == 0 else qW1
                    if tEh[n - 2]:
                        E.wait("y", tE, tEh[n - 2])
                    for half in range(2):
                        E.opi("y", lambda e, n=n, half=half: e.dma_start(
                            out=w1s[n % 2][:, 4096 * half:4096 * (half + 1)]
                            .rearrange("p (c h) -> p c h", h=512),
                            in_=dr["w1b"].ap()[8 * half:8 * (half + 1), :,
                                               512 * n:512 * (n + 1)]
                            .transpose([1, 0, 2])), q, 16)
                    E.wait("t", q)
                    if n >= 2:
                        E.wait("t", sE, sEh[n - 2])
                    for mi in range(2):
                        bank = psF[2 * mi + n % 2]
                        for k in range(16):
                            E.opi("t", lambda e, mi=mi, k=k, n=n, bank=bank:
                                  e.matmul(bank[:],
                                           xqT[mi][:, 128 * k:128 * (k + 1)],
                                           w1s[n % 2][:, 512 * k:512 * (k + 1)],
                                           start=(k == 0), stop=(k == 15)),
                                  tE, 1 if k == 15 else 0)
                    tEh[n] = tE.cnt
                    E.wait("s", tE)
                    for mi in range(2):
                        c0 = 16 * mi
                        bank = psF[2 * mi + n % 2]
                        E.opi("s", lambda e, mi=mi, n=n, c0=c0, bank=bank:
                              e.activation(hb[mi][:, 512 * n:512 * (n + 1)],
                                           bank[:], AF.Gelu_apprx_tanh,
                                           scale=S[:, c0 + 2:c0 + 3]), sE, 1)
                    sEh[n] = sE.cnt
                    E.wait("v", sE)
                    for mi in range(2):
                        c0 = 16 * mi
                        E.op("v", lambda e, mi=mi, n=n, c0=c0: e.tensor_reduce(
                            S[:, c0 + 8:c0 + 9], hb[mi][:, 512 * n:512 * (n + 1)],
                            axis=AX.X, op=ALU.max, apply_absolute_value=True))
                        E.dr("v")
                        E.op("v", lambda e, c0=c0: e.tensor_tensor(
                            S[:, c0 + 11:c0 + 12], S[:, c0 + 11:c0 + 12],
                            S[:, c0 + 8:c0 + 9], ALU.max))
            b.f0_prev_ffn1_last = E.sem("tE").cnt
            import os as _os
            if p == 0 and _os.environ.get("MK_DBG"):
                E.join()
                E.opi("g", lambda e: e.dma_start(out=io["dbg0"], in_=hb[0][:]),
                      E.sem("qO"), 16)
                E.opi("g", lambda e: e.dma_start(out=io["dbg1"], in_=S[:]),
                      E.sem("qO"), 16)
                E.opi("g", lambda e: e.dma_start(out=io["dbg2"], in_=xqT[0][:]),
                      E.sem("qO"), 16)
                E.join()
            for mi in range(2):
                c0 = 16 * mi
                E.dr("v")
                E.op("v", lambda e, c0=c0: e.tensor_scalar(
                    S[:, c0 + 8:c0 + 9], S[:, c0 + 11:c0 + 12], 1.0, 1e-5,
                    ALU.mult, ALU.max))
                E.dr("v")
                E.op("v", lambda e, c0=c0: e.reciprocal(
                    S[:, c0 + 7:c0 + 8], S[:, c0 + 8:c0 + 9]))
                E.dr("v")
                E.op("v", lambda e, c0=c0: e.tensor_scalar(
                    S[:, c0 + 9:c0 + 10], S[:, c0 + 7:c0 + 8], 127.0, None,
                    ALU.mult))
                E.opi("v", lambda e, c0=c0: e.tensor_scalar(
                    S[:, c0 + 10:c0 + 11], S[:, c0 + 8:c0 + 9], 0.5 / 127.0,
                    None, ALU.mult), vE, 1)
                E.dr("v")
            for mi in range(2):
                c0 = 16 * mi
                for kq in range(4):
                    E.op("v", lambda e, mi=mi, kq=kq, c0=c0: e.tensor_scalar(
                        sq[:], hb[mi][:, 2048 * kq:2048 * (kq + 1)],
                        S[:, c0 + 9:c0 + 10], MAGIC, ALU.mult, ALU.add))
                    E.opi("v", lambda e: e.tensor_scalar(
                        xqb[:], sq[:], MAGIC, None, ALU.subtract), vE, 1)
                    E.wait("t", vE)
                    _transpose_128(b, lambda k: xqb[:, 128 * k:128 * (k + 1)],
                                   16, psT, hqT[mi], 2048 * kq)
            with ExitStack() as pf:
                psD = [pf.enter_context(_psum(nc, f"f0_psD{i}", [128, 512], F32))
                       for i in range(4)]
                visit = 0
                tEh2 = {}
                tEh2[-2] = tEh2[-1] = getattr(b, "f0_prev_ffn2_last", 0)
                for half in range(2):
                    for kq in range(4):
                        for nd2 in range(2):
                            nd = 2 * half + nd2
                            q = qW0 if visit % 2 == 0 else qW1
                            if tEh2[visit - 2]:
                                E.wait("y", tE, tEh2[visit - 2])
                            for hh in range(2):
                                E.opi("y", lambda e, kq=kq, nd=nd, hh=hh, v2=visit:
                                      e.dma_start(
                                          out=w2s[v2 % 2][:, 4096 * hh:4096 * (hh + 1)]
                                          .rearrange("p (c d) -> p c d", d=512),
                                          in_=dr["w2b"].ap()[16 * kq + 8 * hh:
                                                             16 * kq + 8 * (hh + 1),
                                                             :, 512 * nd:512 * (nd + 1)]
                                          .transpose([1, 0, 2])), q, 16)
                            E.wait("t", q)
                            if kq == 0:
                                E.wait("t", vE)
                            for mi in range(2):
                                bank = psD[2 * mi + nd2]
                                for k in range(16):
                                    E.opi("t", lambda e, mi=mi, k=k, kq=kq, v2=visit,
                                          bank=bank: e.matmul(
                                              bank[:],
                                              hqT[mi][:, 2048 * kq + 128 * k:
                                                      2048 * kq + 128 * (k + 1)],
                                              w2s[v2 % 2][:, 512 * k:512 * (k + 1)],
                                              start=(kq == 0 and k == 0),
                                              stop=(kq == 3 and k == 15)),
                                          tE, 1 if k == 15 else 0)
                            tEh2[visit] = tE.cnt
                            visit += 1
                    E.wait("v", tE)
                    for mi in range(2):
                        c0 = 16 * mi
                        for nd2 in range(2):
                            nd = 2 * half + nd2
                            bank = psD[2 * mi + nd2]
                            E.opi("v", lambda e, mi=mi, nd=nd, c0=c0, bank=bank:
                                  e.scalar_tensor_tensor(
                                      xt[mi][:, 512 * nd:512 * (nd + 1)],
                                      bank[:], S[:, c0 + 10:c0 + 11],
                                      xt[mi][:, 512 * nd:512 * (nd + 1)],
                                      ALU.mult, ALU.add), vE, 1)
            b.f0_prev_ffn2_last = E.sem("tE").cnt
            E.wait("g", vE)
            for mi in range(2):
                m = 2 * p + mi
                E.opi("g", lambda e, mi=mi, m=m: e.dma_start(
                    out=xdst[128 * m:128 * (m + 1), :], in_=xt[mi][:]), qO, 16)
            qO_hist.append(qO.cnt)
        E.join()
        E.flush_block("ffn0")



def _ffn1(b, src, dst, final=False):
    nc, E, io, dr, g, st_ = b.nc, b.E, b.io, b.dr, b.g, b.st
    qW0, qW1 = E.sem("qW0"), E.sem("qW1")
    qX0, qX1 = E.sem("qX0"), E.sem("qX1")
    qA0, qA1 = E.sem("qA0"), E.sem("qA1")
    qO = E.sem("qO")
    tE, vE, sE = E.sem("tE"), E.sem("vE"), E.sem("sE")
    xsrc = dr[src].ap()
    xdst = io["y"] if final else dr[dst].ap()
    S = g["stat"]

    E.wait("y", E.sem("cc"), b.cc_marks["db"])
    with ExitStack() as px:
        xt = [px.enter_context(st_(f"f1_xt{i}", [128, D], F32)) for i in range(2)]
        sq = px.enter_context(st_("f1_sq", [128, D], F32))
        xqb = px.enter_context(st_("f1_xqb", [128, D], BF16))
        xqT = [px.enter_context(st_(f"f1_xqT{i}", [128, D], BF16)) for i in range(2)]
        gst = [px.enter_context(st_(f"f1_gst{i}", [128, 512], F32)) for i in range(2)]
        mid = px.enter_context(st_("f1_mid", [128, 512], BF16))
        midT = [px.enter_context(st_(f"f1_midT{i}", [128, 44 * 128], BF16))
                for i in range(2)]
        wgs = [px.enter_context(st_(f"f1_wgs{i}", [128, 16 * 512], BF16))
               for i in range(2)]
        wus = [px.enter_context(st_(f"f1_wus{i}", [128, 16 * 512], BF16))
               for i in range(2)]
        wds = [px.enter_context(st_(f"f1_wds{i}", [128, 11 * 512], BF16))
               for i in range(2)]
        psT = px.enter_context(_psum(nc, "f1_psT", [128, 1024], BF16))
        for p in range(4):
            qA = qA0 if p % 2 == 0 else qA1
            if p >= 1:
                E.wait("y", qO)
            for mi in range(2):
                m = 2 * p + mi
                E.opi("y", lambda e, mi=mi, m=m: e.dma_start(
                    out=xt[mi][:], in_=xsrc[128 * m:128 * (m + 1), :]), qA, 16)
            for mi in range(2):
                E.wait("v", qA)
                E.wait("s", qA)
                _ln_stats(b, xt[mi][:], sq, mi, quant=False)
                c0 = 16 * mi
                E.dr("v")
                E.opi("v", lambda e, mi=mi, c0=c0: e.tensor_scalar(
                    xqb[:], xt[mi][:], S[:, c0:c0 + 1], S[:, c0 + 1:c0 + 2],
                    ALU.subtract, ALU.mult), vE, 1)
                E.wait("t", vE)
                _transpose_128(b, lambda k: xqb[:, 128 * k:128 * (k + 1)],
                               16, psT, xqT[mi], 0)
            with ExitStack() as pf:
                psG = [pf.enter_context(_psum(nc, f"f1_psG{i}", [128, 512], F32))
                       for i in range(2)]
                psU = [pf.enter_context(_psum(nc, f"f1_psU{i}", [128, 512], F32))
                       for i in range(2)]
                tEh, vEh, sEh = {}, {}, {}
                tEh[-2] = tEh[-1] = getattr(b, "f1_prev_gu_last", 0)
                for n in range(11):
                    q = qW0 if n % 2 == 0 else qW1
                    qg = qX0 if n % 2 == 0 else qX1
                    if tEh[n - 2]:
                        E.wait("y", tE, tEh[n - 2])
                    for hh in range(2):
                        E.opi("y", lambda e, n=n, hh=hh: e.dma_start(
                            out=wgs[n % 2][:, 4096 * hh:4096 * (hh + 1)]
                            .rearrange("p (c h) -> p c h", h=512),
                            in_=dr["gb"].ap()[8 * hh:8 * (hh + 1), :,
                                              512 * n:512 * (n + 1)]
                            .transpose([1, 0, 2])), q, 16)
                        E.opi("y", lambda e, n=n, hh=hh: e.dma_start(
                            out=wus[n % 2][:, 4096 * hh:4096 * (hh + 1)]
                            .rearrange("p (c h) -> p c h", h=512),
                            in_=dr["ub"].ap()[8 * hh:8 * (hh + 1), :,
                                              512 * n:512 * (n + 1)]
                            .transpose([1, 0, 2])), q, 16)
                    E.wait("t", q)
                    if n >= 1:
                        E.wait("t", sE, sEh[n - 1])
                        E.wait("t", vE, vEh[n - 1])
                    for mi in range(2):
                        for k in range(16):
                            E.opi("t", lambda e, mi=mi, k=k, n=n: e.matmul(
                                psG[mi][:], xqT[mi][:, 128 * k:128 * (k + 1)],
                                wgs[n % 2][:, 512 * k:512 * (k + 1)],
                                start=(k == 0), stop=(k == 15)),
                                tE, 1 if k == 15 else 0)
                        for k in range(16):
                            E.opi("t", lambda e, mi=mi, k=k, n=n: e.matmul(
                                psU[mi][:], xqT[mi][:, 128 * k:128 * (k + 1)],
                                wus[n % 2][:, 512 * k:512 * (k + 1)],
                                start=(k == 0), stop=(k == 15)),
                                tE, 1 if k == 15 else 0)
                    tEh[n] = tE.cnt
                    E.wait("s", tE)
                    for mi in range(2):
                        E.opi("s", lambda e, mi=mi: e.activation(
                            gst[mi][:], psG[mi][:], AF.Silu), sE, 1)
                    sEh[n] = sE.cnt
                    E.wait("v", sE)
                    for mi in range(2):
                        E.opi("v", lambda e, mi=mi: e.tensor_tensor(
                            mid[:], gst[mi][:], psU[mi][:], ALU.mult), vE, 1)
                        E.wait("t", vE)
                        nch = 4 if n < 10 else 4
                        _transpose_128(b, lambda k: mid[:, 128 * k:128 * (k + 1)],
                                       4, psT, midT[mi], 512 * n)
                    vEh[n] = vE.cnt
            b.f1_prev_gu_last = E.sem("tE").cnt
            with ExitStack() as pf:
                psD = [pf.enter_context(_psum(nc, f"f1_psD{i}", [128, 512], F32))
                       for i in range(4)]
                visit = 0
                tEh2 = {}
                tEh2[-2] = tEh2[-1] = getattr(b, "f1_prev_dn_last", 0)
                for half in range(2):
                    for kdq in range(4):
                        nk = 11
                        for nd2 in range(2):
                            nd = 2 * half + nd2
                            q = qW0 if visit % 2 == 0 else qW1
                            if tEh2[visit - 2]:
                                E.wait("y", tE, tEh2[visit - 2])
                            E.opi("y", lambda e, kdq=kdq, nd=nd, v2=visit:
                                  e.dma_start(
                                      out=wds[v2 % 2][:, :11 * 512]
                                      .rearrange("p (c d) -> p c d", d=512),
                                      in_=dr["db"].ap()[11 * kdq:11 * (kdq + 1),
                                                        :, 512 * nd:512 * (nd + 1)]
                                      .transpose([1, 0, 2])), q, 16)
                            E.wait("t", q)
                            if kdq == 0:
                                E.wait("t", vE)
                            for mi in range(2):
                                bank = psD[2 * mi + nd2]
                                for k in range(11):
                                    kk = 11 * kdq + k
                                    E.opi("t", lambda e, mi=mi, k=k, kk=kk, v2=visit,
                                          bank=bank: e.matmul(
                                              bank[:],
                                              midT[mi][:, 128 * kk:128 * (kk + 1)],
                                              wds[v2 % 2][:, 512 * k:512 * (k + 1)],
                                              start=(kk == 0), stop=(kk == 43)),
                                          tE, 1 if k == 10 else 0)
                            tEh2[visit] = tE.cnt
                            visit += 1
                    E.wait("v", tE)
                    for mi in range(2):
                        for nd2 in range(2):
                            nd = 2 * half + nd2
                            bank = psD[2 * mi + nd2]
                            E.opi("v", lambda e, mi=mi, nd=nd, bank=bank:
                                  e.scalar_tensor_tensor(
                                      xt[mi][:, 512 * nd:512 * (nd + 1)],
                                      bank[:], 0.5,
                                      xt[mi][:, 512 * nd:512 * (nd + 1)],
                                      ALU.mult, ALU.add), vE, 1)
            b.f1_prev_dn_last = E.sem("tE").cnt
            E.wait("g", vE)
            for mi in range(2):
                m = 2 * p + mi
                E.opi("g", lambda e, mi=mi, m=m: e.dma_start(
                    out=xdst[128 * m:128 * (m + 1), :], in_=xt[mi][:]), qO, 16)
        E.join()
        E.flush_block("ffn1")



def _lattn(b, src, dst, final=False):
    nc, E, io, dr, g, st_ = b.nc, b.E, b.io, b.dr, b.g, b.st
    qW0, qW1 = E.sem("qW0"), E.sem("qW1")
    qX0, qX1 = E.sem("qX0"), E.sem("qX1")
    qA0, qA1 = E.sem("qA0"), E.sem("qA1")
    qS, qO, qG = E.sem("qS"), E.sem("qO"), E.sem("qG")
    tE, vE, sE, cc = E.sem("tE"), E.sem("vE"), E.sem("sE"), E.sem("cc")
    xsrc = dr[src].ap()
    xdst = io["y"] if final else dr[dst].ap()
    S = g["stat"]

    with ExitStack() as px:
        xt1 = px.enter_context(st_("la_xt", [128, D], F32))
        sq = px.enter_context(st_("la_sq", [128, D], F32))
        xqb = px.enter_context(st_("la_xqb", [128, D], BF16))
        xlnT = px.enter_context(st_("la_xlnT", [128, 8 * D], BF16))
        mr = px.enter_context(st_("la_mr", [128, 16], F32))
        psT = px.enter_context(_psum(nc, "la_psT", [128, 1024], BF16))
        for m in range(8):
            qA = qA0 if m % 2 == 0 else qA1
            if m >= 1:
                E.wait("y", vE)
            E.opi("y", lambda e, m=m: e.dma_start(
                out=xt1[:], in_=xsrc[128 * m:128 * (m + 1), :]), qA, 16)
            E.wait("v", qA)
            E.wait("s", qA)
            _ln_stats(b, xt1[:], sq, 0, quant=False)
            E.dr("v")
            E.opi("v", lambda e, m=m: e.tensor_scalar(
                xqb[:], xt1[:], S[:, 0:1], S[:, 1:2],
                ALU.subtract, ALU.mult), vE, 1)
            E.wait("t", vE)
            _transpose_128(b, lambda k: xqb[:, 128 * k:128 * (k + 1)],
                           16, psT, xlnT, D * m)
        with ExitStack() as pf:
            wsl = [pf.enter_context(st_(f"la_w{i}", [128, 16 * 512], BF16))
                   for i in range(2)]
            brow = pf.enter_context(st_("la_brow", [1, 3 * D], BF16))
            stg = [pf.enter_context(st_(f"la_stg{i}", [128, 512], BF16))
                   for i in range(4)]
            psP = [pf.enter_context(_psum(nc, f"la_psP{i}", [128, 512], F32))
                   for i in range(4)]
            E.opi("g", lambda e: e.dma_start(out=brow[:], in_=io["bqkv"]), qG, 16)
            tEh = {}
            vEh = {}
            qO_hist = {}
            for n in range(12):
                q = qX0 if n % 2 == 0 else qX1
                if n >= 2:
                    E.wait("g", tE, tEh[n - 2])
                for hh in range(2):
                    E.opi("g", lambda e, n=n, hh=hh: e.dma_start(
                        out=wsl[n % 2][:, 4096 * hh:4096 * (hh + 1)]
                        .rearrange("p (c o) -> p c o", o=512),
                        in_=io["wqkv"].rearrange("(c p) o -> p c o", p=128)
                        [:, 8 * hh:8 * (hh + 1), 512 * n:512 * (n + 1)]), q, 16)
                E.wait("t", q)
                E.wait("t", qG)
                for mh in range(2):
                    E.wait("t", vE)
                    for m4 in range(4):
                        m = 4 * mh + m4
                        for k in range(16):
                            E.opi("t", lambda e, m=m, k=k, n=n, m4=m4: e.matmul(
                                psP[m4][:],
                                xlnT[:, D * m + 128 * k:D * m + 128 * (k + 1)],
                                wsl[n % 2][:, 512 * k:512 * (k + 1)],
                                start=(k == 0), stop=False), tE, 0)
                        E.opi("t", lambda e, n=n, m4=m4: e.matmul(
                            psP[m4][:], g["ones_rbf"][0:1, :],
                            brow[0:1, 512 * n:512 * (n + 1)],
                            start=False, stop=True), tE, 1)
                    tEh[n] = tE.cnt
                    E.wait("v", tE)
                    for m4 in range(4):
                        m = 4 * mh + m4
                        si = m4
                        if (n, mh) > (0, 0):
                            E.wait("v", qO, qO_hist.get((n, mh, m4), 0))
                        E.opi("v", lambda e, m4=m4, si=si: e.tensor_copy(
                            stg[si][:], psP[m4][:]), vE, 1)
                        E.wait("g", vE)
                        E.opi("g", lambda e, m=m, n=n, si=si: e.dma_start(
                            out=dr["qkv0"].ap()[128 * m:128 * (m + 1),
                                                512 * n:512 * (n + 1)],
                            in_=stg[si][:]), qO, 16)
                        nxt = (n, mh, m4) if mh == 1 else (n, mh + 1, m4)
                        nxt = (n + 1, 0, m4) if mh == 1 else (n, 1, m4)
                        qO_hist[nxt] = qO.cnt
        E.join()
        E.flush_block("la_proj")

    with ExitStack() as pxo:
      oT = pxo.enter_context(st_("la_oT", [128, 16 * 1024], BF16))
      with ExitStack() as px:
          scol = px.enter_context(st_("la_scol", [128, 16], F32))
          nscol = px.enter_context(st_("la_nscol", [128, 16], F32))
          bdcol = px.enter_context(st_("la_bdcol", [128, 16], F32))
          kdcol = px.enter_context(st_("la_kdcol", [128, 16], F32))
          nsdiag = px.enter_context(st_("la_nsdiag", [16, 1], F32))
          k64c = px.enter_context(st_("la_k64c", [128, 1], F32))
          Dts = px.enter_context(st_("la_Dts", [64, 64], F32))
          Pcs = px.enter_context(st_("la_Pcs", [16, 8], F32))
          W16 = px.enter_context(st_("la_W16", [16, 8], F32))
          Wcols = px.enter_context(st_("la_Wcols", [128, 128], F32))
          decT = px.enter_context(st_("la_decT", [64, 16 * 64], BF16))
          rowq = px.enter_context(st_("la_rowq", [1, TC], F32))
          g["row"] = px.enter_context(st_("la_row", [1, TC], F32))
          g["row2"] = px.enter_context(st_("la_row2", [1, TC], F32))
          qd64 = px.enter_context(st_("la_qd64", [128, TC], F32))
          qgdec = px.enter_context(st_("la_qgdec", [128, TC], F32))
          slrow = px.enter_context(st_("la_slrow", [1, 16], F32))
          q_h = px.enter_context(st_("la_qh", [128, 1024], BF16))
          k_h = px.enter_context(st_("la_kh", [128, 1024], BF16))
          k64 = px.enter_context(st_("la_k64", [64, 2048], BF16))
          v64 = px.enter_context(st_("la_v64", [64, 2048], BF16))
          qT = px.enter_context(st_("la_qT", [128, 1024], BF16))
          kT = px.enter_context(st_("la_kT", [128, 1024], BF16))
          qd = px.enter_context(st_("la_qd", [128, 1024], BF16))
          qgdT = px.enter_context(st_("la_qgdT", [128, 16 * 1024], BF16))
          kd_st = px.enter_context(st_("la_kdst", [64, 128], BF16))
          PT = px.enter_context(st_("la_PT", [64, 64], BF16))
          Ssb = px.enter_context(st_("la_Ssb", [128, 128], F32))
          Sbf = px.enter_context(st_("la_Sbf", [128, 128], BF16))
          psT = px.enter_context(_psum(nc, "la_psT2", [128, 1024], BF16))
          psA = [px.enter_context(_psum(nc, f"la_psA{i}", [64, 64], F32))
                 for i in range(2)]
          psS = px.enter_context(_psum(nc, "la_psS", [128, 128], F32))
          psO = [px.enter_context(_psum(nc, f"la_psO{i}", [128, 64], F32))
                 for i in range(2)]
          psB = px.enter_context(_psum(nc, "la_psB", [128, 512], F32))
          E.opi("y", lambda e: e.dma_start(out=slrow[:], in_=io["slopes"]), qS, 16)
          E.opi("y", lambda e: e.dma_start(out=k64c[:], in_=io["k64m"]), qS, 16)
          E.opi("y", lambda e: e.dma_start(out=Dts[:], in_=io["Dt"]), qS, 16)
          E.opi("y", lambda e: e.dma_start(out=Pcs[:], in_=io["Pc"]), qS, 16)
          E.opi("y", lambda e: e.dma_start(out=g["row"][0:1, :TC],
                                           in_=io["iota_m64"]), qS, 16)
          E.opi("y", lambda e: e.dma_start(out=g["row2"][0:1, :TC],
                                           in_=io["iota_loc"]), qS, 16)
          E.wait("t", qS)
          E.opi("t", lambda e: e.matmul(psB[:, 0:16], g["ones_f"][0:1, :],
                                        slrow[0:1, :], start=True, stop=True), tE, 1)
          E.opi("t", lambda e: e.transpose(psS[0:16, 0:1], slrow[0:1, :],
                                           g["ident_f"][0:1, 0:1]), tE, 1)
          E.wait("v", tE)
          E.opi("v", lambda e: e.tensor_copy(scol[:], psB[:, 0:16]), vE, 1)
          E.dr("v")
          E.op("v", lambda e: e.tensor_scalar(nscol[:], scol[:], -1.0, None,
                                              ALU.mult))
          E.dr("v")
          E.opi("v", lambda e: e.tensor_scalar(nsdiag[:], psS[0:16, 0:1], -1.0,
                                               None, ALU.mult), vE, 1)
          E.dr("v")
          E.wait("s", vE)
          E.opi("s", lambda e: e.activation(bdcol[:], scol[:], AF.Exp,
                                            scale=-64.0), sE, 1)
          for h in range(NH):
              E.opi("s", lambda e, h=h: e.activation(
                  kdcol[:, h:h + 1], k64c[:], AF.Exp,
                  scale=nscol[:, h:h + 1]), sE, 1)
              E.opi("s", lambda e, h=h: e.activation(
                  decT[:, 64 * h:64 * (h + 1)], Dts[:], AF.Exp,
                  scale=nscol[0:64, h:h + 1]), sE, 1)
          E.opi("s", lambda e: e.activation(W16[:], Pcs[:], AF.Exp,
                                            scale=nsdiag[:]), sE, 1)
          E.wait("t", sE)
          for r in range(8):
              E.opi("t", lambda e, r=r: e.transpose(
                  psB[0:1, 16 * r:16 * (r + 1)], W16[:, r:r + 1],
                  g["ident_f"][0:16, 0:16]), tE, 1)
          E.wait("v", tE)
          E.opi("v", lambda e: e.tensor_copy(rowq[0:1, 0:128], psB[0:1, 0:128]),
                vE, 1)
          E.wait("t", vE)
          E.opi("t", lambda e: e.matmul(psB[:, 0:128], g["ones_f"][0:1, :],
                                        rowq[0:1, 0:128], start=True, stop=True),
                tE, 1)
          E.wait("v", tE)
          E.opi("v", lambda e: e.tensor_copy(Wcols[:], psB[:, 0:128]), vE, 1)
          E.join()

          ev_seed = {-2: 0, -1: 0}
          for h in range(NH):
              qX = qX0 if h % 2 == 0 else qX1
              if h >= 1:
                  E.wait("g", tE)
                  E.wait("g", vE)
              E.opi("g", lambda e, h=h: e.dma_start(
                  out=q_h[:].rearrange("p (m c) -> p m c", c=128),
                  in_=dr["qkv0"].ap()[:, 128 * h:128 * (h + 1)]
                  .rearrange("(m p) c -> m p c", p=128).transpose([1, 0, 2])),
                  qX, 16)
              E.opi("g", lambda e, h=h: e.dma_start(
                  out=k_h[:].rearrange("p (m c) -> p m c", c=128),
                  in_=dr["qkv0"].ap()[:, D + 128 * h:D + 128 * (h + 1)]
                  .rearrange("(m p) c -> m p c", p=128).transpose([1, 0, 2])),
                  qX, 16)
              E.opi("g", lambda e, h=h: e.dma_start(
                  out=k64[:].rearrange("p (m c) -> p m c", c=128),
                  in_=dr["qkv0"].ap()[:, D + 128 * h:D + 128 * (h + 1)]
                  .rearrange("(m p) c -> m p c", p=64).transpose([1, 0, 2])),
                  qX, 16)
              E.opi("g", lambda e, h=h: e.dma_start(
                  out=v64[:].rearrange("p (m c) -> p m c", c=128),
                  in_=dr["qkv0"].ap()[:, 2 * D + 128 * h:2 * D + 128 * (h + 1)]
                  .rearrange("(m p) c -> m p c", p=64).transpose([1, 0, 2])),
                  qX, 16)
              E.wait("t", qX)
              _transpose_128(b, lambda k: q_h[:, 128 * k:128 * (k + 1)], 8, psT, qT, 0)
              _transpose_128(b, lambda k: k_h[:, 128 * k:128 * (k + 1)], 8, psT, kT, 0)
              E.wait("s", tE)
              E.opi("s", lambda e, h=h: e.activation(
                  rowq[:], g["row"][0:1, :TC], AF.Exp,
                  scale=nscol[0:1, h:h + 1]), sE, 1)
              E.wait("t", sE)
              _bcast(b, psB, lambda j, ww: rowq[0:1, 512 * j:512 * j + ww], qd64, TC)
              E.wait("s", tE)
              E.opi("s", lambda e, h=h: e.activation(
                  rowq[:], g["row2"][0:1, :TC], AF.Exp,
                  scale=nscol[0:1, h:h + 1]), sE, 1)
              E.wait("t", sE)
              _bcast(b, psB, lambda j, ww: rowq[0:1, 512 * j:512 * j + ww], qgdec, TC)
              E.wait("v", tE)
              E.dr("v")
              E.opi("v", lambda e: e.tensor_tensor(qd[:], qT[:], qd64[:], ALU.mult),
                    vE, 1)
              E.opi("v", lambda e, h=h: e.tensor_tensor(
                  qgdT[:, 1024 * h:1024 * (h + 1)], qT[:], qgdec[:], ALU.mult),
                  vE, 1)
              E.wait("v", E.sem("qO"))
              E.op("v", lambda e: e.memset(Ssb[:], 0.0))
              E.opi("v", lambda e: e.memset(Sbf[:], 0.0), vE, 1)
              E.wait("t", vE)
              tA, tMMb, tPsS = {}, {}, {}
              vPT, vKD, vStt = {}, {}, {}
              sCopy, sE_evac = {}, dict(ev_seed)

              def emitA(c_):
                  E.opi("t", lambda e, c_=c_, pa=psA[c_ % 2]: e.matmul(
                      pa[:], kT[:, 64 * c_:64 * (c_ + 1)],
                      qT[:, 64 * c_:64 * (c_ + 1)], start=True, stop=True), tE, 1)
                  tA[c_] = tE.cnt

              emitA(0)
              emitA(1)
              for c_ in range(16):
                  pa = psA[c_ % 2]
                  po = psO[c_ % 2]
                  E.wait("v", tE, tPsS.get(c_ - 1, tA[c_]))
                  E.opi("v", lambda e, h=h, pa=pa: e.tensor_tensor(
                      PT[:], pa[:], decT[:, 64 * h:64 * (h + 1)], ALU.mult), vE, 1)
                  vPT[c_] = vE.cnt
                  E.opi("v", lambda e, c_=c_, h=h: e.tensor_scalar(
                      kd_st[:], k64[:, 128 * c_:128 * (c_ + 1)],
                      kdcol[0:64, h:h + 1], None, ALU.mult), vE, 1)
                  vKD[c_] = vE.cnt
                  E.wait("t", sE, sE_evac[c_ - 2])
                  if c_ >= 1:
                      E.wait("t", sE, sCopy[c_ - 1])
                  E.opi("t", lambda e, c_=c_, po=po: e.matmul(
                      po[:], Sbf[:], qd[:, 64 * c_:64 * (c_ + 1)],
                      start=True, stop=False), tE, 1)
                  E.wait("t", vE, vPT[c_])
                  E.opi("t", lambda e, c_=c_, po=po: e.matmul(
                      po[:], v64[:, 128 * c_:128 * (c_ + 1)], PT[:],
                      start=False, stop=True), tE, 1)
                  tMMb[c_] = tE.cnt
                  if c_ >= 1:
                      E.wait("t", vE, vStt[c_ - 1])
                  E.opi("t", lambda e, c_=c_: e.matmul(
                      psS[:], kd_st[:], v64[:, 128 * c_:128 * (c_ + 1)],
                      start=True, stop=True), tE, 1)
                  tPsS[c_] = tE.cnt
                  if c_ + 2 < 16:
                      emitA(c_ + 2)
                  E.wait("v", tE, tPsS[c_])
                  if c_ >= 1:
                      E.wait("v", sE, sCopy[c_ - 1])
                  E.opi("v", lambda e, h=h: e.scalar_tensor_tensor(
                      Ssb[:], Ssb[:], bdcol[:, h:h + 1], psS[:],
                      ALU.mult, ALU.add), vE, 1)
                  vStt[c_] = vE.cnt
                  E.dr("v")
                  E.wait("s", tE, tMMb[c_])
                  E.opi("s", lambda e, c_=c_, h=h, po=po: e.activation(
                      oT[:, 1024 * h + 64 * c_:1024 * h + 64 * (c_ + 1)], po[:],
                      AF.Copy), sE, 1)
                  sE_evac[c_] = sE.cnt
                  E.wait("s", vE, vStt[c_])
                  E.opi("s", lambda e: e.activation(Sbf[:], Ssb[:], AF.Copy),
                        sE, 1)
                  sCopy[c_] = sE.cnt
              ev_seed = {-2: sE_evac[14], -1: sE_evac[15]}
              E.wait("g", vE)
              E.wait("g", sE)
              E.opi("g", lambda e, h=h: e.dma_start(
                  out=dr["Tc_sh"].ap()[h], in_=Ssb[:]), qO, 16)
          _ag(b, "Tc_sh", "Tc")

          with ExitStack() as pf:
              tc0 = pf.enter_context(st_("la_tc0", [128, 128], F32))
              tc1 = pf.enter_context(st_("la_tc1", [128, 128], F32))
              Sin = pf.enter_context(st_("la_Sin", [128, 128], F32))
              Sinb = pf.enter_context(st_("la_Sinb", [128, 128], BF16))
              E.wait("g", cc)
              for h in range(NH):
                  E.op("v", lambda e: e.memset(Sin[:], 0.0))
                  for r in range(8):
                      buf = tc0 if r % 2 == 0 else tc1
                      q = qX0 if r % 2 == 0 else qX1
                      if r >= 2:
                          E.wait("g", vE)
                      E.opi("g", lambda e, r=r, h=h, bu=buf: e.dma_start(
                          out=bu[:], in_=dr["Tc"].ap()[r, h]), q, 16)
                      E.wait("v", q)
                      E.opi("v", lambda e, r=r, h=h, bu=buf: e.scalar_tensor_tensor(
                          Sin[:], bu[:], Wcols[:, 16 * r + h:16 * r + h + 1],
                          Sin[:], ALU.mult, ALU.add), vE, 1)
                  E.opi("v", lambda e: e.tensor_copy(Sinb[:], Sin[:]), vE, 1)
                  E.wait("t", vE)
                  for j in range(2):
                      E.opi("t", lambda e, h=h, j=j: e.matmul(
                          psB[:], Sinb[:],
                          qgdT[:, 1024 * h + 512 * j:1024 * h + 512 * (j + 1)],
                          start=True, stop=True), tE, 1)
                      E.wait("v", tE)
                      E.opi("v", lambda e, h=h, j=j: e.tensor_tensor(
                          oT[:, 1024 * h + 512 * j:1024 * h + 512 * (j + 1)],
                          oT[:, 1024 * h + 512 * j:1024 * h + 512 * (j + 1)],
                          psB[:], ALU.add), vE, 1)
                      E.wait("t", vE)
          E.join()

      with ExitStack() as pf:
          xts = [pf.enter_context(st_(f"la_x{i}", [128, D], F32))
                 for i in range(8)]
          wsl = [pf.enter_context(st_(f"la_wo{i}", [128, 16 * 512], BF16))
                 for i in range(2)]
          brow = pf.enter_context(st_("la_bo", [1, D], BF16))
          psP = [pf.enter_context(_psum(nc, f"la_psQ{i}", [128, 512], F32))
                 for i in range(2)]
          E.opi("g", lambda e: e.dma_start(out=brow[:], in_=io["bout"]), qG, 16)
          for m in range(8):
              E.opi("y", lambda e, m=m: e.dma_start(
                  out=xts[m][:], in_=xsrc[128 * m:128 * (m + 1), :]), qA0, 16)
          tEh = {}
          for n in range(4):
              q = qX0 if n % 2 == 0 else qX1
              if n >= 2:
                  E.wait("g", tE, tEh[n - 2])
              E.opi("g", lambda e, n=n: e.dma_start(
                  out=wsl[n % 2][:, :].rearrange("p (c o) -> p c o", o=512),
                  in_=io["wout"].rearrange("(c p) o -> p c o", p=128)
                  [:, :, 512 * n:512 * (n + 1)]), q, 16)
              E.wait("t", q)
              E.wait("t", qG)
              E.wait("t", qA0)
              for m in range(8):
                  pb = psP[m % 2]
                  if m >= 2:
                      E.wait("t", vE)
                  for k in range(16):
                      E.opi("t", lambda e, m=m, k=k, n=n, pb=pb: e.matmul(
                          pb[:], oT[:, 1024 * k + 128 * m:1024 * k + 128 * (m + 1)],
                          wsl[n % 2][:, 512 * k:512 * (k + 1)],
                          start=(k == 0), stop=False), tE, 0)
                  E.opi("t", lambda e, n=n, pb=pb: e.matmul(
                      pb[:], g["ones_rbf"][0:1, :],
                      brow[0:1, 512 * n:512 * (n + 1)],
                      start=False, stop=True), tE, 1)
                  E.wait("v", tE)
                  E.opi("v", lambda e, m=m, n=n, pb=pb: e.tensor_tensor(
                      xts[m][:, 512 * n:512 * (n + 1)],
                      xts[m][:, 512 * n:512 * (n + 1)], pb[:], ALU.add),
                      vE, 1)
              tEh[n] = tE.cnt
          E.wait("g", vE)
          for m in range(8):
              E.opi("g", lambda e, m=m: e.dma_start(
                  out=xdst[128 * m:128 * (m + 1), :], in_=xts[m][:]), qO, 16)
      E.join()
      E.flush_block("la_main")



def _sattn(b, src, dst, final=False):
    nc, E, io, dr, g, st_ = b.nc, b.E, b.io, b.dr, b.g, b.st
    qW0, qW1 = E.sem("qW0"), E.sem("qW1")
    qX0, qX1 = E.sem("qX0"), E.sem("qX1")
    qA0, qA1 = E.sem("qA0"), E.sem("qA1")
    qS, qO, qG = E.sem("qS"), E.sem("qO"), E.sem("qG")
    tE, vE, sE, cc = E.sem("tE"), E.sem("vE"), E.sem("sE"), E.sem("cc")
    xsrc = dr[src].ap()
    xdst = io["y"] if final else dr[dst].ap()
    S = g["stat"]

    def emit_qslab(dst, mat, n, q):
        for c in range(2):
            E.opi("y", lambda e, mat=mat, n=n, c=c: e.dma_start(
                out=dst[:].rearrange("p (r c o) -> p r c o", c=2, o=512)
                [:, :, c, :],
                in_=dr["qkvob"].ap()[:, mat, c, :, 512 * n:512 * (n + 1)]
                .transpose([1, 0, 2])), q, 16)

    E.wait("y", E.sem("cc"), b.cc_marks["qkvob"])
    with ExitStack() as px:
        xt1 = px.enter_context(st_("sa_xt", [128, D], F32))
        sq = px.enter_context(st_("sa_sq", [128, D], F32))
        xqb = px.enter_context(st_("sa_xqb", [128, D], BF16))
        xlnT = px.enter_context(st_("sa_xlnT", [128, 8 * D], BF16))
        psT = px.enter_context(_psum(nc, "sa_psT", [128, 1024], BF16))
        for m in range(8):
            qA = qA0 if m % 2 == 0 else qA1
            if m >= 1:
                E.wait("y", vE)
            E.opi("y", lambda e, m=m: e.dma_start(
                out=xt1[:], in_=xsrc[128 * m:128 * (m + 1), :]), qA, 16)
            E.wait("v", qA)
            E.wait("s", qA)
            _ln_stats(b, xt1[:], sq, 0, quant=False)
            E.dr("v")
            E.opi("v", lambda e, m=m: e.tensor_scalar(
                xqb[:], xt1[:], S[:, 0:1], S[:, 1:2],
                ALU.subtract, ALU.mult), vE, 1)
            E.wait("t", vE)
            _transpose_128(b, lambda k: xqb[:, 128 * k:128 * (k + 1)],
                           16, psT, xlnT, D * m)
        coss = px.enter_context(st_("sa_cos", [128, 8 * 64], F32))
        sins = px.enter_context(st_("sa_sin", [128, 8 * 64], F32))
        E.opi("y", lambda e: e.dma_start(
            out=coss[:].rearrange("p (m f) -> p m f", f=64),
            in_=io["cos"].rearrange("(m p) f -> m p f", p=128)
            .transpose([1, 0, 2])), qS, 16)
        E.opi("y", lambda e: e.dma_start(
            out=sins[:].rearrange("p (m f) -> p m f", f=64),
            in_=io["sin"].rearrange("(m p) f -> m p f", p=128)
            .transpose([1, 0, 2])), qS, 16)

        with ExitStack() as pf:
            wsl = [pf.enter_context(st_(f"sa_w{i}", [128, 16 * 512], BF16))
                   for i in range(2)]
            stg = [pf.enter_context(st_(f"sa_stg{i}", [128, 512], BF16))
                   for i in range(4)]
            qk_sb = pf.enter_context(st_("sa_qk", [128, 8 * D], BF16))
            rt = [pf.enter_context(st_(f"sa_rt{i}", [128, 1024], F32))
                  for i in range(4)]
            kstg = [pf.enter_context(st_(f"sa_kstg{i}", [128, 16 * 128], BF16))
                    for i in range(2)]
            psP = [pf.enter_context(_psum(nc, f"sa_psP{i}", [128, 512], F32))
                   for i in range(4)]

            vq_hist = {}
            vreg = (dr["qkv1_sh"].ap()[2].rearrange("h a t -> h (a t)")
                    .rearrange("h (t c) -> h t c", c=128))
            for n in range(4):
                q = qW0 if n % 2 == 0 else qW1
                if n >= 2:
                    E.wait("y", tE)
                emit_qslab(wsl[n % 2], 2, n, q)
                E.wait("t", q)
                for mh in range(2):
                    E.wait("t", vE)
                    for m4 in range(4):
                        m = 4 * mh + m4
                        for k in range(16):
                            E.opi("t", lambda e, m=m, k=k, n=n, m4=m4: e.matmul(
                                psP[m4][:],
                                xlnT[:, D * m + 128 * k:D * m + 128 * (k + 1)],
                                wsl[n % 2][:, 512 * k:512 * (k + 1)],
                                start=(k == 0), stop=(k == 15)),
                                tE, 1 if k == 15 else 0)
                    E.wait("v", tE)
                    for m4 in range(4):
                        m = 4 * mh + m4
                        vi = 2 * n + mh
                        if vi >= 1:
                            E.wait("v", qO, vq_hist.get(vi - 1, 0))
                        E.opi("v", lambda e, m4=m4: e.tensor_copy(
                            stg[m4][:], psP[m4][:]), vE, 1)
                        E.wait("g", vE)
                        E.opi("g", lambda e, m=m, n=n, m4=m4: e.dma_start(
                            out=vreg[4 * n:4 * (n + 1), 128 * m:128 * (m + 1), :]
                            .transpose([1, 0, 2]),
                            in_=stg[m4][:].rearrange("p (h c) -> p h c", c=128)),
                            qO, 16)
                        vq_hist[vi] = qO.cnt
            kq_hist = {}
            for mat in (1, 0):
                for n in range(4):
                    q = qW0 if n % 2 == 0 else qW1
                    E.wait("y", tE)
                    emit_qslab(wsl[n % 2], mat, n, q)
                    E.wait("t", q)
                    for mh in range(2):
                        E.wait("t", vE)
                        for m4 in range(4):
                            m = 4 * mh + m4
                            for k in range(16):
                                E.opi("t", lambda e, m=m, k=k, n=n, m4=m4:
                                      e.matmul(
                                          psP[m4][:],
                                          xlnT[:, D * m + 128 * k:
                                               D * m + 128 * (k + 1)],
                                          wsl[n % 2][:, 512 * k:512 * (k + 1)],
                                          start=(k == 0), stop=(k == 15)),
                                      tE, 1 if k == 15 else 0)
                        E.wait("v", tE)
                        for m4 in range(4):
                            m = 4 * mh + m4
                            if mat == 0:
                                E.opi("v", lambda e, m=m, n=n, m4=m4:
                                      e.tensor_scalar(
                                          qk_sb[:, D * m + 512 * n:
                                                D * m + 512 * (n + 1)],
                                          psP[m4][:], QSCL, None, ALU.mult),
                                      vE, 1)
                            else:
                                E.opi("v", lambda e, m=m, n=n, m4=m4:
                                      e.tensor_copy(
                                          qk_sb[:, D * m + 512 * n:
                                                D * m + 512 * (n + 1)],
                                          psP[m4][:]), vE, 1)
                dst_mat = dr["qkv1_sh"].ap()[mat]
                for m in range(8):
                    base = D * m
                    k1 = qk_sb[:, base:base + D].rearrange(
                        "p (h c) -> p h c", c=128)[:, :, 0:64]
                    k2 = qk_sb[:, base:base + D].rearrange(
                        "p (h c) -> p h c", c=128)[:, :, 64:128]
                    cosb = (coss[:, 64 * m:64 * (m + 1)]
                            .rearrange("p (a f) -> p a f", a=1)
                            .broadcast_to([128, 16, 64]))
                    sinb = (sins[:, 64 * m:64 * (m + 1)]
                            .rearrange("p (a f) -> p a f", a=1)
                            .broadcast_to([128, 16, 64]))
                    rt3 = [r[:].rearrange("p (h f) -> p h f", f=64) for r in rt]
                    E.op("v", lambda e, k1=k1, cosb=cosb, r=rt3[0]:
                         e.tensor_tensor(r, k1, cosb, ALU.mult))
                    E.op("v", lambda e, k1=k1, sinb=sinb, r=rt3[1]:
                         e.tensor_tensor(r, k1, sinb, ALU.mult))
                    E.op("v", lambda e, k2=k2, sinb=sinb, r=rt3[2]:
                         e.tensor_tensor(r, k2, sinb, ALU.mult))
                    E.op("v", lambda e, k2=k2, cosb=cosb, r=rt3[3]:
                         e.tensor_tensor(r, k2, cosb, ALU.mult))
                    E.op("v", lambda e, k1=k1, a=rt3[0], c=rt3[2]:
                         e.tensor_tensor(k1, a, c, ALU.subtract))
                    E.opi("v", lambda e, k2=k2, a=rt3[1], c=rt3[3]:
                          e.tensor_tensor(k2, a, c, ALU.add), vE, 1)
                    E.wait("t", vE)
                    if m >= 1:
                        E.wait("v", qO, kq_hist.get((mat, m - 1), 0))
                    _transpose_128(b, lambda k, base=base:
                                   qk_sb[:, base + 128 * k:base + 128 * (k + 1)],
                                   16, psT, kstg[m % 2], 0)
                    E.wait("g", vE)
                    E.opi("g", lambda e, m=m, dm=dst_mat: e.dma_start(
                        out=dm[:, :, 128 * m:128 * (m + 1)].transpose([1, 0, 2]),
                        in_=kstg[m % 2][:].rearrange("p (h t) -> p h t", t=128)),
                        qO, 16)
                    kq_hist[(mat, m)] = qO.cnt
        _ag(b, "qkv1_sh", "qkv1")
        E.join()
        E.flush_block("sa_proj")

    from concourse.bass import ds
    with ExitStack() as px:
        kT = px.enter_context(st_("sa_kT", [128, 8192], BF16))
        vsb = px.enter_context(st_("sa_v", [128, 8192], BF16))
        qTf = px.enter_context(st_("sa_qTf", [128, 8192], BF16))
        PT = [px.enter_context(st_(f"sa_PT{i}", [128, 512], BF16))
              for i in range(3)]
        Q2 = [px.enter_context(st_(f"sa_q2{i}", [128, 512], BF16))
              for i in range(2)]
        obuf = [px.enter_context(st_(f"sa_ob{i}", [128, 512], BF16))
                for i in range(2)]
        rlb = [px.enter_context(st_(f"sa_rlb{i}", [128, 512], F32))
               for i in range(2)]
        lrow = [px.enter_context(st_(f"sa_lrow{i}", [1, 512], F32))
                for i in range(2)]
        msk = px.enter_context(st_("sa_msk", [128, 4 * 512], BF16))
        psA = [px.enter_context(_psum(nc, f"sa_psA{i}", [128, 512], F32))
               for i in range(3)]
        psO = [px.enter_context(_psum(nc, f"sa_psO{i}", [128, 512], F32))
               for i in range(2)]
        psL = [px.enter_context(_psum(nc, f"sa_psL{i}", [1, 512], F32))
               for i in range(2)]
        psB = [px.enter_context(_psum(nc, "sa_psB", [128, 512], F32))
               for i in range(1)]
        E.wait("y", cc)
        E.opi("y", lambda e: e.dma_start(
            out=msk[:].rearrange("p (j q) -> p j q", q=512),
            in_=io["fmask"].transpose([1, 0, 2])), qS, 16)
        E.opi("v", lambda e: e.memset(S[:, 30:31], -SOFF), vE, 1)
        E.wait("s", vE)
        hpbox = {}

        def snap_heads(e):
            pid = nc.partition_id(engines=[e.engine])
            for j in range(2):
                hpbox[("hp", j)] = e.snap((pid * 2 + j) * 128,
                                          min_val=0, max_val=15 * 128)
                hpbox[("hf", j)] = e.snap((pid * 2 + j) * (128 * TC),
                                          min_val=0, max_val=15 * 128 * TC)
            return None
        E.op("y", snap_heads)
        for j in range(2):
            def load_kv(e, j=j):
                hp = hpbox[("hp", j)]
                hf = hpbox[("hf", j)]
                i1 = (dr["qkv1"].ap()[:, 1].rearrange("r h p t -> r (h p) t")
                      [:, ds(hp, 128), :].transpose([1, 0, 2]))
                e.dma_start(out=kT[:].rearrange("p (r t) -> p r t", t=TC),
                            in_=i1).then_inc(E.sems["qD0"].h, 16)
                i1q = (dr["qkv1"].ap()[:, 0].rearrange("r h p t -> r (h p) t")
                       [:, ds(hp, 128), :].transpose([1, 0, 2]))
                e.dma_start(out=qTf[:].rearrange("p (r t) -> p r t", t=TC),
                            in_=i1q).then_inc(E.sems["qD0"].h, 16)
                for r in range(8):
                    i2 = (dr["qkv1"].ap()[r, 2]
                          .rearrange("h a t -> (h a t)")[ds(hf, 128 * TC)]
                          .rearrange("(kt p c) -> kt p c", p=128, c=128)
                          .transpose([1, 0, 2]))
                    e.dma_start(
                        out=vsb[:, 1024 * r:1024 * (r + 1)]
                        .rearrange("p (kt c) -> p kt c", c=128),
                        in_=i2).then_inc(E.sems["qD0"].h, 16)
                return None
            if j >= 1:
                E.wait("y", tE)
            E.sems["qD0"].cnt += 16 * 10
            E.op("y", load_kv)
            E.wait("t", E.sems["qD0"])
            sE_qt_last = 0
            tE_qt_last = 0
            vE_qt_last = 0
            obuf_hist = {}
            for qt in range(16):
                nkt = 4 * qt + 4
                qb = qTf[:, 512 * qt:512 * (qt + 1)]
                po = psO[qt % 2]
                pl = psL[qt % 2]
                pb2 = psB[0]
                ob2 = obuf[qt % 2]
                rl2 = rlb[qt % 2]
                lr2 = lrow[qt % 2]
                sEh, tEh, aEh = {}, {}, {}
                sExph, vAddh, tLh = {}, {}, {}

                def emitA(kt, qb=qb):
                    if kt >= 3:
                        E.wait("t", sE, sEh[kt - 3])
                    elif sE_qt_last:
                        E.wait("t", sE, sE_qt_last)
                    if kt == 0 and qt >= 2:
                        E.wait("t", vE, obuf_hist[qt - 2])
                    E.opi("t", lambda e, kt=kt, pa=psA[kt % 3], qb=qb: e.matmul(
                        pa[:], kT[:, 128 * kt:128 * (kt + 1)], qb,
                        start=True, stop=True), tE, 1)
                    aEh[kt] = tE.cnt

                def emitL(i, stop, nkt=nkt, pl=pl):
                    E.wait("t", vE, vAddh[2 * i + 1])
                    E.opi("t", lambda e, i=i, q2=Q2[i % 2], pl=pl: e.matmul(
                        pl[:], g["ones_bf"][:], q2[:],
                        start=(i == 0), stop=stop), tE, 1)
                    tLh[i] = tE.cnt

                def emitPV(kt, nkt=nkt, po=po, pl=pl):
                    pa = psA[kt % 3]
                    pt = PT[kt % 3]
                    if kt >= 3:
                        E.wait("s", tE, tEh[kt - 3])
                        E.wait("s", vE, vAddh[(kt - 3) | 1])
                    elif tE_qt_last:
                        E.wait("s", tE, tE_qt_last)
                        E.wait("s", vE, vE_qt_last)
                    E.wait("s", tE, aEh[kt])
                    E.opi("s", lambda e, pa=pa, pt=pt: e.activation(
                        pt[:], pa[:], AF.Exp, bias=S[:, 30:31]), sE, 1)
                    sExph[kt] = sE.cnt
                    diag = kt - 4 * qt
                    if diag >= 0:
                        E.wait("v", sE)
                        E.opi("v", lambda e, pt=pt, diag=diag: e.tensor_tensor(
                            pt[:], pt[:], msk[:, 512 * diag:512 * (diag + 1)],
                            ALU.mult), vE, 1)
                        E.wait("t", vE)
                    else:
                        E.wait("t", sE)
                    sEh[kt] = sE.cnt
                    E.opi("t", lambda e, kt=kt, pt=pt, nkt=nkt, po=po: e.matmul(
                        po[:], vsb[:, 128 * kt:128 * (kt + 1)], pt[:],
                        start=(kt == 0), stop=(kt == nkt - 1)), tE, 1)
                    tEh[kt] = tE.cnt
                    if kt % 2 == 1:
                        i = kt // 2
                        E.wait("v", sE, sExph[kt])
                        if i >= 2:
                            E.wait("v", tE, tLh[i - 2])
                        E.opi("v", lambda e, i=i, q2=Q2[i % 2], kt=kt:
                              e.tensor_tensor(q2[:], PT[(kt - 1) % 3][:],
                                              PT[kt % 3][:], ALU.add), vE, 1)
                        vAddh[kt] = vE.cnt
                        if i >= 1:
                            emitL(i - 1, stop=False)

                emitA(0)
                if nkt > 1:
                    emitA(1)
                for kt in range(nkt):
                    emitPV(kt)
                    if kt + 2 < nkt:
                        emitA(kt + 2)
                emitL(nkt // 2 - 1, stop=True)
                vE_qt_last = vE.cnt
                sE_qt_last = sE.cnt
                tE_qt_last = tE.cnt
                E.wait("v", tE)
                E.opi("v", lambda e, pl=pl, lr2=lr2: e.tensor_copy(
                    lr2[:], pl[:]), vE, 1)
                E.dr("v")
                E.opi("v", lambda e, lr2=lr2: e.reciprocal(lr2[:], lr2[:]),
                      vE, 1)
                E.dr("v")
                E.wait("t", vE)
                E.opi("t", lambda e, lr2=lr2, pb2=pb2: e.matmul(
                    pb2[:], g["ones_f"][0:1, :], lr2[:],
                    start=True, stop=True), tE, 1)
                E.wait("v", tE)
                E.opi("v", lambda e, pb2=pb2, rl2=rl2: e.tensor_copy(
                    rl2[:], pb2[:]), vE, 1)
                E.dr("v")
                if qt >= 2:
                    E.wait("v", qO)
                E.opi("v", lambda e, po=po, rl2=rl2, ob2=ob2: e.tensor_tensor(
                    ob2[:], po[:], rl2[:], ALU.mult), vE, 1)
                obuf_hist[qt] = vE.cnt
                E.wait("g", vE)
                E.opi("g", lambda e, j=j, qt=qt, ob2=ob2: e.dma_start(
                    out=dr["o1_sh"].ap()[j][:, 512 * qt:512 * (qt + 1)],
                    in_=ob2[:]), qO, 16)
        _ag(b, "o1_sh", "o1")
        E.join()
        E.flush_block("sa_flash")

    with ExitStack() as px:
        xts = [px.enter_context(st_(f"sa_x{i}", [128, D], F32)) for i in range(8)]
        oTa = px.enter_context(st_("sa_oTa", [128, 16 * 1024], BF16))
        wsl = [px.enter_context(st_(f"sa_wo{i}", [128, 16 * 512], BF16))
               for i in range(2)]
        psP = [px.enter_context(_psum(nc, f"sa_psQ{i}", [128, 512], F32))
               for i in range(2)]
        E.wait("g", cc)
        import os as _os
        if _os.environ.get("MK_DBG5"):
            E.join()
            E.opi("g", lambda e: e.dma_start(out=io["dbg0"],
                                             in_=dr["o1"].ap()[0, 0]),
                  E.sem("qO"), 16)
            E.opi("g", lambda e: e.dma_start(out=io["dbg2"],
                                             in_=dr["o1"].ap()[0, 1][:, 0:D]),
                  E.sem("qO"), 16)
            E.join()

        E.wait("s", cc)
        def load_oT(e):
            pid = nc.partition_id(engines=[e.engine])
            qoff = e.snap(pid * 1024, min_val=0, max_val=7 * 1024)
            i4 = (dr["o1"].ap().rearrange("r j p q -> (r j) p q")
                  [:, :, ds(qoff, 1024)].transpose([1, 0, 2]))
            e.dma_start(out=oTa[:].rearrange("p (h t) -> p h t", t=1024),
                        in_=i4).then_inc(E.sems["qD1"].h, 16)
            return None
        E.sems["qD1"].cnt += 16
        E.op("s", load_oT)
        for m in range(8):
            E.opi("y", lambda e, m=m: e.dma_start(
                out=xts[m][:], in_=xsrc[128 * m:128 * (m + 1), :]), qA0, 16)
        E.wait("t", E.sems["qD1"])
        E.wait("t", qA0)
        tEh = {}
        for n in range(4):
            q = qW0 if n % 2 == 0 else qW1
            if n >= 2:
                E.wait("y", tE, tEh[n - 2])
            emit_qslab(wsl[n % 2], 3, n, q)
            E.wait("t", q)
            for m in range(8):
                pb = psP[m % 2]
                if (n, m) >= (0, 2):
                    E.wait("t", vE)
                for k in range(16):
                    E.opi("t", lambda e, m=m, k=k, n=n, pb=pb: e.matmul(
                        pb[:], oTa[:, 1024 * k + 128 * m:1024 * k + 128 * (m + 1)],
                        wsl[n % 2][:, 512 * k:512 * (k + 1)],
                        start=(k == 0), stop=(k == 15)), tE,
                        1 if k == 15 else 0)
                E.wait("v", tE)
                E.opi("v", lambda e, m=m, n=n, pb=pb: e.tensor_tensor(
                    xts[m][:, 512 * n:512 * (n + 1)],
                    xts[m][:, 512 * n:512 * (n + 1)], pb[:], ALU.add), vE, 1)
            tEh[n] = tE.cnt
        E.wait("g", vE)
        for m in range(8):
            E.opi("g", lambda e, m=m: e.dma_start(
                out=xdst[128 * m:128 * (m + 1), :], in_=xts[m][:]), qO, 16)
        E.join()
        E.flush_block("sa_out")


def kernel(**inputs):
    maps = _shard_inputs(inputs)
    import os
    upto = int(os.environ.get("MK_UPTO", "6"))
    nc = build(upto=upto)
    res = run_bass_kernel_spmd(nc, maps, core_ids=list(range(NCORES)))
    out = np.concatenate([np.asarray(r["y"]) for r in res.results], axis=0)
    return out.reshape(1, 8192, D).astype(np.float32)
